# version 06
# baseline (speedup 1.0000x reference)
import numpy as np
import ml_dtypes
from contextlib import ExitStack
import concourse.bass as bass
import concourse.mybir as mybir
from concourse.bass_utils import run_bass_kernel_spmd

F32 = mybir.dt.float32
BF16 = mybir.dt.bfloat16
AF = mybir.ActivationFunctionType
ALU = mybir.AluOpType
AX = mybir.AxisListType

NCORES = 8
D_MODEL = 1024
FCH = 8
NH, HD = 16, 64
M_MEM = 4
DI = 2048
DICH = 16
N16 = 16
DTR = 64
EPS0 = 1e-6


_EDGE_ENGINES = {"vector", "scalar"}
_NOEDGE = {"wait_ge", "nop", "sem_inc", "wait_op", "drain"}


class EngW:

    def __init__(self, e, ename, emit):
        object.__setattr__(self, "_e", e)
        object.__setattr__(self, "_last", None)
        object.__setattr__(self, "_n", ename)
        object.__setattr__(self, "_E", emit)

    def __getattr__(self, k):
        e = object.__getattribute__(self, "_e")
        f = getattr(e, k)
        if not callable(f):
            return f

        def g(*a, **kw):
            E = object.__getattribute__(self, "_E")
            nm = object.__getattribute__(self, "_n")
            if nm in _EDGE_ENGINES and k not in _NOEDGE:
                def _nm_of(x):
                    t = getattr(x, "tensor", None)
                    if t is not None and hasattr(t, "name"):
                        return t.name
                    if hasattr(x, "name") and hasattr(x, "dtype"):
                        return x.name
                    return None
                names = {y for y in (_nm_of(x) for x in list(a) + list(kw.values()))
                         if y}
                out = a[0] if a else kw.get("out")
                oname = _nm_of(out)
                dirty = E._dirty.setdefault(nm, set())
                rd = E._rd.setdefault(nm, set())
                if (names & dirty) or (oname in rd) or (oname in dirty):
                    last = E._lasti.get(nm)
                    if last is not None:
                        up = E._lastupd.get(nm)
                        if up is not None and up[0] is last:
                            e.wait_ge(up[1], up[2])
                        else:
                            h, v = E.inc("ae_" + nm)
                            last.then_inc(h, 1)
                            E._lastupd[nm] = (last, h, v)
                            e.wait_ge(h, v)
                    dirty.clear()
                    rd.clear()
                if oname:
                    dirty.add(oname)
                rd |= names - {oname}
            r = f(*a, **kw)
            if hasattr(r, "then_inc"):
                object.__setattr__(self, "_last", r)
                if k != "wait_ge":
                    E = object.__getattribute__(self, "_E")
                    E._lasti[object.__getattribute__(self, "_n")] = r
            return r
        return g


class BlockW:

    def __init__(self, block, emit):
        self._block = block
        self._emit = emit

    def __getattr__(self, name):
        dec = getattr(self._block, name)

        def deco(fn):
            return dec(lambda e, _fn=fn, _n=name: _fn(EngW(e, _n, self._emit)))
        return deco


class Emit:

    def __init__(self, nc, block, es):
        self.nc, self.es = nc, es
        self.block = BlockW(block, self)
        self._sems = {}
        self._dirty = {}
        self._rd = {}
        self._lasti = {}
        self._lastupd = {}

    def recycle(self, keep=("j", "ae_")):
        pool = getattr(self, "_pool", None)
        if pool is None:
            pool = self._pool = []
        for name in list(self._sems):
            if not any(name.startswith(k) for k in keep):
                pool.append(self._sems.pop(name))

    def fin(self, eng, name, n=1):
        h, v = self.inc(name, n)
        e = object.__getattribute__(eng, "_e")
        en = object.__getattribute__(eng, "_n")
        last = object.__getattribute__(eng, "_last")
        up = self._lastupd.get(en)
        if last is None:
            e.nop().then_inc(h, n)
        elif up is not None and up[0] is last:
            e.wait_ge(up[1], up[2])
            e.nop().then_inc(h, n)
        else:
            last.then_inc(h, n)
            if self._lasti.get(en) is last:
                self._lastupd[en] = (last, h, v)

    def sem(self, name):
        if name not in self._sems:
            pool = getattr(self, "_pool", None)
            if pool:
                self._sems[name] = pool.pop()
            else:
                h = self.es.enter_context(self.nc.semaphore(name))
                self._sems[name] = [h, 0]
        return self._sems[name]

    def inc(self, name, n=1):
        s = self.sem(name)
        s[1] += n
        return s[0], s[1]

    def at(self, name):
        s = self.sem(name)
        return s[0], s[1]

    def wait_all(self, eng, names):
        for nm in names:
            h, v = self.at(nm)
            if v > 0:
                eng.wait_ge(h, v)


_ALL_ENGINES = ("sync", "vector", "scalar", "tensor", "gpsimd")


def barrier(E):
    b = E.block
    for eng_name in _ALL_ENGINES:
        @getattr(b, eng_name)
        def _(e):
            h, _ = E.inc("j")
            e.nop().then_inc(h, 1)
    tgt = E.at("j")[1]
    for eng_name in _ALL_ENGINES:
        @getattr(b, eng_name)
        def _(e):
            e.wait_ge(E.sem("j")[0], tgt)


class Tiles:
    def __init__(self, nc, es):
        self.nc, self.es = nc, es
        self.t = {}

    def sb(self, name, shape, dt=F32):
        if name not in self.t:
            self.t[name] = self.es.enter_context(self.nc.sbuf_tensor(name, shape, dt))
        return self.t[name]

    def ps(self, name, shape, dt=F32):
        if name not in self.t:
            self.t[name] = self.es.enter_context(self.nc.psum_tensor(name, shape, dt))
        return self.t[name]


def ceil_div(a, b):
    return (a + b - 1) // b


def emit_ln(E, T, x_fm, xn_bf, ts_name, xn_f32=None, rms_of=None, nsc=None,
            consts=None, scratch=None, banks=None, pre=()):
    b = E.block
    TT = min(512, T)
    ntt = ceil_div(T, TT)
    rms = rms_of is not None
    xin = rms_of if rms else x_fm
    sq = scratch["sq"]
    st_mean, st_var = scratch["stat0"], scratch["stat1"]
    st_rstd, st_negmr = scratch["stat2"], scratch["stat3"]
    A = banks["A"]
    ones_f1 = consts["ones_f1"]
    ones_1f = consts["ones_1f"]
    tag = ts_name

    if not rms:
        @b.tensor
        def _(t):
            E.wait_all(t, list(pre))
            for c in range(FCH):
                for i in range(ntt):
                    sl = slice(i * TT, (i + 1) * TT)
                    t.matmul(A[0:1, sl], ones_f1[:, 0:1], xin[c][:, sl],
                             start=(c == 0), stop=(c == FCH - 1))
            E.fin(t, tag + "_s1mm")

        @b.vector
        def _(v):
            E.wait_all(v, [tag + "_s1mm"])
            v.tensor_scalar(st_mean[0:1, 0:T], A[0:1, 0:T], 1.0 / D_MODEL,
                            None, ALU.mult)
            E.fin(v, tag + "_s1cp")

    bsq = E.at(tag + "_pe_sq")[1]

    @b.scalar
    def _(sc):
        E.wait_all(sc, list(pre))
        for c in range(FCH):
            if c > 0:
                sc.wait_ge(E.sem(tag + "_pe_sq")[0], bsq + c)
            sc.activation(sq[:, 0:T], xin[c][:, 0:T], AF.Square)
            E.fin(sc, tag + "_sqr")

    bsqr = E.at(tag + "_sqr")[1] - FCH

    @b.tensor
    def _(t):
        if not rms:
            E.wait_all(t, [tag + "_s1cp"])
        for c in range(FCH):
            t.wait_ge(E.sem(tag + "_sqr")[0], bsqr + c + 1)
            for i in range(ntt):
                sl = slice(i * TT, (i + 1) * TT)
                t.matmul(A[0:1, sl], ones_f1[:, 0:1], sq[:, sl],
                         start=(c == 0), stop=(c == FCH - 1))
            E.fin(t, tag + "_pe_sq")

    @b.vector
    def _(v):
        E.wait_all(v, [tag + "_pe_sq"])
        v.tensor_scalar(st_var[0:1, 0:T], A[0:1, 0:T], 1.0 / D_MODEL, None, ALU.mult)
        if not rms:
            v.tensor_tensor(st_negmr[0:1, 0:T], st_mean[0:1, 0:T],
                            st_mean[0:1, 0:T], ALU.mult)
            E.fin(v, tag + "_vtmp")
            v.wait_ge(E.sem(tag + "_vtmp")[0], E.at(tag + "_vtmp")[1])
            v.tensor_tensor(st_var[0:1, 0:T], st_var[0:1, 0:T],
                            st_negmr[0:1, 0:T], ALU.subtract)
        E.fin(v, tag + "_dve_var")

    @b.scalar
    def _(sc):
        E.wait_all(sc, [tag + "_dve_var", "kon"])
        kon = consts["kon"]
        sc.activation(st_rstd[0:1, 0:T], st_var[0:1, 0:T], AF.Ln, bias=kon[0:1, 0:1])
        sc.activation(st_rstd[0:1, 0:T], st_rstd[0:1, 0:T], AF.Exp,
                      scale=kon[0:1, 1:2])
        E.fin(sc, tag + "_act_rstd")

    @b.vector
    def _(v):
        E.wait_all(v, [tag + "_act_rstd"])
        if not rms:
            v.scalar_tensor_tensor(st_negmr[0:1, 0:T], st_mean[0:1, 0:T], -1.0,
                                   st_rstd[0:1, 0:T], ALU.mult, ALU.mult)
        E.fin(v, tag + "_negmr")

    for c in range(FCH):
        for i in range(ntt):
            sl = slice(i * TT, (i + 1) * TT)

            @b.tensor
            def _(t, c=c, i=i, sl=sl):
                E.wait_all(t, [tag + "_negmr"])
                if not (c == 0 and i == 0):
                    t.wait_ge(E.sem(tag + "_app")[0], E.at(tag + "_app")[1])
                t.matmul(A[:, 0:TT], ones_1f[0:1, :], st_rstd[0:1, sl],
                         start=True, stop=True)
                if not rms:
                    t.matmul(A[:, TT:2 * TT], ones_1f[0:1, :], st_negmr[0:1, sl],
                             start=True, stop=True)
                E.fin(t, tag + "_bc")

            @b.vector
            def _(v, c=c, i=i, sl=sl):
                v.wait_ge(E.sem(tag + "_bc")[0], E.at(tag + "_bc")[1])
                if rms:
                    v.scalar_tensor_tensor(xn_bf[c][:, sl], xin[c][:, sl],
                                           nsc[:, c:c + 1], A[:, 0:TT],
                                           ALU.mult, ALU.mult)
                else:
                    v.tensor_tensor(sq[:, 0:TT], x_fm[c][:, sl], A[:, 0:TT], ALU.mult)
                    if xn_bf is not None:
                        v.tensor_tensor(xn_bf[c][:, sl], sq[:, 0:TT], A[:, TT:2 * TT],
                                        ALU.add)
                    if xn_f32 is not None:
                        v.tensor_tensor(xn_f32[c][:, sl], sq[:, 0:TT],
                                        A[:, TT:2 * TT], ALU.add)
                E.fin(v, tag + "_app")

    @b.vector
    def _(v):
        v.wait_ge(E.sem(tag + "_app")[0], E.at(tag + "_app")[1])
        E.fin(v, tag + "_done")


def emit_l0_attn(E, Ti, ts, w, pre=(), tag="l0a"):
    b = E.block
    TT = min(512, Ti)
    ntt = ceil_div(Ti, TT)
    ntile = Ti // 128
    CPB = NCORES // 2
    Sq = CPB * Ti
    xn, w0 = ts["xn"], ts["w0"]
    sq, sk_tm, v_tm, loc = ts["sq"], ts["sk"], ts["v"], ts["loc"]
    e0, e1, e2, e3 = ts["e0"], ts["e1"], ts["e2"], ts["e3"]
    relT, rr = ts["relT"], ts["rr"]
    st_sb, pfx, bsm, agr = ts["st_sb"], ts["pfx"], ts["bsm"], ts["agr"]
    mn_sb, selrr, memb = ts["mn_sb"], ts["selrr"], ts["memb"]
    gsg, g1m = ts["gsg"], ts["g1m"]
    skT, kvr, kvrb, kcr, kcrb = ts["skT"], ts["kvr"], ts["kvrb"], ts["kcr"], ts["kcrb"]
    sqh = ts["sqh"]
    den_r, dbc, sqs, smB = ts["den_r"], ts["dbc"], ts["sqs"], ts["smB"]
    xrl = (ts["xrl0"], ts["xrl1"])
    wvec, rmax, wcol, rmcol = ts["wvec"], ts["rmax"], ts["wcol"], ts["rmcol"]
    pmb, bmb = ts["pmb"], ts["bmb"]
    ones_f1, ones_1f, ones_b1 = ts["ones_f1"], ts["ones_1f"], ts["ones_b1"]
    mask_ut, ident_b = ts["mask_ut"], ts["ident_b"]
    PB, PBT = ts["PB"], ts["PBT"]

    def base(name):
        return E.at(tag + name)[1]

    def wld(g, name):
        src = w[name].ap().rearrange("(c p) o -> p c o", p=128)
        ins = g.dma_start(out=w0[:, :].rearrange("p (c o) -> p c o", c=FCH), in_=src)
        ins.then_inc(E.inc(tag + "_wld", 16)[0], 16)

    @b.gpsimd
    def _(g):
        g.dma_start(out=mn_sb[:, 0:512].rearrange("p (c o) -> p c o", c=FCH),
                    in_=w["MN"].ap().rearrange("(c p) o -> p c o", p=128)
                    ).then_inc(E.inc(tag + "_cld", 16)[0], 16)
        g.dma_start(out=ts["gate"][0:16, 0:1], in_=w["gate"].ap()
                    ).then_inc(E.inc(tag + "_cld", 16)[0], 16)
        g.dma_start(out=ts["pmrow"][0:1, 0:8], in_=w["pmask"].ap()
                    ).then_inc(E.inc(tag + "_cld", 16)[0], 16)
        g.dma_start(out=ts["bmrow"][0:1, 0:8], in_=w["bmask"].ap()
                    ).then_inc(E.inc(tag + "_cld", 16)[0], 16)
        wld(g, "wqT")

    @b.scalar
    def _(sc):
        E.wait_all(sc, [tag + "_cld"])
        sc.activation(ts["gate"][0:16, 0:1], ts["gate"][0:16, 0:1], AF.Sigmoid)
        E.fin(sc, tag + "_gsig")

    @b.sync
    def _(s):
        E.wait_all(s, [tag + "_gsig"])
        s.dma_start(out=w["gate_sc"].ap(), in_=ts["gate"][0:16, 0:1]
                    ).then_inc(E.inc(tag + "_gd1", 16)[0], 16)
        E.wait_all(s, [tag + "_gd1"])
        s.dma_start(out=gsg[0:64, 0:8],
                    in_=bass.AP(w["gate_sc"], 0, [[0, 64], [2, 8]]),
                    allow_slow_non_contiguous=True
                    ).then_inc(E.inc(tag + "_gd2", 16)[0], 16)
        s.dma_start(out=gsg[64:128, 0:8],
                    in_=bass.AP(w["gate_sc"], 1, [[0, 64], [2, 8]]),
                    allow_slow_non_contiguous=True
                    ).then_inc(E.inc(tag + "_gd2", 16)[0], 16)

    @b.vector
    def _(v):
        E.wait_all(v, [tag + "_gd2"])
        v.tensor_scalar(g1m[:, 0:8], gsg[:, 0:8], -1.0, 1.0, ALU.mult, ALU.add)
        E.fin(v, tag + "_g1m")

    def elu1_jobs(njobs, bank_of, out_of, tg, W=None):
        b_mm = base(tg + "_mm") - njobs
        b_act = base(tg + "_act")
        b_min = base(tg + "_min")
        W = TT if W is None else W

        @b.vector
        def _(v):
            for j in range(njobs + 1):
                if j < njobs:
                    v.wait_ge(E.sem(tag + tg + "_mm")[0], b_mm + j + 1)
                    v.tensor_scalar((e0 if j % 2 == 0 else e1)[:, 0:W],
                                    bank_of(j)[:, 0:W], 0.0, None, ALU.min)
                    E.fin(v, tag + tg + "_min")
                if j >= 1:
                    v.wait_ge(E.sem(tag + tg + "_act")[0], b_act + j)
                    v.scalar_tensor_tensor(out_of(j - 1), bank_of(j - 1)[:, 0:W],
                                           0.0,
                                           (e2 if (j - 1) % 2 == 0 else e3)[:, 0:W],
                                           ALU.max, ALU.add)
                    E.fin(v, tag + tg + "_done")

        @b.scalar
        def _(sc):
            for j in range(njobs):
                sc.wait_ge(E.sem(tag + tg + "_min")[0], b_min + j + 1)
                if j >= 2:
                    sc.wait_ge(E.sem(tag + tg + "_done")[0],
                               base(tg + "_done") - njobs + j - 1)
                sc.activation((e2 if j % 2 == 0 else e3)[:, 0:W],
                              (e0 if j % 2 == 0 else e1)[:, 0:W], AF.Exp)
                E.fin(sc, tag + tg + "_act")

    def base2(nm):
        return E.at(tag + nm)[1]

    qjobs = [(oc, tt) for oc in range(FCH) for tt in range(ntt)]
    bq_done = base2("_q_done")

    @b.tensor
    def _(t):
        E.wait_all(t, list(pre))
        t.wait_ge(E.sem(tag + "_wld")[0], 16)
        for j, (oc, tt) in enumerate(qjobs):
            if j >= 2:
                t.wait_ge(E.sem(tag + "_q_done")[0], bq_done + j - 1)
            for fc in range(FCH):
                t.matmul(PB[j % 2][:, 0:TT],
                         w0[:, fc * 1024 + oc * 128:fc * 1024 + oc * 128 + 128],
                         xn[:, fc * Ti + tt * TT:fc * Ti + (tt + 1) * TT],
                         start=(fc == 0), stop=(fc == FCH - 1))
            E.fin(t, tag + "_q_mm")

    elu1_jobs(len(qjobs), lambda j: PB[j % 2],
              lambda j: sq[:, qjobs[j][0] * Ti + qjobs[j][1] * TT:
                           qjobs[j][0] * Ti + (qjobs[j][1] + 1) * TT], "_q")

    @b.gpsimd
    def _(g):
        g.wait_ge(E.sem(tag + "_q_mm")[0], E.at(tag + "_q_mm")[1])
        wld(g, "wkT")

    kjobs = [(tI, oh) for tI in range(ntile) for oh in range(2)]

    def tm_proj_section(wneed, bank_base, consname, mmname):
        b_cons = base2(consname)

        @b.tensor
        def _(t):
            t.wait_ge(E.sem(tag + "_wld")[0], wneed)
            for j, (tI, oh) in enumerate(kjobs):
                if j >= 2:
                    t.wait_ge(E.sem(tag + consname)[0], b_cons + j - 1)
                for fc in range(FCH):
                    t.matmul(PB[bank_base + j % 2][:, 0:512],
                             xn[:, fc * Ti + tI * 128:fc * Ti + (tI + 1) * 128],
                             w0[:, fc * 1024 + oh * 512:fc * 1024 + (oh + 1) * 512],
                             start=(fc == 0), stop=(fc == FCH - 1))
                E.fin(t, tag + mmname)

    tm_proj_section(32, 2, "_k_done", "_k_mm")
    elu1_jobs(len(kjobs), lambda j: PB[2 + j % 2],
              lambda j: sk_tm[:, kjobs[j][0] * 1024 + kjobs[j][1] * 512:
                              kjobs[j][0] * 1024 + (kjobs[j][1] + 1) * 512], "_k",
              W=512)

    @b.gpsimd
    def _(g):
        g.wait_ge(E.sem(tag + "_k_mm")[0], E.at(tag + "_k_mm")[1])
        wld(g, "wvT")

    tm_proj_section(48, 4, "_v_cp", "_v_mm")
    bv_mm = base2("_v_mm") - len(kjobs)

    @b.vector
    def _(v):
        for j, (tI, oh) in enumerate(kjobs):
            v.wait_ge(E.sem(tag + "_v_mm")[0], bv_mm + j + 1)
            v.tensor_copy(v_tm[:, tI * 1024 + oh * 512:tI * 1024 + (oh + 1) * 512],
                          PB[4 + j % 2][:, 0:512])
            E.fin(v, tag + "_v_cp")

    brel_cp = base2("_rel_cp")

    @b.tensor
    def _(t):
        t.wait_ge(E.sem(tag + "_q_done")[0], E.at(tag + "_q_done")[1])
        E.wait_all(t, [tag + "_cld"])
        for tt in range(ntt):
            if tt >= 1:
                t.wait_ge(E.sem(tag + "_rel_cp")[0], brel_cp + tt)
            for fc in range(FCH):
                t.matmul(PB[6][0:64, 0:TT], mn_sb[:, fc * 64:(fc + 1) * 64],
                         sq[:, fc * Ti + tt * TT:fc * Ti + (tt + 1) * TT],
                         start=(fc == 0), stop=(fc == FCH - 1))
            E.fin(t, tag + "_rel_mm")

    brel_mm = base2("_rel_mm") - ntt

    @b.vector
    def _(v):
        for tt in range(ntt):
            v.wait_ge(E.sem(tag + "_rel_mm")[0], brel_mm + tt + 1)
            v.tensor_copy(relT[0:64, tt * TT:(tt + 1) * TT], PB[6][0:64, 0:TT])
            E.fin(v, tag + "_rel_cp")

    bkv_cp = base2("_kv_cp")

    @b.tensor
    def _(t):
        for nm in (tag + "_v_cp", tag + "_k_done", tag + "_rel_cp"):
            t.wait_ge(E.sem(nm)[0], E.at(nm)[1])
        for hh in range(NH):
            if hh >= 2:
                t.wait_ge(E.sem(tag + "_kv_cp")[0], bkv_cp + hh - 1)
            for tI in range(ntile):
                t.matmul(PB[hh % 2][0:64, 0:64],
                         sk_tm[:, tI * 1024 + hh * 64:tI * 1024 + (hh + 1) * 64],
                         v_tm[:, tI * 1024 + hh * 64:tI * 1024 + (hh + 1) * 64],
                         start=(tI == 0), stop=(tI == ntile - 1))
            E.fin(t, tag + "_kv_mm")
        for oc in range(FCH):
            for tI in range(ntile):
                t.matmul(PB[2][:, oc:oc + 1],
                         sk_tm[:, tI * 1024 + oc * 128:tI * 1024 + (oc + 1) * 128],
                         ones_b1[:, 0:1], start=(tI == 0), stop=(tI == ntile - 1))
        E.fin(t, tag + "_kc_mm")

    bkv_mm = base2("_kv_mm") - NH

    @b.vector
    def _(v):
        for hh in range(NH):
            v.wait_ge(E.sem(tag + "_kv_mm")[0], bkv_mm + hh + 1)
            oc, half = hh // 2, hh % 2
            v.tensor_copy(st_sb[64 * half:64 * half + 64, oc * 66:oc * 66 + 64],
                          PB[hh % 2][0:64, 0:64])
            E.fin(v, tag + "_kv_cp")
        v.wait_ge(E.sem(tag + "_kc_mm")[0], E.at(tag + "_kc_mm")[1])
        for oc in range(FCH):
            v.tensor_copy(st_sb[:, oc * 66 + 64:oc * 66 + 65], PB[2][:, oc:oc + 1])
        for oc in range(FCH):
            v.reduce_sum(st_sb[:, oc * 66 + 65:oc * 66 + 66],
                         sq[:, oc * Ti:(oc + 1) * Ti], axis=AX.X)
        E.fin(v, tag + "_st_done")

    @b.sync
    def _(s):
        E.wait_all(s, [tag + "_st_done"])
        s.dma_start(out=w["cc_in"].ap().rearrange("(c p) w -> p c w", p=128),
                    in_=st_sb[:, :].rearrange("p (c w) -> p c w", w=66)
                    ).then_inc(E.inc(tag + "_ccin", 16)[0], 16)

    @b.gpsimd
    def _(g):
        E.wait_all(g, [tag + "_ccin"])
        g.collective_compute(
            "AllGather", ALU.bypass, replica_groups=[list(range(NCORES))],
            ins=[w["cc_in"].ap()], outs=[w["cc_out"].ap()],
        ).then_inc(E.inc(tag + "_cc")[0], 1)

    @b.tensor
    def _(t):
        E.wait_all(t, [tag + "_cld", tag + "_kc_mm"])
        t.matmul(PB[3][:, 0:8], ones_1f[0:1, :], ts["pmrow"][0:1, 0:8],
                 start=True, stop=True)
        t.matmul(PB[3][:, 8:16], ones_1f[0:1, :], ts["bmrow"][0:1, 0:8],
                 start=True, stop=True)
        E.fin(t, tag + "_mb_mm")

    @b.vector
    def _(v):
        E.wait_all(v, [tag + "_mb_mm"])
        v.tensor_copy(pmb[:, 0:8], PB[3][:, 0:8])
        v.tensor_copy(bmb[:, 0:8], PB[3][:, 8:16])
        E.fin(v, tag + "_mb_cp")

    bagacc = base2("_agacc")
    for r in range(NCORES):
        @b.sync
        def _(s, r=r):
            E.wait_all(s, [tag + "_cc"])
            if r >= 1:
                s.wait_ge(E.sem(tag + "_agacc")[0], bagacc + r)
            s.dma_start(out=agr[:, 0:528].rearrange("p (c w) -> p c w", w=66),
                        in_=w["cc_out"].ap()[r * 8 * 128:(r + 1) * 8 * 128, :]
                        .rearrange("(c p) w -> p c w", p=128)
                        ).then_inc(E.inc(tag + "_agld", 16)[0], 16)

        @b.vector
        def _(v, r=r):
            v.wait_ge(E.sem(tag + "_agld")[0], E.at(tag + "_agld")[1])
            E.wait_all(v, [tag + "_mb_cp"])
            if r == 0:
                v.tensor_scalar(pfx[:, 0:528], agr[:, 0:528], pmb[:, 0:1], None, ALU.mult)
                v.tensor_scalar(bsm[:, 0:528], agr[:, 0:528], bmb[:, 0:1], None, ALU.mult)
            else:
                v.scalar_tensor_tensor(pfx[:, 0:528], agr[:, 0:528], pmb[:, r:r + 1],
                                       pfx[:, 0:528], ALU.mult, ALU.add)
                v.scalar_tensor_tensor(bsm[:, 0:528], agr[:, 0:528], bmb[:, r:r + 1],
                                       bsm[:, 0:528], ALU.mult, ALU.add)
            E.fin(v, tag + "_agacc")

    @b.vector
    def _(v):
        E.wait_all(v, [tag + "_agacc"])
        for c in range(FCH):
            v.tensor_copy(sqs[:, c:c + 1], bsm[:, c * 66 + 65:c * 66 + 66])
        E.fin(v, tag + "_sqb")

    @b.tensor
    def _(t):
        E.wait_all(t, [tag + "_sqb"])
        for c in range(FCH):
            t.matmul(PB[4][0:64, 0:1], mn_sb[:, c * 64:(c + 1) * 64],
                     sqs[:, c:c + 1], start=(c == 0), stop=(c == FCH - 1))
        E.fin(t, tag + "_rm_mm")

    @b.vector
    def _(v):
        E.wait_all(v, [tag + "_rm_mm"])
        v.tensor_scalar(rmcol[0:64, 0:1], PB[4][0:64, 0:1], 1.0 / Sq, None, ALU.mult)
        E.fin(v, tag + "_rmc")

    @b.sync
    def _(s):
        E.wait_all(s, [tag + "_rmc"])
        s.dma_start(out=w["rm_sc"].ap(), in_=rmcol[0:64, 0:1]
                    ).then_inc(E.inc(tag + "_rmd1", 16)[0], 16)
        E.wait_all(s, [tag + "_rmd1"])
        s.dma_start(out=wvec[0:1, 0:64], in_=w["rm_sc"].ap().rearrange("a b -> b a")
                    ).then_inc(E.inc(tag + "_rmtr", 16)[0], 16)

    @b.vector
    def _(v):
        E.wait_all(v, [tag + "_rmtr"])
        wv3 = wvec[0:1, 0:64].rearrange("a (h m) -> a h m", m=4)
        v.reduce_max(bass.AP(rmax, 0, [[32, 1], [1, 16], [1, 1]]), wv3, axis=AX.X)
        E.fin(v, tag + "_sm0")
        v.wait_ge(E.sem(tag + "_sm0")[0], E.at(tag + "_sm0")[1])
        mxb = bass.AP(rmax, 0, [[32, 1], [1, 16], [0, 4]])
        v.tensor_tensor(wv3, wv3, mxb, ALU.subtract)
        E.fin(v, tag + "_sm1")

    @b.scalar
    def _(sc):
        E.wait_all(sc, [tag + "_sm1"])
        sc.activation(wvec[0:1, 0:64], wvec[0:1, 0:64], AF.Exp)
        E.fin(sc, tag + "_sm2")

    @b.vector
    def _(v):
        E.wait_all(v, [tag + "_sm2"])
        wv3 = wvec[0:1, 0:64].rearrange("a (h m) -> a h m", m=4)
        v.reduce_sum(bass.AP(rmax, 0, [[32, 1], [1, 16], [1, 1]]), wv3, axis=AX.X)
        E.fin(v, tag + "_sm3")
        v.wait_ge(E.sem(tag + "_sm3")[0], E.at(tag + "_sm3")[1])
        v.reciprocal(bass.AP(rmax, 16, [[32, 1], [1, 16]]),
                     bass.AP(rmax, 0, [[32, 1], [1, 16]]))
        E.fin(v, tag + "_sm4")
        v.wait_ge(E.sem(tag + "_sm4")[0], E.at(tag + "_sm4")[1])
        rcb = bass.AP(rmax, 16, [[32, 1], [1, 16], [0, 4]])
        v.tensor_tensor(wv3, wv3, rcb, ALU.mult)
        E.fin(v, tag + "_smw")

    @b.sync
    def _(s):
        E.wait_all(s, [tag + "_smw"])
        s.dma_start(out=w["rm_sc"].ap().rearrange("a b -> b a"),
                    in_=wvec[0:1, 0:64]).then_inc(E.inc(tag + "_wcd1", 16)[0], 16)
        E.wait_all(s, [tag + "_wcd1"])
        s.dma_start(out=wcol[0:64, 0:1], in_=w["rm_sc"].ap()
                    ).then_inc(E.inc(tag + "_wcol", 16)[0], 16)

    @b.vector
    def _(v):
        E.wait_all(v, [tag + "_wcol"])
        for tt in range(ntt):
            v.tensor_scalar(e0[0:64, 0:TT], relT[0:64, tt * TT:(tt + 1) * TT],
                            EPS0, None, ALU.max)
            v.reciprocal(e1[0:64, 0:TT], e0[0:64, 0:TT])
            v.tensor_scalar(rr[0:64, tt * TT:(tt + 1) * TT], e1[0:64, 0:TT],
                            wcol[0:64, 0:1], None, ALU.mult)
        E.fin(v, tag + "_rr")

    @b.vector
    def _(v):
        E.wait_all(v, [tag + "_agacc"])
        for hh in range(NH):
            oc, half = hh // 2, hh % 2
            src = pfx[64 * half:64 * half + 64, oc * 66:oc * 66 + 64]
            v.tensor_copy(kvr[0:64, hh * 64:(hh + 1) * 64], src)
            v.tensor_copy(kvrb[0:64, hh * 64:(hh + 1) * 64], src)
            v.tensor_copy(kcr[0:64, hh:hh + 1],
                          pfx[64 * half:64 * half + 64, oc * 66 + 64:oc * 66 + 65])
        v.tensor_copy(kcrb[0:64, 0:16], kcr[0:64, 0:16])
        E.fin(v, tag + "_kvinit")

    for ci in range(ntile):
        for hh in range(NH):
            j = ci * NH + hh
            oc, half = hh // 2, hh % 2
            sq_sl = sq[64 * half:64 * half + 64,
                       oc * Ti + ci * 128:oc * Ti + (ci + 1) * 128]
            sk_sl = sk_tm[:, ci * 1024 + hh * 64:ci * 1024 + (hh + 1) * 64]
            v_sl = v_tm[:, ci * 1024 + hh * 64:ci * 1024 + (hh + 1) * 64]

            @b.tensor
            def _(t, j=j, sk_sl=sk_sl):
                if j == 0:
                    E.wait_all(t, [tag + "_kvinit", tag + "_rr", tag + "_g1m"])
                else:
                    t.wait_ge(E.sem(tag + "_lp_d")[0], E.at(tag + "_lp_d")[1])
                t.transpose(PBT[0:64, 0:128], sk_sl, ident_b[:, 0:128])
                E.fin(t, tag + "_lp_tr")

            @b.vector
            def _(v, j=j, sq_sl=sq_sl):
                v.wait_ge(E.sem(tag + "_lp_tr")[0], E.at(tag + "_lp_tr")[1])
                v.tensor_copy(skT[0:64, 0:128], PBT[0:64, 0:128])
                v.tensor_copy(sqh[0:64, 0:128], sq_sl)
                E.fin(v, tag + "_lp_skT")

            @b.tensor
            def _(t, j=j):
                t.wait_ge(E.sem(tag + "_lp_skT")[0], E.at(tag + "_lp_skT")[1])
                t.matmul(PB[0][:, 0:128], skT[0:64, 0:128], sqh[0:64, 0:128],
                         start=True, stop=True)
                E.fin(t, tag + "_lp_sc")

            @b.vector
            def _(v, j=j):
                v.wait_ge(E.sem(tag + "_lp_sc")[0], E.at(tag + "_lp_sc")[1])
                v.tensor_tensor(smB[:, 0:128], PB[0][:, 0:128], mask_ut[:, 0:128],
                                ALU.mult)
                E.fin(v, tag + "_lp_sm")

            @b.tensor
            def _(t, j=j, hh=hh, v_sl=v_sl, sk_sl=sk_sl):
                t.wait_ge(E.sem(tag + "_lp_sm")[0], E.at(tag + "_lp_sm")[1])
                t.matmul(PB[1][0:1, 0:128], ones_b1[:, 0:1], smB[:, 0:128],
                         start=True, stop=False)
                t.matmul(PB[1][0:1, 0:128], kcrb[0:64, hh:hh + 1], sqh[0:64, 0:128],
                         start=False, stop=True)
                t.matmul(PB[2][0:64, 0:128], v_sl, smB[:, 0:128],
                         start=True, stop=False)
                t.matmul(PB[2][0:64, 0:128], kvrb[0:64, hh * 64:(hh + 1) * 64],
                         sqh[0:64, 0:128], start=False, stop=True)
                t.matmul(PB[3][0:64, 0:64], sk_sl, v_sl, start=True, stop=True)
                t.matmul(PB[3][0:64, 64:65], sk_sl, ones_b1[:, 0:1],
                         start=True, stop=True)
                E.fin(t, tag + "_lp_mm")

            @b.vector
            def _(v, j=j, hh=hh):
                v.wait_ge(E.sem(tag + "_lp_mm")[0], E.at(tag + "_lp_mm")[1])
                v.tensor_scalar(den_r[0:1, 0:128], PB[1][0:1, 0:128], EPS0,
                                None, ALU.max)
                E.fin(v, tag + "_lp_dr")
                v.wait_ge(E.sem(tag + "_lp_dr")[0], E.at(tag + "_lp_dr")[1])
                v.reciprocal(den_r[0:1, 0:128], den_r[0:1, 0:128])
                v.tensor_tensor(kvr[0:64, hh * 64:(hh + 1) * 64],
                                kvr[0:64, hh * 64:(hh + 1) * 64],
                                PB[3][0:64, 0:64], ALU.add)
                v.tensor_tensor(kcr[0:64, hh:hh + 1], kcr[0:64, hh:hh + 1],
                                PB[3][0:64, 64:65], ALU.add)
                v.tensor_copy(kvrb[0:64, hh * 64:(hh + 1) * 64],
                              kvr[0:64, hh * 64:(hh + 1) * 64])
                v.tensor_copy(kcrb[0:64, hh:hh + 1], kcr[0:64, hh:hh + 1])
                E.fin(v, tag + "_lp_upd")

            @b.tensor
            def _(t, j=j):
                t.wait_ge(E.sem(tag + "_lp_upd")[0], E.at(tag + "_lp_upd")[1])
                t.matmul(PB[4][0:64, 0:128], ones_1f[0:1, 0:64], den_r[0:1, 0:128],
                         start=True, stop=True)
                E.fin(t, tag + "_lp_bc")

            @b.vector
            def _(v, j=j, ci=ci, oc=oc, half=half):
                v.wait_ge(E.sem(tag + "_lp_bc")[0], E.at(tag + "_lp_bc")[1])
                v.tensor_copy(dbc[0:64, 0:128], PB[4][0:64, 0:128])
                E.fin(v, tag + "_lp_ev")
                v.wait_ge(E.sem(tag + "_lp_ev")[0], E.at(tag + "_lp_ev")[1])
                v.tensor_tensor(loc[64 * half:64 * half + 64,
                                    oc * Ti + ci * 128:oc * Ti + (ci + 1) * 128],
                                PB[2][0:64, 0:128], dbc[0:64, 0:128], ALU.mult)
                E.fin(v, tag + "_lp_d")

    @b.gpsimd
    def _(g):
        g.wait_ge(E.sem(tag + "_v_mm")[0], E.at(tag + "_v_mm")[1])
        wld(g, "woT")
        g.wait_ge(E.sem(tag + "_lp_d")[0], E.at(tag + "_lp_d")[1])
        g.dma_start(out=selrr[:, :].rearrange("p (m c o) -> p m c o", m=M_MEM, c=FCH),
                    in_=w["selRR"].ap().rearrange("m c p o -> p m c o")
                    ).then_inc(E.inc(tag + "_cld2", 16)[0], 16)
        g.dma_start(out=memb[:, :].rearrange("p (m c o) -> p m c o", m=M_MEM, c=FCH),
                    in_=w["MEMB"].ap().rearrange("m c p o -> p m c o")
                    ).then_inc(E.inc(tag + "_cld2", 16)[0], 16)

    for tt in range(ntt):
        for oc in range(FCH):
            idx = tt * FCH + oc
            b_bc = base2("_mp_bc")
            b_sq2 = base2("_mp_sq2")
            b_mm = base2("_mp_mm")

            @b.tensor
            def _(t, tt=tt, oc=oc, b_sq2=b_sq2):
                if tt == 0 and oc == 0:
                    E.wait_all(t, [tag + "_lp_d", tag + "_rr", tag + "_cld2"])
                    t.wait_ge(E.sem(tag + "_wld")[0], E.at(tag + "_wld")[1])
                else:
                    t.wait_ge(E.sem(tag + "_mp_done")[0], E.at(tag + "_mp_done")[1])
                if tt > 0 and oc == 0:
                    t.wait_ge(E.sem(tag + "_wo_add")[0], E.at(tag + "_wo_add")[1])
                for m in range(M_MEM):
                    t.matmul(PB[5][:, 0:TT], selrr[:, (m * FCH + oc) * 128:
                                                   (m * FCH + oc + 1) * 128],
                             rr[0:64, tt * TT:(tt + 1) * TT], start=True, stop=True)
                    E.fin(t, tag + "_mp_bc")
                    t.wait_ge(E.sem(tag + "_mp_sq2")[0], b_sq2 + m + 1)
                    t.matmul(PB[6][:, 0:TT], memb[:, (m * FCH + oc) * 128:
                                                  (m * FCH + oc + 1) * 128],
                             sqs[:, 0:TT], start=(m == 0), stop=(m == M_MEM - 1))
                    E.fin(t, tag + "_mp_mm")

            @b.vector
            def _(v, tt=tt, oc=oc, b_bc=b_bc, b_mm=b_mm):
                if tt == 0 and oc == 0:
                    E.wait_all(v, [tag + "_g1m"])
                for m in range(M_MEM):
                    v.wait_ge(E.sem(tag + "_mp_bc")[0], b_bc + m + 1)
                    if m >= 1:
                        v.wait_ge(E.sem(tag + "_mp_mm")[0], b_mm + m)
                    v.tensor_tensor(sqs[:, 0:TT],
                                    sq[:, oc * Ti + tt * TT:oc * Ti + (tt + 1) * TT],
                                    PB[5][:, 0:TT], ALU.mult)
                    E.fin(v, tag + "_mp_sq2")
                v.wait_ge(E.sem(tag + "_mp_mm")[0], b_mm + M_MEM)
                v.tensor_scalar(e0[:, 0:TT], PB[6][:, 0:TT], gsg[:, oc:oc + 1],
                                None, ALU.mult)
                E.fin(v, tag + "_mp_gm")
                v.wait_ge(E.sem(tag + "_mp_gm")[0], E.at(tag + "_mp_gm")[1])
                v.scalar_tensor_tensor(
                    loc[:, oc * Ti + tt * TT:oc * Ti + (tt + 1) * TT],
                    loc[:, oc * Ti + tt * TT:oc * Ti + (tt + 1) * TT],
                    g1m[:, oc:oc + 1], e0[:, 0:TT], ALU.mult, ALU.add)
                E.fin(v, tag + "_mp_done")

        for oc2 in range(FCH):
            k = tt * FCH + oc2

            @b.tensor
            def _(t, tt=tt, oc2=oc2, k=k):
                t.wait_ge(E.sem(tag + "_mp_done")[0], E.at(tag + "_mp_done")[1])
                if k >= 1:
                    t.wait_ge(E.sem(tag + "_wo_add")[0], E.at(tag + "_wo_add")[1])
                for oc in range(FCH):
                    t.matmul(PB[6][:, 0:TT],
                             w0[:, oc * 1024 + oc2 * 128:oc * 1024 + oc2 * 128 + 128],
                             loc[:, oc * Ti + tt * TT:oc * Ti + (tt + 1) * TT],
                             start=(oc == 0), stop=(oc == FCH - 1))
                E.fin(t, tag + "_wo_mm")

            @b.sync
            def _(s, tt=tt, oc2=oc2, k=k):
                if k >= 2:
                    s.wait_ge(E.sem(tag + "_wo_add")[0], E.at(tag + "_wo_add")[1] - 1)
                s.dma_start(out=xrl[k % 2][:, 0:TT],
                            in_=w["x_in"].ap()[oc2 * 128:(oc2 + 1) * 128,
                                               tt * TT:(tt + 1) * TT]
                            ).then_inc(E.inc(tag + "_xrl", 16)[0], 16)

            @b.vector
            def _(v, tt=tt, oc2=oc2, k=k):
                v.wait_ge(E.sem(tag + "_wo_mm")[0], E.at(tag + "_wo_mm")[1])
                v.wait_ge(E.sem(tag + "_xrl")[0], E.at(tag + "_xrl")[1])
                stp = tag + ("_wo_st0" if k % 2 == 0 else "_wo_st1")
                if k >= 2:
                    v.wait_ge(E.sem(stp)[0], E.at(stp)[1])
                v.tensor_tensor((e1 if k % 2 == 0 else e3)[:, 0:TT], PB[6][:, 0:TT],
                                xrl[k % 2][:, 0:TT], ALU.add)
                E.fin(v, tag + "_wo_add")

            @b.sync
            def _(s, tt=tt, oc2=oc2, k=k):
                s.wait_ge(E.sem(tag + "_wo_add")[0], E.at(tag + "_wo_add")[1])
                stp = tag + ("_wo_st0" if k % 2 == 0 else "_wo_st1")
                s.dma_start(out=w["x_out"].ap()[oc2 * 128:(oc2 + 1) * 128,
                                                tt * TT:(tt + 1) * TT],
                            in_=(e1 if k % 2 == 0 else e3)[:, 0:TT]
                            ).then_inc(E.inc(stp, 16)[0], 16)

    @b.sync
    def _(s):
        s.wait_ge(E.sem(tag + "_wo_st0")[0], E.at(tag + "_wo_st0")[1])
        s.wait_ge(E.sem(tag + "_wo_st1")[0], E.at(tag + "_wo_st1")[1])
        E.fin(s, tag + "_done")


def build_moe_tiles(nc, ts, T):
    at = ts["_at"]
    AB = ts["AB"]
    xn_b = 16 * T
    TQ = min(512, T)
    ts["moe_Y"] = at("moe_Y", [128, 8 * T], BF16, AB + xn_b)
    ts["moe_H"] = at("moe_H", [128, 22 * TQ], BF16, AB + 2 * xn_b)
    ts["moe_dw"] = at("moe_dw", [128, 22 * 1024], BF16, AB + 2 * xn_b + 22 * TQ * 2)
    goff = AB + 2 * xn_b + 22 * TQ * 2 + 45056
    ts["moe_gw"] = at("moe_gw", [128, 2 * 1024], BF16, goff)
    ts["moe_uw"] = at("moe_uw", [128, 2 * 1024], BF16, goff + 4096)
    o = max(goff + 8192, AB + 66 * T + 16384)
    ts["rw_sb"] = at("rw_sb", [128, 64], BF16, o); o += 128
    ts["wte"] = at("wte", [8, T], BF16, o); o += 2 * T
    ts["sel8"] = at("c_sel8", [8, 1024], BF16, o); o += 2048
    ts["m8"] = at("m8", [128, 8], F32, o); o += 32
    ts["nmx"] = at("nmx", [128, 1], F32, o); o += 32
    ts["sre"] = at("sre", [128, 1], F32, o); o += 32
    ts["wbc"] = at("wbc", [128, 512], F32, o); o += 2048
    ts["prb"] = at("prb", [128, 8], F32, o); o += 32
    ts["wtm"] = at("wtm", [128, 8], BF16, o); o += 32
    assert o <= 206 * 1024, f"moe arena overflow {o}"
    return ts


def emit_l0_ffn(E, Ti, ts, w, pre=(), tag="moe"):
    b = E.block
    TQ = min(512, Ti)
    ntq = Ti // TQ
    ntile = Ti // 128
    NE, NIC = 8, 22
    xn, Y, H = ts["xn"], ts["moe_Y"], ts["moe_H"]
    dw, gw, uw = ts["moe_dw"], ts["moe_gw"], ts["moe_uw"]
    rw_sb, wte, sel8 = ts["rw_sb"], ts["wte"], ts["sel8"]
    m8, nmx, sre, wbc, prb, wtm = ts["m8"], ts["nmx"], ts["sre"], ts["wbc"], ts["prb"], ts["wtm"]
    e0, e1, e2 = ts["e0"], ts["e1"], ts["e2"]
    xrl = (ts["xrl0"], ts["xrl1"])
    ident_b = ts["ident_b"]
    PB, PBT = ts["PB"], ts["PBT"]

    @b.gpsimd
    def _(g):
        g.dma_start(out=rw_sb[:, 0:64].rearrange("p (c o) -> p c o", c=FCH),
                    in_=w["rwT"].ap().rearrange("(c p) o -> p c o", p=128)
                    ).then_inc(E.inc(tag + "_cld", 16)[0], 16)
        g.dma_start(out=sel8[:, :], in_=w["sel8"].ap()
                    ).then_inc(E.inc(tag + "_cld", 16)[0], 16)

    @b.tensor
    def _(t):
        E.wait_all(t, list(pre) + [tag + "_cld"])
        for tI in range(ntile):
            for fc in range(FCH):
                t.matmul(PB[6][:, tI * 8:(tI + 1) * 8],
                         xn[:, fc * Ti + tI * 128:fc * Ti + (tI + 1) * 128],
                         rw_sb[:, fc * 8:(fc + 1) * 8],
                         start=(fc == 0), stop=(fc == FCH - 1))
        E.fin(t, tag + "_rt_mm")

    for tI in range(ntile):
        @b.vector
        def _(v, tI=tI):
            E.wait_all(v, [tag + "_rt_mm"])
            if tI >= 1:
                v.wait_ge(E.sem(tag + "_rt_ex")[0], E.at(tag + "_rt_ex")[1])
            lg = PB[6][:, tI * 8:(tI + 1) * 8]
            v.reduce_max(nmx[:, 0:1], lg, axis=AX.X, negate=True)
            E.fin(v, tag + "_rt_nm")

        @b.scalar
        def _(sc, tI=tI):
            sc.wait_ge(E.sem(tag + "_rt_nm")[0], E.at(tag + "_rt_nm")[1])
            sc.activation(prb[:, 0:8], PB[6][:, tI * 8:(tI + 1) * 8], AF.Exp,
                          bias=nmx[:, 0:1])
            E.fin(sc, tag + "_rt_ex")

        @b.vector
        def _(v, tI=tI):
            v.wait_ge(E.sem(tag + "_rt_ex")[0], E.at(tag + "_rt_ex")[1])
            if tI >= 1:
                v.wait_ge(E.sem(tag + "_rt_tr")[0], E.at(tag + "_rt_tr")[1])
            v.reduce_sum(sre[:, 0:1], prb[:, 0:8], axis=AX.X)
            v.reciprocal(sre[:, 0:1], sre[:, 0:1])
            v.tensor_scalar(prb[:, 0:8], prb[:, 0:8], sre[:, 0:1], None, ALU.mult)
            v.max(m8[:, 0:8], prb[:, 0:8])
            v.tensor_scalar(e0[:, 0:8], prb[:, 0:8], m8[:, 1:2], None, ALU.is_ge)
            v.tensor_tensor(wtm[:, 0:8], prb[:, 0:8], e0[:, 0:8], ALU.mult)
            E.fin(v, tag + "_rt_w")

        @b.tensor
        def _(t, tI=tI):
            t.wait_ge(E.sem(tag + "_rt_w")[0], E.at(tag + "_rt_w")[1])
            t.transpose(PBT[0:8, 0:128], wtm[:, 0:8], ident_b[:, 0:128])
            E.fin(t, tag + "_rt_tr")

        @b.vector
        def _(v, tI=tI):
            v.wait_ge(E.sem(tag + "_rt_tr")[0], E.at(tag + "_rt_tr")[1])
            v.tensor_copy(wte[0:8, tI * 128:(tI + 1) * 128], PBT[0:8, 0:128])
            E.fin(v, tag + "_rt_cp")

    for e in range(NE):
        @b.gpsimd
        def _(g, e=e):
            if e == 0:
                E.wait_all(g, list(pre))
            if e >= 1:
                g.wait_ge(E.sem(tag + "_dn_mm")[0], E.at(tag + "_dn_mm")[1])
            g.dma_start(out=dw[:, :].rearrange("p (ic h) -> p ic h", ic=NIC),
                        in_=w["dwT"].ap()[e].rearrange("(ic p) h -> p ic h", p=128)
                        ).then_inc(E.inc(tag + "_dwld", 16)[0], 16)

        for tq in range(ntq):
            for ic in range(NIC):
                j = ic
                @b.gpsimd
                def _(g, e=e, tq=tq, ic=ic):
                    k = (tq * NIC + ic)
                    if tq == 0 and ic == 0 and e == 0:
                        E.wait_all(g, list(pre))
                    if k >= 2 or e > 0:
                        g.wait_ge(E.sem(tag + "_h_mm")[0],
                                  max(0, E.at(tag + "_h_mm")[1] - 1))
                    gp = tag + ("_gld0" if ic % 2 == 0 else "_gld1")
                    for nm, buf in (("gwT", gw), ("uwT", uw)):
                        g.dma_start(
                            out=buf[:, (ic % 2) * 1024:(ic % 2) * 1024 + 1024]
                            .rearrange("p (c i) -> p c i", c=FCH),
                            in_=w[nm].ap()[e][:, ic * 128:(ic + 1) * 128]
                            .rearrange("(c p) i -> p c i", p=128)
                        ).then_inc(E.inc(gp, 16)[0], 16)

                @b.tensor
                def _(t, e=e, tq=tq, ic=ic):
                    gp = tag + ("_gld0" if ic % 2 == 0 else "_gld1")
                    t.wait_ge(E.sem(gp)[0], E.at(gp)[1])
                    if not (e == 0 and tq == 0 and ic == 0):
                        t.wait_ge(E.sem(tag + "_h_dve")[0],
                                  max(0, E.at(tag + "_h_dve")[1] - 1))
                    for fc in range(FCH):
                        t.matmul(PB[0 + (ic % 2)][:, 0:TQ],
                                 gw[:, (ic % 2) * 1024 + fc * 128:
                                    (ic % 2) * 1024 + (fc + 1) * 128],
                                 xn[:, fc * Ti + tq * TQ:fc * Ti + (tq + 1) * TQ],
                                 start=(fc == 0), stop=(fc == FCH - 1))
                    for fc in range(FCH):
                        t.matmul(PB[2 + (ic % 2)][:, 0:TQ],
                                 uw[:, (ic % 2) * 1024 + fc * 128:
                                    (ic % 2) * 1024 + (fc + 1) * 128],
                                 xn[:, fc * Ti + tq * TQ:fc * Ti + (tq + 1) * TQ],
                                 start=(fc == 0), stop=(fc == FCH - 1))
                    E.fin(t, tag + "_h_mm")

                @b.scalar
                def _(sc, e=e, tq=tq, ic=ic):
                    sc.wait_ge(E.sem(tag + "_h_mm")[0], E.at(tag + "_h_mm")[1])
                    sc.wait_ge(E.sem(tag + "_h_dve")[0],
                               max(0, E.at(tag + "_h_dve")[1] - 1))
                    sc.activation(e0[:, 0:TQ] if ic % 2 == 0 else e1[:, 0:TQ],
                                  PB[0 + (ic % 2)][:, 0:TQ], AF.Silu)
                    E.fin(sc, tag + "_h_act")

                @b.vector
                def _(v, e=e, tq=tq, ic=ic):
                    v.wait_ge(E.sem(tag + "_h_act")[0], E.at(tag + "_h_act")[1])
                    v.tensor_tensor(H[:, ic * TQ:(ic + 1) * TQ],
                                    PB[2 + (ic % 2)][:, 0:TQ],
                                    e0[:, 0:TQ] if ic % 2 == 0 else e1[:, 0:TQ],
                                    ALU.mult)
                    E.fin(v, tag + "_h_dve")

            @b.tensor
            def _(t, e=e, tq=tq):
                t.wait_ge(E.sem(tag + "_h_dve")[0], E.at(tag + "_h_dve")[1])
                t.wait_ge(E.sem(tag + "_rt_cp")[0], E.at(tag + "_rt_cp")[1])
                if not (e == 0 and tq == 0):
                    t.wait_ge(E.sem(tag + "_wbc_ev")[0], E.at(tag + "_wbc_ev")[1])
                t.matmul(PB[6][:, 0:TQ], sel8[:, e * 128:(e + 1) * 128],
                         wte[0:8, tq * TQ:(tq + 1) * TQ], start=True, stop=True)
                E.fin(t, tag + "_wbc_mm")

            @b.vector
            def _(v, e=e, tq=tq):
                v.wait_ge(E.sem(tag + "_wbc_mm")[0], E.at(tag + "_wbc_mm")[1])
                v.tensor_copy(wbc[:, 0:TQ], PB[6][:, 0:TQ])
                E.fin(v, tag + "_wbc_ev")

            for oc2 in range(FCH):
                @b.tensor
                def _(t, e=e, tq=tq, oc2=oc2):
                    t.wait_ge(E.sem(tag + "_dwld")[0], E.at(tag + "_dwld")[1])
                    k = oc2
                    if not (e == 0 and tq == 0 and oc2 <= 1):
                        t.wait_ge(E.sem(tag + "_dn_y")[0],
                                  max(0, E.at(tag + "_dn_y")[1] - 1))
                    for ic in range(NIC):
                        t.matmul(PB[4 + oc2 % 2][:, 0:TQ],
                                 dw[:, ic * 1024 + oc2 * 128:ic * 1024 + oc2 * 128 + 128],
                                 H[:, ic * TQ:(ic + 1) * TQ],
                                 start=(ic == 0), stop=(ic == NIC - 1))
                    E.fin(t, tag + "_dn_mm")

                @b.vector
                def _(v, e=e, tq=tq, oc2=oc2):
                    v.wait_ge(E.sem(tag + "_dn_mm")[0], E.at(tag + "_dn_mm")[1])
                    v.wait_ge(E.sem(tag + "_wbc_ev")[0], E.at(tag + "_wbc_ev")[1])
                    ysl = Y[:, oc2 * Ti + tq * TQ:oc2 * Ti + (tq + 1) * TQ]
                    if e == 0:
                        v.tensor_tensor(ysl, PB[4 + oc2 % 2][:, 0:TQ], wbc[:, 0:TQ],
                                        ALU.mult)
                    else:
                        v.tensor_tensor(e2[:, 0:TQ], PB[4 + oc2 % 2][:, 0:TQ],
                                        wbc[:, 0:TQ], ALU.mult)
                        v.tensor_tensor(ysl, ysl, e2[:, 0:TQ], ALU.add)
                    E.fin(v, tag + "_dn_y")

    TT = min(512, Ti)
    ntt = Ti // TT
    for tt in range(ntt):
        for oc2 in range(FCH):
            k = tt * FCH + oc2

            @b.sync
            def _(s, tt=tt, oc2=oc2, k=k):
                E.wait_all(s, [tag + "_dn_y"])
                stp = tag + ("_st0" if k % 2 == 0 else "_st1")
                if k >= 2:
                    s.wait_ge(E.sem(stp)[0], E.at(stp)[1])
                s.dma_start(out=xrl[k % 2][:, 0:TT],
                            in_=w["x_in"].ap()[oc2 * 128:(oc2 + 1) * 128,
                                               tt * TT:(tt + 1) * TT]
                            ).then_inc(E.inc(tag + "_xrl", 16)[0], 16)

            @b.vector
            def _(v, tt=tt, oc2=oc2, k=k):
                v.wait_ge(E.sem(tag + "_xrl")[0], E.at(tag + "_xrl")[1])
                stp = tag + ("_st0" if k % 2 == 0 else "_st1")
                if k >= 2:
                    v.wait_ge(E.sem(stp)[0], E.at(stp)[1])
                v.tensor_tensor((e1 if k % 2 == 0 else e2)[:, 0:TT],
                                Y[:, oc2 * Ti + tt * TT:oc2 * Ti + (tt + 1) * TT],
                                xrl[k % 2][:, 0:TT], ALU.add)
                E.fin(v, tag + "_add")

            @b.sync
            def _(s, tt=tt, oc2=oc2, k=k):
                s.wait_ge(E.sem(tag + "_add")[0], E.at(tag + "_add")[1])
                stp = tag + ("_st0" if k % 2 == 0 else "_st1")
                s.dma_start(out=w["x_out"].ap()[oc2 * 128:(oc2 + 1) * 128,
                                                tt * TT:(tt + 1) * TT],
                            in_=(e1 if k % 2 == 0 else e2)[:, 0:TT]
                            ).then_inc(E.inc(stp, 16)[0], 16)

    @b.sync
    def _(s):
        s.wait_ge(E.sem(tag + "_st0")[0], E.at(tag + "_st0")[1])
        s.wait_ge(E.sem(tag + "_st1")[0], E.at(tag + "_st1")[1])
        E.fin(s, tag + "_done")


def build_moe_program(T):
    nc = bass.Bass("TRN2", target_bir_lowering=False)
    w = {}
    w["x_in"] = nc.dram_tensor("x_in", [D_MODEL, T], F32, kind="ExternalInput")
    w["x_out"] = nc.dram_tensor("x_out", [D_MODEL, T], F32, kind="ExternalOutput")
    for nm, shp in (("rwT", [1024, 8]), ("gwT", [8, 1024, 2816]),
                    ("uwT", [8, 1024, 2816]), ("dwT", [8, 2816, 1024]),
                    ("sel8", [8, 1024]),
                    ("ident", [128, 128]), ("ones_f1", [128, 1]),
                    ("ones_1f", [1, 128]), ("mask_ut", [128, 128])):
        w[nm] = nc.dram_tensor(nm, shp, F32, kind="ExternalInput")
    es = ExitStack()
    with es:
        ts = build_base_tiles(nc, T)
        build_moe_tiles(nc, ts, T)
        with nc.Block() as block:
            E = Emit(nc, block, es)
            emit_const_init(E, ts, w)
            emit_load_x(E, T, ts, w)
            xn_list = [ts["xn"][:, c * T:(c + 1) * T] for c in range(FCH)]
            emit_ln(E, T, ts["x_fm"], xn_list, "ln",
                    consts=ts,
                    scratch={"sq": ts["ln_sq"], "stat0": ts["ln_s0"],
                             "stat1": ts["ln_s1"], "stat2": ts["ln_s2"],
                             "stat3": ts["ln_s3"]},
                    banks={"A": ts["psA"]},
                    pre=("ldx", "cst", "cst_g"))
            emit_l0_ffn(E, T, ts, w, pre=("ln_done",))

            @block.sync
            def _(s):
                E.wait_all(s, ["moe_done"])
    return nc


def moe_in_maps(x_flat, p, T):
    sh = {}
    sh["rwT"] = _bf(p["L0_ffn_router_w"].T)
    sh["gwT"] = _bf(p["L0_ffn_gate_w"].transpose(0, 2, 1))
    sh["uwT"] = _bf(p["L0_ffn_up_w"].transpose(0, 2, 1))
    sh["dwT"] = _bf(p["L0_ffn_down_w"].transpose(0, 2, 1))
    sel8 = np.zeros((8, 1024), np.float32)
    for e in range(8):
        sel8[e, e * 128:(e + 1) * 128] = 1.0
    sh["sel8"] = sel8
    sh["ident"] = np.eye(128, dtype=np.float32)
    sh["ones_f1"] = np.ones((128, 1), np.float32)
    sh["ones_1f"] = np.ones((1, 128), np.float32)
    jj, ii = np.meshgrid(np.arange(128), np.arange(128), indexing="ij")
    sh["mask_ut"] = (jj <= ii).astype(np.float32)
    maps = []
    for c in range(NCORES):
        d = dict(sh)
        d["x_in"] = np.ascontiguousarray(x_flat[c * T:(c + 1) * T].T)
        maps.append(d)
    return maps


def run_moe(x, p, T=None):
    B, S, D = x.shape
    x_flat = np.ascontiguousarray(x.reshape(B * S, D), dtype=np.float32)
    if T is None:
        T = B * S // NCORES
    nc = build_moe_program(T)
    maps = moe_in_maps(x_flat, p, T)
    res = run_bass_kernel_spmd(nc, maps, core_ids=list(range(NCORES)))
    outs = [np.asarray(res.results[c]["x_out"]).T for c in range(NCORES)]
    return np.concatenate(outs, 0).reshape(B, S, D)


def build_l1a_tiles(nc, ts, T):
    at = ts["_at"]
    AB = ts["AB"]
    LC = 256
    TE = T + 8
    big = 16 * T >= 30 * 1024
    lo = AB if big else None

    def lat(name, shape, dt, sz):
        nonlocal lo
        h = at(name, shape, dt, lo)
        lo += (sz + 31) // 32 * 32
        return h

    xse_off = AB + 16 * T
    ts["xse"] = at("xse", [128, 16 * TE], BF16, xse_off)
    scro = xse_off + 32 * T + 256
    ts["cvacc"] = at("cvacc", [128, T], F32, scro)
    ts["dA"] = at("dA_t", [128, 16 * LC], BF16, scro)
    ts["dbx"] = at("dbx_t", [128, 16 * LC], BF16, scro + 32 * LC)
    ts["hsc"] = at("hsc", [128, 16 * LC], BF16, scro + 64 * LC)
    ts["Bbc"] = at("Bbc", [128, 16 * LC], BF16, scro + 96 * LC)
    ts["Cbc"] = at("Cbc", [128, 16 * LC], BF16, scro + 128 * LC)
    o = scro + 160 * LC
    ts["wsl"] = at("wsl", [128, 2048], BF16, o); o += 4096
    ts["wxp"] = at("wxp_t", [128, 16 * 96], BF16, o); o += 3072
    ts["wdt"] = at("wdt_t", [64, 2048], BF16, o); o += 4096
    ts["xr2a"] = at("xr2a", [128, 512], F32, o); o += 2048
    ts["xr2b"] = at("xr2b", [128, 512], F32, o); o += 2048
    ts["res0"] = at("res0", [128, 512], BF16, o); o += 1024
    ts["res1"] = at("res1", [128, 512], BF16, o); o += 1024
    if not big:
        lo = o
        o += 30 * 1024
    ts["dtr"] = lat("dtr", [64, T], BF16, 2 * T)
    ts["BT"] = lat("BT", [16, T], BF16, 2 * T)
    ts["CT"] = lat("CT", [16, T], BF16, 2 * T)
    ts["A_sb"] = lat("A_sb", [128, 256], F32, 1024)
    ts["Ap_sb"] = lat("Ap_sb", [128, 256], F32, 1024)
    ts["states"] = lat("states", [128, 256], F32, 1024)
    ts["hacc"] = lat("hacc", [128, 256], F32, 1024)
    ts["t256"] = lat("t256", [128, 256], F32, 1024)
    ts["cw_sb"] = lat("cw_sb", [128, 64], F32, 256)
    ts["cb_sb"] = lat("cb_sb", [128, 16], F32, 64)
    ts["D_sb"] = lat("D_sb", [128, 16], F32, 64)
    ts["nsc_sb"] = lat("nsc_sb", [128, 8], F32, 32)
    ts["bdt_sb"] = lat("bdt_sb", [128, 16], F32, 64)
    ts["seln"] = lat("seln", [16, 2048], BF16, 4096)
    ts["st3"] = lat("st3", [128, 512], F32, 2048)
    ts["y_sc"] = lat("y_sc", [128, LC], F32, 4 * LC)
    ts["dsl_sc"] = lat("dsl_sc", [128, LC], F32, 4 * LC)
    ts["dslb"] = lat("dslb", [128, LC], BF16, 2 * LC)
    ts["sd_sc"] = lat("sd_sc", [128, 16], F32, 64)
    ts["rl0"] = lat("rl0", [128, LC], BF16, 2 * LC)
    ts["rl1"] = lat("rl1", [128, LC], BF16, 2 * LC)
    if big:
        assert lo <= AB + 16 * T, f"late smalls overflow xn region {lo}"
        end = o
    else:
        end = max(o, lo)
    assert end <= 206 * 1024, f"l1a arena overflow {end}"
    return ts


def emit_l1_attn(E, Ti, ts, w, pre=(), tag="l1a"):
    b = E.block
    LC = 256
    nlc = Ti // LC
    TT = min(512, Ti)
    ntt = Ti // TT
    TE = Ti + 8
    xse = ts["xse"]
    dA, dbx, hsc = ts["dA"], ts["dbx"], ts["hsc"]
    Bbc, Cbc, cvacc = ts["Bbc"], ts["Cbc"], ts["cvacc"]
    wsl, wxp, wdt = ts["wsl"], ts["wxp"], ts["wdt"]
    dtr, BT, CT = ts["dtr"], ts["BT"], ts["CT"]
    A_sb, Ap_sb, states, hacc, t256 = (ts["A_sb"], ts["Ap_sb"], ts["states"],
                                       ts["hacc"], ts["t256"])
    cw_sb, cb_sb, D_sb, bdt_sb = ts["cw_sb"], ts["cb_sb"], ts["D_sb"], ts["bdt_sb"]
    seln, st3, y_sc, sd_sc = ts["seln"], ts["st3"], ts["y_sc"], ts["sd_sc"]
    dsl_sc, dslb = ts["dsl_sc"], ts["dslb"]
    xr2 = (ts["xr2a"], ts["xr2b"])
    rl = (ts["rl0"], ts["rl1"])
    e0, e1, e3 = ts["e0"], ts["e1"], ts["e3"]
    xn = ts["xn"]
    agr, pmb, bmb = ts["agr"], ts["pmb"], ts["bmb"]
    xrl = (ts["xrl0"], ts["xrl1"])
    ones_1f = ts["ones_1f"]
    PB = ts["PB"]

    @b.gpsimd
    def _(g):
        E.wait_all(g, ["ln_done"])
        g.dma_start(out=wxp[:, :].rearrange("p (c o) -> p c o", c=16),
                    in_=w["wxp"].ap().rearrange("(c p) o -> p c o", p=128)
                    ).then_inc(E.inc(tag + "_cld", 16)[0], 16)
        g.dma_start(out=wdt[0:64, 0:2048], in_=w["wdt"].ap()
                    ).then_inc(E.inc(tag + "_cld", 16)[0], 16)

    wjobs = [(oc, tt) for oc in range(32) for tt in range(ntt)]
    cons_toks = []
    for j, (oc, tt) in enumerate(wjobs):
        if tt == 0:
            @b.gpsimd
            def _(g, j=j, oc=oc):
                gp = tag + ("_wld0" if oc % 2 == 0 else "_wld1")
                if oc < 2:
                    E.wait_all(g, ["ln_done"])
                if oc >= 2:
                    g.wait_ge(E.sem(tag + "_wi_mm")[0],
                              max(0, E.at(tag + "_wi_mm")[1] - ntt))
                g.dma_start(out=wsl[:, (oc % 2) * 1024:(oc % 2) * 1024 + 1024]
                            .rearrange("p (c i) -> p c i", c=FCH),
                            in_=w["w_in"].ap()[:, oc * 128:(oc + 1) * 128]
                            .rearrange("(c p) i -> p c i", p=128)
                            ).then_inc(E.inc(gp, 16)[0], 16)

        @b.tensor
        def _(t, j=j, oc=oc, tt=tt):
            gp = tag + ("_wld0" if oc % 2 == 0 else "_wld1")
            t.wait_ge(E.sem(gp)[0], E.at(gp)[1])
            if j == 0:
                E.wait_all(t, list(pre))
            if j >= 2:
                t.wait_ge(*cons_toks[j - 2])
            for fc in range(FCH):
                t.matmul(PB[j % 2][:, 0:TT],
                         wsl[:, (oc % 2) * 1024 + fc * 128:(oc % 2) * 1024 + (fc + 1) * 128],
                         xn[:, fc * Ti + tt * TT:fc * Ti + (tt + 1) * TT],
                         start=(fc == 0), stop=(fc == FCH - 1))
            E.fin(t, tag + "_wi_mm")

        if oc < 16:
            @b.vector
            def _(v, j=j, oc=oc, tt=tt):
                v.wait_ge(E.sem(tag + "_wi_mm")[0], E.at(tag + "_wi_mm")[1])
                v.tensor_copy(xse[:, oc * TE + 8 + tt * TT:oc * TE + 8 + (tt + 1) * TT],
                              PB[j % 2][:, 0:TT])
                E.fin(v, tag + "_wi_cv")
            cons_toks.append((E.sem(tag + "_wi_cv")[0], E.at(tag + "_wi_cv")[1]))
        else:
            j2 = (oc - 16) * ntt + tt

            @b.scalar
            def _(sc, j=j, oc=oc, tt=tt, j2=j2):
                sc.wait_ge(E.sem(tag + "_wi_mm")[0], E.at(tag + "_wi_mm")[1])
                stp = tag + ("_rs0" if j2 % 2 == 0 else "_rs1")
                if j2 >= 2:
                    sc.wait_ge(E.sem(stp)[0], E.at(stp)[1])
                sc.activation((ts["res0"] if j2 % 2 == 0 else ts["res1"])[:, 0:TT],
                              PB[j % 2][:, 0:TT], AF.Silu)
                E.fin(sc, tag + "_wi_ca")
            cons_toks.append((E.sem(tag + "_wi_ca")[0], E.at(tag + "_wi_ca")[1]))

            @b.sync
            def _(s, oc=oc, tt=tt, j2=j2):
                s.wait_ge(E.sem(tag + "_wi_ca")[0], E.at(tag + "_wi_ca")[1])
                stp = tag + ("_rs0" if j2 % 2 == 0 else "_rs1")
                s.dma_start(out=w["res_sp"].ap()[(oc - 16) * 128:(oc - 15) * 128,
                                                 tt * TT:(tt + 1) * TT],
                            in_=(ts["res0"] if j2 % 2 == 0 else ts["res1"])[:, 0:TT]
                            ).then_inc(E.inc(stp, 16)[0], 16)

    @b.gpsimd
    def _(g):
        g.wait_ge(E.sem(tag + "_wi_mm")[0], E.at(tag + "_wi_mm")[1])
        g.dma_start(out=seln[0:16, :], in_=w["selN"].ap()
                    ).then_inc(E.inc(tag + "_cld2", 16)[0], 16)
        for nm, buf in (("cwP", cw_sb), ("cbP", cb_sb), ("DP", D_sb),
                        ("bdtP", bdt_sb)):
            g.dma_start(out=buf[:, :], in_=w[nm].ap()
                        ).then_inc(E.inc(tag + "_cld2", 16)[0], 16)
        g.dma_start(out=ts["pmrow"][0:1, 0:8], in_=w["pmask"].ap()
                    ).then_inc(E.inc(tag + "_cld2", 16)[0], 16)
        g.dma_start(out=ts["bmrow"][0:1, 0:8], in_=w["hmask"].ap()
                    ).then_inc(E.inc(tag + "_cld2", 16)[0], 16)
        g.dma_start(out=A_sb[:, :].rearrange("p (c n) -> p c n", c=16),
                    in_=w["A_log"].ap().rearrange("(c p) n -> p c n", p=128)
                    ).then_inc(E.inc(tag + "_cld2", 16)[0], 16)

    @b.scalar
    def _(sc):
        E.wait_all(sc, [tag + "_cld2"])
        sc.activation(A_sb[:, :], A_sb[:, :], AF.Exp)
        E.fin(sc, tag + "_aexp")

    @b.vector
    def _(v):
        E.wait_all(v, [tag + "_aexp"])
        v.tensor_scalar(A_sb[:, :], A_sb[:, :], -1.0, None, ALU.mult)
        E.fin(v, tag + "_aneg")

    @b.tensor
    def _(t):
        E.wait_all(t, [tag + "_cld2"])
        t.matmul(PB[3][:, 0:8], ones_1f[0:1, :], ts["pmrow"][0:1, 0:8],
                 start=True, stop=True)
        t.matmul(PB[3][:, 8:16], ones_1f[0:1, :], ts["bmrow"][0:1, 0:8],
                 start=True, stop=True)
        E.fin(t, tag + "_mb_mm")

    @b.vector
    def _(v):
        E.wait_all(v, [tag + "_mb_mm"])
        v.tensor_copy(pmb[:, 0:8], PB[3][:, 0:8])
        v.tensor_copy(bmb[:, 0:8], PB[3][:, 8:16])
        E.fin(v, tag + "_mb_cp")

    @b.vector
    def _(v):
        E.wait_all(v, [tag + "_wi_cv", tag + "_wi_ca", tag + "_cld2"])
        v.memset(st3[:, 0:64], 0.0)
        for ic in range(16):
            v.tensor_copy(st3[:, ic * 4:ic * 4 + 3],
                          xse[:, ic * TE + 8 + Ti - 3:ic * TE + 8 + Ti])
        E.fin(v, tag + "_halo_st")

    @b.sync
    def _(s):
        E.wait_all(s, [tag + "_halo_st"])
        s.dma_start(out=w["cc2_in"].ap().rearrange("(c p) wd -> p c wd", p=128),
                    in_=st3[:, 0:64].rearrange("p (c wd) -> p c wd", wd=4)
                    ).then_inc(E.inc(tag + "_cc2in", 16)[0], 16)

    @b.gpsimd
    def _(g):
        E.wait_all(g, [tag + "_cc2in"])
        g.collective_compute(
            "AllGather", ALU.bypass, replica_groups=[list(range(NCORES))],
            ins=[w["cc2_in"].ap()], outs=[w["cc2_out"].ap()],
        ).then_inc(E.inc(tag + "_cc2")[0], 1)

    for r in range(NCORES):
        @b.sync
        def _(s, r=r):
            E.wait_all(s, [tag + "_cc2"])
            if r >= 1:
                s.wait_ge(E.sem(tag + "_hacc2")[0], E.at(tag + "_hacc2")[1])
            s.dma_start(out=agr[:, 0:64].rearrange("p (c wd) -> p c wd", wd=4),
                        in_=w["cc2_out"].ap()[r * 2048:(r + 1) * 2048, :]
                        .rearrange("(c p) wd -> p c wd", p=128)
                        ).then_inc(E.inc(tag + "_agl2", 16)[0], 16)

        @b.vector
        def _(v, r=r):
            v.wait_ge(E.sem(tag + "_agl2")[0], E.at(tag + "_agl2")[1])
            E.wait_all(v, [tag + "_mb_cp"])
            if r == 0:
                v.tensor_scalar(st3[:, 64:128], agr[:, 0:64], bmb[:, 0:1],
                                None, ALU.mult)
            else:
                v.scalar_tensor_tensor(st3[:, 64:128], agr[:, 0:64],
                                       bmb[:, r:r + 1], st3[:, 64:128],
                                       ALU.mult, ALU.add)
            E.fin(v, tag + "_hacc2")

    @b.vector
    def _(v):
        E.wait_all(v, [tag + "_hacc2"])
        for ic in range(16):
            v.tensor_copy(xse[:, ic * TE + 5:ic * TE + 8],
                          st3[:, 64 + ic * 4:64 + ic * 4 + 3])
        E.fin(v, tag + "_halo")

    for ic in range(16):
        @b.vector
        def _(v, ic=ic):
            E.wait_all(v, [tag + "_halo"])
            if ic >= 1:
                v.wait_ge(E.sem(tag + "_cvs")[0], E.at(tag + "_cvs")[1])
            base = ic * TE + 8
            v.tensor_scalar(cvacc[:, 0:Ti], xse[:, base - 3:base - 3 + Ti],
                            cw_sb[:, ic * 4:ic * 4 + 1], None, ALU.mult)
            for kk in range(1, 4):
                v.scalar_tensor_tensor(cvacc[:, 0:Ti],
                                       xse[:, base - 3 + kk:base - 3 + kk + Ti],
                                       cw_sb[:, ic * 4 + kk:ic * 4 + kk + 1],
                                       cvacc[:, 0:Ti], ALU.mult, ALU.add)
            E.fin(v, tag + "_cv")

        @b.scalar
        def _(sc, ic=ic):
            sc.wait_ge(E.sem(tag + "_cv")[0], E.at(tag + "_cv")[1])
            sc.activation(xse[:, ic * TE + 8:ic * TE + 8 + Ti], cvacc[:, 0:Ti],
                          AF.Silu, bias=cb_sb[:, ic:ic + 1])
            E.fin(sc, tag + "_cvs")

    for tt in range(ntt):
        for grp, (m0, m1) in enumerate(((0, 80), (80, 96))):
            @b.tensor
            def _(t, tt=tt, grp=grp, m0=m0, m1=m1):
                E.wait_all(t, [tag + "_cvs", tag + "_cld"])
                if not (tt == 0 and grp == 0):
                    t.wait_ge(E.sem(tag + "_xp_ev")[0], E.at(tag + "_xp_ev")[1])
                for ic in range(16):
                    t.matmul(PB[4][0:m1 - m0, 0:TT],
                             wxp[:, ic * 96 + m0:ic * 96 + m1],
                             xse[:, ic * TE + 8 + tt * TT:ic * TE + 8 + (tt + 1) * TT],
                             start=(ic == 0), stop=(ic == 15))
                E.fin(t, tag + "_xp_mm")

            @b.vector
            def _(v, tt=tt, grp=grp):
                v.wait_ge(E.sem(tag + "_xp_mm")[0], E.at(tag + "_xp_mm")[1])
                if grp == 0:
                    v.tensor_copy(dtr[0:64, tt * TT:(tt + 1) * TT], PB[4][0:64, 0:TT])
                    v.tensor_copy(BT[0:16, tt * TT:(tt + 1) * TT], PB[4][64:80, 0:TT])
                else:
                    v.tensor_copy(CT[0:16, tt * TT:(tt + 1) * TT], PB[4][0:16, 0:TT])
                E.fin(v, tag + "_xp_ev")

    def scan_phase(ph):
        @b.vector
        def _(v):
            if ph == 1:
                E.wait_all(v, [tag + "_xp_ev", tag + "_aneg"])
                v.memset(states[:, :], 0.0)
                v.memset(sd_sc[:, :], 0.0)
            else:
                E.wait_all(v, [tag + "_st_ld"])
                v.tensor_copy(states[:, :], hacc[:, :])
            E.fin(v, tag + "_st_init")

        for lc in range(nlc):
            for n in range(16):
                @b.tensor
                def _(t, lc=lc, n=n, ph=ph):
                    if lc == 0 and n == 0:
                        E.wait_all(t, [tag + "_st_init"])
                    if not (lc == 0 and n <= 1):
                        t.wait_ge(E.sem(tag + "_bc_ev")[0],
                                  E.at(tag + "_bc_ev")[1] - 1)
                    t.matmul((PB[6] if n % 2 == 0 else PB[5])[:, 0:LC],
                             seln[0:16, n * 128:(n + 1) * 128],
                             BT[0:16, lc * LC:(lc + 1) * LC], start=True, stop=True)
                    if ph == 2:
                        t.matmul((PB[4] if n % 2 == 0 else PB[3])[:, 0:LC],
                                 seln[0:16, n * 128:(n + 1) * 128],
                                 CT[0:16, lc * LC:(lc + 1) * LC],
                                 start=True, stop=True)
                    E.fin(t, tag + "_bc_mm")

                @b.vector
                def _(v, lc=lc, n=n, ph=ph):
                    v.wait_ge(E.sem(tag + "_bc_mm")[0], E.at(tag + "_bc_mm")[1])
                    v.tensor_copy(Bbc[:, n * LC:(n + 1) * LC],
                                  (PB[6] if n % 2 == 0 else PB[5])[:, 0:LC])
                    if ph == 2:
                        v.tensor_copy(Cbc[:, n * LC:(n + 1) * LC],
                                      (PB[4] if n % 2 == 0 else PB[3])[:, 0:LC])
                    E.fin(v, tag + "_bc_ev")

            for dc in range(16):
                if ph == 2:
                    @b.sync
                    def _(s, lc=lc, dc=dc):
                        E.wait_all(s, [tag + "_rs0", tag + "_rs1"])
                        idx = lc * 16 + dc
                        if idx >= 2:
                            s.wait_ge(E.sem(tag + "_fin_ev")[0],
                                      E.at(tag + "_fin_ev")[1] - 1)
                        rp = tag + ("_rl_ld0" if dc % 2 == 0 else "_rl_ld1")
                        s.dma_start(out=rl[dc % 2][:, 0:LC],
                                    in_=w["res_sp"].ap()[dc * 128:(dc + 1) * 128,
                                                         lc * LC:(lc + 1) * LC]
                                    ).then_inc(E.inc(rp, 16)[0], 16)

                @b.tensor
                def _(t, lc=lc, dc=dc):
                    t.wait_ge(E.sem(tag + "_bc_ev")[0], E.at(tag + "_bc_ev")[1])
                    if not (lc == 0 and dc == 0):
                        t.wait_ge(E.sem(tag + "_dsl")[0], E.at(tag + "_dsl")[1])
                    t.matmul(PB[2][:, 0:LC], wdt[0:64, dc * 128:(dc + 1) * 128],
                             dtr[0:64, lc * LC:(lc + 1) * LC], start=True, stop=True)
                    E.fin(t, tag + "_dl_mm")

                @b.scalar
                def _(sc, lc=lc, dc=dc):
                    sc.wait_ge(E.sem(tag + "_dl_mm")[0], E.at(tag + "_dl_mm")[1])
                    sc.activation(e0[:, 0:LC], PB[2][:, 0:LC], AF.Abs,
                                  bias=bdt_sb[:, dc:dc + 1])
                    sc.activation(e0[:, 0:LC], e0[:, 0:LC], AF.Exp,
                                  scale=ts["kon"][:, 2:3])
                    sc.activation(e0[:, 0:LC], e0[:, 0:LC], AF.Ln,
                                  bias=ts["kon"][:, 3:4])
                    E.fin(sc, tag + "_dl_act")

                @b.vector
                def _(v, lc=lc, dc=dc, ph=ph):
                    v.wait_ge(E.sem(tag + "_dl_act")[0], E.at(tag + "_dl_act")[1])
                    v.tensor_scalar(e1[:, 0:LC], PB[2][:, 0:LC], bdt_sb[:, dc:dc + 1],
                                    0.0, ALU.add, ALU.max)
                    v.tensor_tensor(dsl_sc[:, 0:LC], e1[:, 0:LC], e0[:, 0:LC],
                                    ALU.add)
                    v.tensor_copy(dslb[:, 0:LC], dsl_sc[:, 0:LC])
                    if ph == 1:
                        v.reduce_sum(e1[:, 0:1], dsl_sc[:, 0:LC], axis=AX.X)
                        v.tensor_tensor(sd_sc[:, dc:dc + 1], sd_sc[:, dc:dc + 1],
                                        e1[:, 0:1], ALU.add)
                    E.fin(v, tag + "_dsl")

                @b.scalar
                def _(sc, lc=lc, dc=dc):
                    sc.wait_ge(E.sem(tag + "_dsl")[0], E.at(tag + "_dsl")[1])
                    for n in range(16):
                        sc.activation(dA[:, n * LC:(n + 1) * LC], dsl_sc[:, 0:LC],
                                      AF.Exp, scale=A_sb[:, dc * 16 + n:dc * 16 + n + 1])
                    E.fin(sc, tag + "_sE")

                @b.vector
                def _(v, lc=lc, dc=dc, ph=ph):
                    v.wait_ge(E.sem(tag + "_sE")[0], E.at(tag + "_sE")[1])
                    xsl = xse[:, dc * TE + 8 + lc * LC:dc * TE + 8 + (lc + 1) * LC]
                    v.tensor_tensor(hsc[:, 0:LC], dslb[:, 0:LC], xsl, ALU.mult)
                    d0b = bass.AP(hsc, 0, [[16 * LC, 128], [0, 16], [1, LC]])
                    v.tensor_tensor(dbx[:, :].rearrange("p (n l) -> p n l", n=16),
                                    d0b, Bbc[:, :].rearrange("p (n l) -> p n l", n=16),
                                    ALU.mult)
                    E.fin(v, tag + "_sB")
                    v.wait_ge(E.sem(tag + "_sB")[0], E.at(tag + "_sB")[1])
                    for n in range(16):
                        v.tensor_tensor_scan(
                            hsc[:, n * LC:(n + 1) * LC],
                            dA[:, n * LC:(n + 1) * LC],
                            dbx[:, n * LC:(n + 1) * LC],
                            states[:, dc * 16 + n:dc * 16 + n + 1],
                            ALU.mult, ALU.add)
                    v.tensor_copy(states[:, dc * 16:(dc + 1) * 16],
                                  bass.AP(hsc, LC - 1, [[16 * LC, 128], [LC, 16]]))
                    E.fin(v, tag + "_sS")
                    if ph == 2:
                        rp = tag + ("_rl_ld0" if dc % 2 == 0 else "_rl_ld1")
                        v.wait_ge(E.sem(rp)[0], E.at(rp)[1])
                        v.tensor_tensor(hsc[:, :], hsc[:, :], Cbc[:, :], ALU.mult)
                        v.reduce_sum(y_sc[:, 0:LC],
                                     bass.AP(hsc, 0, [[16 * LC, 128], [1, LC], [LC, 16]]),
                                     axis=AX.X)
                        v.scalar_tensor_tensor(y_sc[:, 0:LC], xsl,
                                               D_sb[:, dc:dc + 1], y_sc[:, 0:LC],
                                               ALU.mult, ALU.add)
                        v.tensor_tensor(xsl, y_sc[:, 0:LC],
                                        rl[dc % 2][:, 0:LC], ALU.mult)
                        E.fin(v, tag + "_fin_ev")

    scan_phase(1)

    @b.vector
    def _(v):
        E.wait_all(v, [tag + "_dsl", tag + "_aneg"])
        for dc in range(16):
            v.tensor_scalar(Ap_sb[:, dc * 16:(dc + 1) * 16],
                            A_sb[:, dc * 16:(dc + 1) * 16],
                            sd_sc[:, dc:dc + 1], None, ALU.mult)
        E.fin(v, tag + "_ap_s")

    @b.scalar
    def _(sc):
        E.wait_all(sc, [tag + "_ap_s"])
        sc.activation(Ap_sb[:, :], Ap_sb[:, :], AF.Exp)
        E.fin(sc, tag + "_ap_e")

    @b.vector
    def _(v):
        E.wait_all(v, [tag + "_sS", tag + "_ap_e"])
        v.tensor_copy(bass.AP(st3, 0, [[512, 128], [32, 16], [1, 16]]),
                      Ap_sb[:, :].rearrange("p (c n) -> p c n", n=16))
        v.tensor_copy(bass.AP(st3, 16, [[512, 128], [32, 16], [1, 16]]),
                      states[:, :].rearrange("p (c n) -> p c n", n=16))
        E.fin(v, tag + "_st3")

    @b.sync
    def _(s):
        E.wait_all(s, [tag + "_st3"])
        s.dma_start(out=w["cc3_in"].ap().rearrange("(c p) wd -> p c wd", p=128),
                    in_=st3[:, 0:512].rearrange("p (c wd) -> p c wd", wd=32)
                    ).then_inc(E.inc(tag + "_cc3in", 16)[0], 16)

    @b.gpsimd
    def _(g):
        E.wait_all(g, [tag + "_cc3in"])
        g.collective_compute(
            "AllGather", ALU.bypass, replica_groups=[list(range(NCORES))],
            ins=[w["cc3_in"].ap()], outs=[w["cc3_out"].ap()],
        ).then_inc(E.inc(tag + "_cc3")[0], 1)

    @b.vector
    def _(v):
        E.wait_all(v, [tag + "_cc3"])
        v.memset(hacc[:, :], 0.0)
        E.fin(v, tag + "_hk")

    for r in range(NCORES):
        @b.sync
        def _(s, r=r):
            E.wait_all(s, [tag + "_cc3"])
            s.wait_ge(E.sem(tag + "_hk")[0], E.at(tag + "_hk")[1])
            s.dma_start(out=agr[:, 0:512].rearrange("p (c wd) -> p c wd", wd=32),
                        in_=w["cc3_out"].ap()[r * 2048:(r + 1) * 2048, :]
                        .rearrange("(c p) wd -> p c wd", p=128)
                        ).then_inc(E.inc(tag + "_agl3", 16)[0], 16)

        @b.vector
        def _(v, r=r):
            v.wait_ge(E.sem(tag + "_agl3")[0], E.at(tag + "_agl3")[1])
            E.wait_all(v, [tag + "_mb_cp"])
            apv = bass.AP(agr, 0, [[528, 128], [32, 16], [1, 16]])
            blv = bass.AP(agr, 16, [[528, 128], [32, 16], [1, 16]])
            hv = hacc[:, :].rearrange("p (c n) -> p c n", n=16)
            t1 = t256[:, :].rearrange("p (c n) -> p c n", n=16)
            v.tensor_tensor(t1, apv, hv, ALU.mult)
            v.tensor_tensor(t1, t1, blv, ALU.add)
            v.tensor_tensor(t1, t1, hv, ALU.subtract)
            v.scalar_tensor_tensor(hv, t1, pmb[:, r:r + 1], hv, ALU.mult, ALU.add)
            E.fin(v, tag + "_hk")

    @b.vector
    def _(v):
        E.wait_all(v, [tag + "_hk"])
        E.fin(v, tag + "_st_ld")

    scan_phase(2)

    for oc2 in range(FCH):
        for tt in range(ntt):
            k = oc2 * ntt + tt
            if tt == 0:
                @b.gpsimd
                def _(g, oc2=oc2):
                    if oc2 >= 1:
                        g.wait_ge(E.sem(tag + "_ou_mm")[0], E.at(tag + "_ou_mm")[1])
                    g.dma_start(out=wsl[:, 0:2048]
                                .rearrange("p (c i) -> p c i", c=16),
                                in_=w["w_out"].ap()[:, oc2 * 128:(oc2 + 1) * 128]
                                .rearrange("(c p) i -> p c i", p=128)
                                ).then_inc(E.inc(tag + "_owld", 16)[0], 16)

            @b.tensor
            def _(t, tt=tt, oc2=oc2, k=k):
                t.wait_ge(E.sem(tag + "_owld")[0], E.at(tag + "_owld")[1])
                E.wait_all(t, [tag + "_fin_ev"])
                if k >= 1:
                    t.wait_ge(E.sem(tag + "_ou_add")[0], E.at(tag + "_ou_add")[1])
                for ic in range(16):
                    t.matmul(PB[6][:, 0:TT],
                             wsl[:, ic * 128:(ic + 1) * 128],
                             xse[:, ic * TE + 8 + tt * TT:ic * TE + 8 + (tt + 1) * TT],
                             start=(ic == 0), stop=(ic == 15))
                E.fin(t, tag + "_ou_mm")

            @b.sync
            def _(s, tt=tt, oc2=oc2, k=k):
                stp = tag + ("_ost0" if k % 2 == 0 else "_ost1")
                if k >= 2:
                    s.wait_ge(E.sem(stp)[0], E.at(stp)[1])
                s.dma_start(out=xrl[k % 2][:, 0:TT],
                            in_=w["x_in"].ap()[oc2 * 128:(oc2 + 1) * 128,
                                               tt * TT:(tt + 1) * TT]
                            ).then_inc(E.inc(tag + "_oxr", 16)[0], 16)
                s.dma_start(out=xr2[k % 2][:, 0:TT],
                            in_=w["xn32"].ap()[oc2 * 128:(oc2 + 1) * 128,
                                               tt * TT:(tt + 1) * TT]
                            ).then_inc(E.inc(tag + "_oxn", 16)[0], 16)

            @b.vector
            def _(v, tt=tt, oc2=oc2, k=k):
                v.wait_ge(E.sem(tag + "_ou_mm")[0], E.at(tag + "_ou_mm")[1])
                v.wait_ge(E.sem(tag + "_oxr")[0], E.at(tag + "_oxr")[1])
                v.wait_ge(E.sem(tag + "_oxn")[0], E.at(tag + "_oxn")[1])
                stp = tag + ("_ost0" if k % 2 == 0 else "_ost1")
                if k >= 2:
                    v.wait_ge(E.sem(stp)[0], E.at(stp)[1])
                buf = (e1 if k % 2 == 0 else e3)
                v.tensor_tensor(buf[:, 0:TT], PB[6][:, 0:TT], xrl[k % 2][:, 0:TT],
                                ALU.add)
                v.tensor_tensor(buf[:, 0:TT], buf[:, 0:TT], xr2[k % 2][:, 0:TT],
                                ALU.add)
                E.fin(v, tag + "_ou_add")

            @b.sync
            def _(s, tt=tt, oc2=oc2, k=k):
                s.wait_ge(E.sem(tag + "_ou_add")[0], E.at(tag + "_ou_add")[1])
                stp = tag + ("_ost0" if k % 2 == 0 else "_ost1")
                s.dma_start(out=w["x_out"].ap()[oc2 * 128:(oc2 + 1) * 128,
                                                tt * TT:(tt + 1) * TT],
                            in_=(e1 if k % 2 == 0 else e3)[:, 0:TT]
                            ).then_inc(E.inc(stp, 16)[0], 16)

    @b.sync
    def _(s):
        s.wait_ge(E.sem(tag + "_ost0")[0], E.at(tag + "_ost0")[1])
        s.wait_ge(E.sem(tag + "_ost1")[0], E.at(tag + "_ost1")[1])
        E.fin(s, tag + "_done")


def build_l1a_program(T):
    nc = bass.Bass("TRN2", target_bir_lowering=False)
    w = {}
    w["x_in"] = nc.dram_tensor("x_in", [D_MODEL, T], F32, kind="ExternalInput")
    w["x_out"] = nc.dram_tensor("x_out", [D_MODEL, T], F32, kind="ExternalOutput")
    for nm, shp in (("w_in", [1024, 4096]), ("wxp", [2048, 96]),
                    ("wdt", [64, 2048]), ("w_out", [2048, 1024]),
                    ("A_log", [2048, 16]), ("selN", [16, 2048]),
                    ("cwP", [128, 64]), ("cbP", [128, 16]), ("DP", [128, 16]),
                    ("nscP", [128, 8]), ("bdtP", [128, 16]),
                    ("ident", [128, 128]), ("ones_f1", [128, 1]),
                    ("ones_1f", [1, 128]), ("mask_ut", [128, 128]),
                    ("pmask", [1, 8]), ("hmask", [1, 8])):
        w[nm] = nc.dram_tensor(nm, shp, F32, kind="ExternalInput")
    w["xn32"] = nc.dram_tensor("xn32", [D_MODEL, T], F32)
    w["res_sp"] = nc.dram_tensor("res_sp", [2048, T], BF16)
    w["cc2_in"] = nc.dram_tensor("cc2_in", [2048, 4], F32)
    w["cc2_out"] = nc.dram_tensor("cc2_out", [NCORES * 2048, 4], F32)
    w["cc3_in"] = nc.dram_tensor("cc3_in", [2048, 32], F32)
    w["cc3_out"] = nc.dram_tensor("cc3_out", [NCORES * 2048, 32], F32)
    es = ExitStack()
    with es:
        ts = build_base_tiles(nc, T)
        build_l1a_tiles(nc, ts, T)
        with nc.Block() as block:
            E = Emit(nc, block, es)
            emit_const_init(E, ts, w)
            emit_load_x(E, T, ts, w)
            @block.gpsimd
            def _(g):
                g.dma_start(out=ts["nsc_sb"][:, :], in_=w["nscP"].ap()
                            ).then_inc(E.inc("nscld", 16)[0], 16)
            xn_list = [ts["xn"][:, c * T:(c + 1) * T] for c in range(FCH)]
            x32 = ts["x_fm"]
            emit_ln(E, T, ts["x_fm"], None, "ln", xn_f32=x32,
                    consts=ts,
                    scratch={"sq": ts["ln_sq"], "stat0": ts["ln_s0"],
                             "stat1": ts["ln_s1"], "stat2": ts["ln_s2"],
                             "stat3": ts["ln_s3"]},
                    banks={"A": ts["psA"]},
                    pre=("ldx", "cst", "cst_g"))

            @block.sync
            def _(s):
                E.wait_all(s, ["ln_done"])
                for c in range(FCH):
                    s.dma_start(out=w["xn32"].ap()[c * 128:(c + 1) * 128, :],
                                in_=ts["x_fm"][c][:, :]
                                ).then_inc(E.inc("l1a_xnsp", 16)[0], 16)

            emit_ln(E, T, None, xn_list, "rms", rms_of=x32, nsc=ts["nsc_sb"],
                    consts=ts,
                    scratch={"sq": ts["ln_sq"], "stat0": ts["ln_s0"],
                             "stat1": ts["ln_s1"], "stat2": ts["ln_s2"],
                             "stat3": ts["ln_s3"]},
                    banks={"A": ts["psA"]},
                    pre=("ln_done", "nscld", "l1a_xnsp"))
            emit_l1_attn(E, T, ts, w, pre=("rms_done",))

            @block.sync
            def _(s):
                E.wait_all(s, ["l1a_done"])
    return nc


def l1a_in_maps(x_flat, p, T):
    sh = {}
    sh["w_in"] = _bf(p["L1_attn_w_in"])
    sh["wxp"] = _bf(p["L1_attn_w_xproj"])
    sh["wdt"] = _bf(p["L1_attn_w_dt"])
    sh["w_out"] = _bf(p["L1_attn_w_out"])
    sh["A_log"] = _bf(p["L1_attn_A_log"])
    seln = np.zeros((16, 2048), np.float32)
    for n in range(16):
        seln[n, n * 128:(n + 1) * 128] = 1.0
    sh["selN"] = seln
    cw = p["L1_attn_conv_w"][:, 0, :]
    cwP = np.zeros((128, 64), np.float32)
    cbP = np.zeros((128, 16), np.float32)
    DP = np.zeros((128, 16), np.float32)
    bdtP = np.zeros((128, 16), np.float32)
    for c in range(16):
        for kk in range(4):
            cwP[:, c * 4 + kk] = cw[kk, c * 128:(c + 1) * 128]
        cbP[:, c] = p["L1_attn_conv_b"][c * 128:(c + 1) * 128]
        DP[:, c] = p["L1_attn_D"][c * 128:(c + 1) * 128]
        bdtP[:, c] = p["L1_attn_b_dt"][c * 128:(c + 1) * 128]
    sh["cwP"], sh["cbP"], sh["DP"], sh["bdtP"] = cwP, cbP, DP, bdtP
    nscP = np.zeros((128, 8), np.float32)
    for c in range(8):
        nscP[:, c] = p["L1_attn_norm_scale"][c * 128:(c + 1) * 128]
    sh["nscP"] = nscP
    sh["ident"] = np.eye(128, dtype=np.float32)
    sh["ones_f1"] = np.ones((128, 1), np.float32)
    sh["ones_1f"] = np.ones((1, 128), np.float32)
    jj, ii = np.meshgrid(np.arange(128), np.arange(128), indexing="ij")
    sh["mask_ut"] = (jj <= ii).astype(np.float32)
    maps = []
    CPB = NCORES // 2
    for c in range(NCORES):
        d = dict(sh)
        d["x_in"] = np.ascontiguousarray(x_flat[c * T:(c + 1) * T].T)
        core = prep_core_inputs(x_flat, c, T)
        d["pmask"] = core["pmask"]
        d["hmask"] = core["halo_sel"]
        maps.append(d)
    return maps


def run_l1a(x, p, T=None):
    B, S, D = x.shape
    x_flat = np.ascontiguousarray(x.reshape(B * S, D), dtype=np.float32)
    if T is None:
        T = B * S // NCORES
    nc = build_l1a_program(T)
    maps = l1a_in_maps(x_flat, p, T)
    res = run_bass_kernel_spmd(nc, maps, core_ids=list(range(NCORES)))
    outs = [np.asarray(res.results[c]["x_out"]).T for c in range(NCORES)]
    return np.concatenate(outs, 0).reshape(B, S, D)


def build_l1f_tiles(nc, ts, T):
    at = ts["_at"]
    AB = ts["AB"]
    ntile = T // 128

    def mk_cursor(start):
        pos = [start]

        def nt(name, shape, dt, sz):
            h = at(name, shape, dt, pos[0])
            pos[0] += (sz + 31) // 32 * 32
            return h
        return nt, pos

    ts["f_xtm"] = at("f_xtm", [128, ntile * 1024], BF16, AB + 16 * T)
    ts["f_ot"] = at("f_ot", [128, 8 * T], BF16, AB + 16 * T)

    nt, p1 = mk_cursor(AB + 32 * T)
    ts["f_alphaT"] = nt("f_alphaT", [64, T], BF16, 2 * T)
    ts["f_browb"] = nt("f_browb", [1, 10816], BF16, 21632)
    ts["f_st4"] = nt("f_st4", [64, 1040], F32, 4160)
    ts["f_acc"] = nt("f_acc", [64, 1040], F32, 4160)
    ts["f_s0T"] = nt("f_s0T", [128, 8 * 64], BF16, 1024)
    ts["f_s0"] = nt("f_s0", [64, 1024], F32, 4096)
    ts["f_rows"] = nt("f_rows", [1, 1024], F32, 4096)
    ts["f_smc"] = nt("f_smc", [128, 8], BF16, 16)
    ts["f_smf"] = nt("f_smf", [128, 8], F32, 32)
    ts["f_tsc"] = nt("f_tsc", [128, 1], F32, 32)
    ts["f_bias"] = nt("f_bias", [128, 4], F32, 32)
    ts["f_b1p"] = nt("f_b1p", [128, 4], F32, 32)
    ts["f_bop"] = nt("f_bop", [128, 8], F32, 32)
    ts["f_r1"] = nt("f_r1", [64, 4], F32, 16)
    if "m8" not in ts:
        ts["m8"] = nt("m8", [128, 8], F32, 32)
        ts["nmx"] = nt("nmx", [128, 1], F32, 32)
        ts["sre"] = nt("sre", [128, 1], F32, 32)
    ts["f_wq"] = nt("f_wq", [128, 8 * 256], BF16, 4096)
    ntd, p2 = mk_cursor(p1[0])
    ts["f_h1"] = ntd("f_h1", [128, 4 * T], BF16, 8 * T)
    ts["f_alpha"] = ntd("f_alpha", [128, ntile * 64], BF16, T)
    ts["f_w1a"] = ntd("f_w1a", [128, 8 * 512], BF16, 8192)
    ts["f_w1b"] = ntd("f_w1b", [128, 8 * 512], BF16, 8192)
    ts["f_w2"] = ntd("f_w2", [128, 4 * 64], BF16, 512)
    adead0, adead1 = p1[0], p2[0]

    ntc, p4 = mk_cursor(adead0)
    ts["f_gi"] = ntc("f_gi", [64, 3072], BF16, 6144)
    ts["f_gh"] = ntc("f_gh", [64, 3072], BF16, 6144)
    ts["f_sn"] = ntc("f_sn", [64, 1024], F32, 4096)
    ts["f_snT"] = ntc("f_snT", [128, 8 * 64], BF16, 1024)
    ts["f_qs"] = ntc("f_qs", [64, 256], F32, 1024)
    ts["f_ks"] = ntc("f_ks", [64, 256], F32, 1024)
    ts["f_vs"] = ntc("f_vs", [64, 1024], BF16, 2048)
    ts["f_qT"] = ntc("f_qT", [128, 2 * 64], BF16, 256)
    ts["f_kT"] = ntc("f_kT", [128, 2 * 64], BF16, 256)
    ts["f_atT"] = ntc("f_atT", [64, 64], BF16, 128)
    ts["f_y"] = ntc("f_y", [64, 1024], F32, 4096)
    ts["f_ynT"] = ntc("f_ynT", [128, 8 * 64], BF16, 1024)
    ts["f_sv"] = ntc("f_sv", [64, 1024], BF16, 2048)
    ts["f_s64a"] = ntc("f_s64a", [64, 1024], BF16, 2048)
    ts["f_s64b"] = ntc("f_s64b", [64, 1024], F32, 4096)
    bp1 = AB if 16 * T >= 30 * 1024 else max(p2[0], p4[0])
    ntb, p3 = mk_cursor(bp1)
    ts["f_wbig"] = ntb("f_wbig", [128, 8 * 1024], BF16, 16384)
    ts["f_wih"] = ntb("f_wih", [128, 8 * 512], BF16, 8192)
    ts["f_si"] = ntb("f_si", [64, 1024], F32, 4096)
    ts["f_siT"] = ntb("f_siT", [128, 8 * 64], BF16, 1024)
    if bp1 == AB:
        assert p3[0] <= AB + 16 * T, f"l1f B1 overflow {p3[0]}"
    end = max(p2[0], p4[0], p3[0] if bp1 != AB else 0)
    assert end <= 206 * 1024, f"l1f arena overflow {end}"
    return ts


def emit_l1_ffn(E, Ti, ts, w, pre=(), tag="l1f"):
    b = E.block
    TT = min(512, Ti)
    ntt = Ti // TT
    ntile = Ti // 128
    xn = ts["xn"]
    xtm, h1, alpha, alphaT, ot = (ts["f_xtm"], ts["f_h1"], ts["f_alpha"],
                                  ts["f_alphaT"], ts["f_ot"])
    rows, browb = ts["f_rows"], ts["f_browb"]
    e0, e1, e3 = ts["e0"], ts["e1"], ts["e3"]
    xrl = (ts["xrl0"], ts["xrl1"])
    ones_1f, ones_b1, ident_b = ts["ones_1f"], ts["ones_b1"], ts["ident_b"]
    PB, PBT = ts["PB"], ts["PBT"]
    bmb = ts["bmb"]
    m8, nmx, sre = ts["m8"], ts["nmx"], ts["sre"]

    @b.gpsimd
    def _(g):
        E.wait_all(g, list(pre))
        for nm, buf, c in (("w1a", ts["f_w1a"], 8), ("w1b", ts["f_w1b"], 8),
                           ("w2w", ts["f_w2"], 4), ("wqw", ts["f_wq"], 8)):
            g.dma_start(out=buf[:, :].rearrange("p (c o) -> p c o", c=c),
                        in_=w[nm].ap().rearrange("(c p) o -> p c o", p=128)
                        ).then_inc(E.inc(tag + "_cld", 16)[0], 16)
        g.dma_start(out=ts["f_s0T"][:, :].rearrange("p (c o) -> p c o", c=8),
                    in_=w["slot_initT"].ap().rearrange("(c p) o -> p c o", p=128)
                    ).then_inc(E.inc(tag + "_cld", 16)[0], 16)
        g.dma_start(out=ts["f_s0"][0:64, :], in_=w["slot_init"].ap()
                    ).then_inc(E.inc(tag + "_cld", 16)[0], 16)
        g.dma_start(out=rows[0:1, 0:1], in_=w["tau"].ap()
                    ).then_inc(E.inc(tag + "_cld", 16)[0], 16)
        g.dma_start(out=ts["bmrow"][0:1, 0:8], in_=w["bmask"].ap()
                    ).then_inc(E.inc(tag + "_cld", 16)[0], 16)
        g.dma_start(out=ts["f_b1p"][:, 0:4], in_=w["b1P"].ap()
                    ).then_inc(E.inc(tag + "_cld", 16)[0], 16)
        g.dma_start(out=ts["f_bop"][:, 0:8], in_=w["boP"].ap()
                    ).then_inc(E.inc(tag + "_cld", 16)[0], 16)
        g.dma_start(out=browb[0:1, 0:10816], in_=w["brows"].ap()
                    ).then_inc(E.inc(tag + "_cld", 16)[0], 16)

    @b.tensor
    def _(t):
        E.wait_all(t, [tag + "_cld"])
        t.matmul(PB[3][:, 0:8], ones_1f[0:1, :], ts["bmrow"][0:1, 0:8],
                 start=True, stop=True)
        E.fin(t, tag + "_mb_mm")

    @b.vector
    def _(v):
        E.wait_all(v, [tag + "_mb_mm"])
        v.tensor_copy(bmb[:, 0:8], PB[3][:, 0:8])
        E.fin(v, tag + "_mb_cp")

    @b.scalar
    def _(sc):
        E.wait_all(sc, [tag + "_cld"])
        sc.activation(rows[0:1, 1:2], rows[0:1, 0:1], AF.Abs)
        E.fin(sc, tag + "_tau1")

    @b.vector
    def _(v):
        E.wait_all(v, [tag + "_tau1"])
        v.tensor_scalar(rows[0:1, 1:2], rows[0:1, 1:2], 0.1, None, ALU.add)
        v.reciprocal(rows[0:1, 1:2], rows[0:1, 1:2])
        E.fin(v, tag + "_tau2")

    @b.tensor
    def _(t):
        E.wait_all(t, [tag + "_tau2"])
        t.matmul(PB[3][:, 8:9], ones_1f[0:1, :], rows[0:1, 1:2],
                 start=True, stop=True)
        E.fin(t, tag + "_tau3")

    @b.vector
    def _(v):
        E.wait_all(v, [tag + "_tau3"])
        v.tensor_copy(ts["f_tsc"][:, 0:1], PB[3][:, 8:9])
        E.fin(v, tag + "_tau")

    @b.vector
    def _(v):
        E.wait_all(v, [tag + "_cld"])
        for c in range(8):
            v.reduce_sum(ts["f_smf"][:, c:c + 1],
                         ts["f_s0T"][:, c * 64:(c + 1) * 64], axis=AX.X)
        v.tensor_scalar(ts["f_smc"][:, 0:8], ts["f_smf"][:, 0:8], 1.0 / 64,
                        None, ALU.mult)
        E.fin(v, tag + "_smc")

    @b.tensor
    def _(t):
        E.wait_all(t, [tag + "_smc"])
        for c in range(8):
            t.matmul(PB[4][0:1, 0:512], ts["f_smc"][:, c:c + 1],
                     ts["f_w1b"][:, c * 512:(c + 1) * 512],
                     start=(c == 0), stop=(c == 7))
        E.fin(t, tag + "_cv_mm")

    @b.vector
    def _(v):
        E.wait_all(v, [tag + "_cv_mm"])
        v.tensor_copy(rows[0:1, 512:1024], PB[4][0:1, 0:512])
        E.fin(v, tag + "_cv_ev")

    @b.sync
    def _(s):
        E.wait_all(s, [tag + "_cv_ev"])
        s.dma_start(out=w["cv_sc"].ap(), in_=rows[0:1, 512:1024]
                    ).then_inc(E.inc(tag + "_cvd1", 16)[0], 16)
        E.wait_all(s, [tag + "_cvd1"])
        s.dma_start(out=ts["f_bias"][:, 0:4],
                    in_=bass.AP(w["cv_sc"], 0, [[1, 128], [128, 4]]),
                    allow_slow_non_contiguous=True
                    ).then_inc(E.inc(tag + "_cvd2", 16)[0], 16)

    @b.vector
    def _(v):
        E.wait_all(v, [tag + "_cvd2"])
        v.tensor_tensor(ts["f_bias"][:, 0:4], ts["f_bias"][:, 0:4],
                        ts["f_b1p"][:, 0:4], ALU.add)
        E.fin(v, tag + "_bias")

    for jc in range(4):
        for tt in range(ntt):
            k = jc * ntt + tt

            @b.tensor
            def _(t, jc=jc, tt=tt, k=k):
                if k == 0:
                    E.wait_all(t, list(pre) + [tag + "_cld"])
                if k >= 2:
                    t.wait_ge(E.sem(tag + "_h1_act")[0],
                              E.at(tag + "_h1_act")[1] - 1)
                for fc in range(FCH):
                    t.matmul(PB[k % 2][:, 0:TT],
                             ts["f_w1a"][:, fc * 512 + jc * 128:fc * 512 + (jc + 1) * 128],
                             xn[:, fc * Ti + tt * TT:fc * Ti + (tt + 1) * TT],
                             start=(fc == 0), stop=(fc == FCH - 1))
                E.fin(t, tag + "_h1_mm")

            @b.scalar
            def _(sc, jc=jc, tt=tt, k=k):
                sc.wait_ge(E.sem(tag + "_h1_mm")[0], E.at(tag + "_h1_mm")[1])
                E.wait_all(sc, [tag + "_bias"])
                sc.activation(h1[:, jc * Ti + tt * TT:jc * Ti + (tt + 1) * TT],
                              PB[k % 2][:, 0:TT], AF.Gelu,
                              bias=ts["f_bias"][:, jc:jc + 1])
                E.fin(sc, tag + "_h1_act")

    for tI in range(ntile):
        for fc in range(FCH):
            kk = tI * FCH + fc

            @b.tensor
            def _(t, tI=tI, fc=fc, kk=kk):
                if kk == 0:
                    E.wait_all(t, list(pre))
                if kk >= 1:
                    t.wait_ge(E.sem(tag + "_xt_ev")[0], E.at(tag + "_xt_ev")[1])
                t.transpose(PBT[:, 0:128],
                            xn[:, fc * Ti + tI * 128:fc * Ti + (tI + 1) * 128],
                            ident_b[:, 0:128])
                E.fin(t, tag + "_xt_tr")

            @b.vector
            def _(v, tI=tI, fc=fc, kk=kk):
                v.wait_ge(E.sem(tag + "_xt_tr")[0], E.at(tag + "_xt_tr")[1])
                v.tensor_copy(xtm[:, tI * 1024 + fc * 128:tI * 1024 + (fc + 1) * 128],
                              PBT[:, 0:128])
                E.fin(v, tag + "_xt_ev")

    for tI in range(ntile):
        @b.tensor
        def _(t, tI=tI):
            E.wait_all(t, [tag + "_h1_act"])
            if tI >= 1:
                t.wait_ge(E.sem(tag + "_rt_w")[0], E.at(tag + "_rt_w")[1])
            for q in range(4):
                t.matmul(PB[6][:, 0:64],
                         h1[:, q * Ti + tI * 128:q * Ti + (tI + 1) * 128],
                         ts["f_w2"][:, q * 64:(q + 1) * 64],
                         start=(q == 0), stop=False)
            t.matmul(PB[6][:, 0:64], ts["ones_1b"][0:1, 0:128], browb[0:1, 0:64],
                     start=False, stop=True)
            E.fin(t, tag + "_rt_mm")

        @b.vector
        def _(v, tI=tI):
            v.wait_ge(E.sem(tag + "_rt_mm")[0], E.at(tag + "_rt_mm")[1])
            E.wait_all(v, [tag + "_tau"])
            v.tensor_scalar(e0[:, 0:64], PB[6][:, 0:64], ts["f_tsc"][:, 0:1],
                            None, ALU.mult)
            v.max(m8[:, 0:8], e0[:, 0:64])
            v.reduce_max(nmx[:, 0:1], e0[:, 0:64], axis=AX.X, negate=True)
            E.fin(v, tag + "_rt_m")

        @b.scalar
        def _(sc, tI=tI):
            sc.wait_ge(E.sem(tag + "_rt_m")[0], E.at(tag + "_rt_m")[1])
            sc.activation(e1[:, 0:64], e0[:, 0:64], AF.Exp, bias=nmx[:, 0:1])
            E.fin(sc, tag + "_rt_e")

        @b.vector
        def _(v, tI=tI):
            v.wait_ge(E.sem(tag + "_rt_e")[0], E.at(tag + "_rt_e")[1])
            v.tensor_scalar(e3[:, 0:64], e0[:, 0:64], m8[:, 7:8], None, ALU.is_ge)
            v.tensor_tensor(e1[:, 0:64], e1[:, 0:64], e3[:, 0:64], ALU.mult)
            v.reduce_sum(sre[:, 0:1], e1[:, 0:64], axis=AX.X)
            v.reciprocal(sre[:, 0:1], sre[:, 0:1])
            v.tensor_scalar(alpha[:, tI * 64:(tI + 1) * 64], e1[:, 0:64],
                            sre[:, 0:1], None, ALU.mult)
            E.fin(v, tag + "_rt_w")

        @b.tensor
        def _(t, tI=tI):
            t.wait_ge(E.sem(tag + "_rt_w")[0], E.at(tag + "_rt_w")[1])
            t.transpose(PBT[0:64, 0:128], alpha[:, tI * 64:(tI + 1) * 64],
                        ident_b[:, 0:128])
            E.fin(t, tag + "_at_tr")

        @b.vector
        def _(v, tI=tI):
            v.wait_ge(E.sem(tag + "_at_tr")[0], E.at(tag + "_at_tr")[1])
            v.tensor_copy(alphaT[0:64, tI * 128:(tI + 1) * 128], PBT[0:64, 0:128])
            E.fin(v, tag + "_at_ev")

    @b.tensor
    def _(t):
        E.wait_all(t, [tag + "_rt_w", tag + "_xt_ev"])
        for half in range(2):
            for tI in range(ntile):
                t.matmul(PB[4 + half][0:64, 0:512],
                         alpha[:, tI * 64:(tI + 1) * 64],
                         xtm[:, tI * 1024 + half * 512:tI * 1024 + (half + 1) * 512],
                         start=(tI == 0), stop=(tI == ntile - 1))
        for tI in range(ntile):
            t.matmul(PB[3][0:64, 0:1], alpha[:, tI * 64:(tI + 1) * 64],
                     ones_b1[:, 0:1], start=(tI == 0), stop=(tI == ntile - 1))
        E.fin(t, tag + "_sip_mm")

    @b.vector
    def _(v):
        E.wait_all(v, [tag + "_sip_mm"])
        v.tensor_copy(ts["f_st4"][0:64, 0:512], PB[4][0:64, 0:512])
        v.tensor_copy(ts["f_st4"][0:64, 512:1024], PB[5][0:64, 0:512])
        v.tensor_copy(ts["f_st4"][0:64, 1024:1025], PB[3][0:64, 0:1])
        v.memset(ts["f_st4"][0:64, 1025:1040], 0.0)
        E.fin(v, tag + "_sip_ev")

    @b.sync
    def _(s):
        E.wait_all(s, [tag + "_sip_ev"])
        s.dma_start(out=w["cc4_in"].ap(), in_=ts["f_st4"][0:64, 0:1040]
                    ).then_inc(E.inc(tag + "_cc4in", 16)[0], 16)

    @b.gpsimd
    def _(g):
        E.wait_all(g, [tag + "_cc4in"])
        g.collective_compute(
            "AllGather", ALU.bypass, replica_groups=[list(range(NCORES))],
            ins=[w["cc4_in"].ap()], outs=[w["cc4_out"].ap()],
        ).then_inc(E.inc(tag + "_cc4")[0], 1)

    for r in range(NCORES):
        @b.sync
        def _(s, r=r):
            E.wait_all(s, [tag + "_cc4"])
            if r >= 1:
                s.wait_ge(E.sem(tag + "_sac")[0], E.at(tag + "_sac")[1])
            s.dma_start(out=ts["f_st4"][0:64, 0:1040],
                        in_=w["cc4_out"].ap()[r * 64:(r + 1) * 64, :]
                        ).then_inc(E.inc(tag + "_agl4", 16)[0], 16)

        @b.vector
        def _(v, r=r):
            v.wait_ge(E.sem(tag + "_agl4")[0], E.at(tag + "_agl4")[1])
            E.wait_all(v, [tag + "_mb_cp"])
            if r == 0:
                v.tensor_scalar(ts["f_acc"][0:64, :], ts["f_st4"][0:64, :],
                                bmb[0:64, 0:1], None, ALU.mult)
            else:
                v.scalar_tensor_tensor(ts["f_acc"][0:64, :], ts["f_st4"][0:64, :],
                                       bmb[0:64, r:r + 1], ts["f_acc"][0:64, :],
                                       ALU.mult, ALU.add)
            E.fin(v, tag + "_sac")

    @b.vector
    def _(v):
        E.wait_all(v, [tag + "_sac"])
        v.tensor_scalar(ts["f_r1"][0:64, 0:1], ts["f_acc"][0:64, 1024:1025],
                        1e-8, None, ALU.add)
        v.reciprocal(ts["f_r1"][0:64, 0:1], ts["f_r1"][0:64, 0:1])
        v.tensor_scalar(ts["f_si"][0:64, :], ts["f_acc"][0:64, 0:1024],
                        ts["f_r1"][0:64, 0:1], None, ALU.mult)
        E.fin(v, tag + "_si")

    for c in range(8):
        @b.tensor
        def _(t, c=c):
            E.wait_all(t, [tag + "_si"])
            if c >= 1:
                t.wait_ge(E.sem(tag + "_siT_ev")[0], E.at(tag + "_siT_ev")[1])
            t.transpose(PB[6][:, 0:64], ts["f_si"][0:64, c * 128:(c + 1) * 128],
                        ts["ident_f"][0:64, 0:64])
            E.fin(t, tag + "_siT_tr")

        @b.vector
        def _(v, c=c):
            v.wait_ge(E.sem(tag + "_siT_tr")[0], E.at(tag + "_siT_tr")[1])
            v.tensor_copy(ts["f_siT"][:, c * 64:(c + 1) * 64], PB[6][:, 0:64])
            E.fin(v, tag + "_siT_ev")

    def gate_mm(nmW, lhs_sb, out_sb, tagx, brow_off):
        for jb in range(6):
            @b.gpsimd
            def _(g, jb=jb):
                E.wait_all(g, [tag + "_cld", tag + "_sip_mm"])
                if jb >= 1:
                    g.wait_ge(E.sem(tag + tagx + "_mm")[0],
                              E.at(tag + tagx + "_mm")[1])
                elif nmW == "WhhT":
                    g.wait_ge(E.sem(tag + "_gi_mm")[0], E.at(tag + "_gi_mm")[1])
                g.dma_start(out=ts["f_wih"][:, :].rearrange("p (c o) -> p c o", c=8),
                            in_=w[nmW].ap()[:, jb * 512:(jb + 1) * 512]
                            .rearrange("(c p) o -> p c o", p=128)
                            ).then_inc(E.inc(tag + tagx + "_ld", 16)[0], 16)

            @b.tensor
            def _(t, jb=jb):
                t.wait_ge(E.sem(tag + tagx + "_ld")[0], E.at(tag + tagx + "_ld")[1])
                E.wait_all(t, [tag + "_siT_ev"])
                if jb >= 1:
                    t.wait_ge(E.sem(tag + tagx + "_ev")[0],
                              E.at(tag + tagx + "_ev")[1])
                elif nmW == "WhhT":
                    t.wait_ge(E.sem(tag + "_gi_ev")[0], E.at(tag + "_gi_ev")[1])
                for c in range(8):
                    t.matmul(PB[2][0:64, 0:512],
                             lhs_sb[:, c * 64:(c + 1) * 64],
                             ts["f_wih"][:, c * 512:(c + 1) * 512],
                             start=(c == 0), stop=False)
                t.matmul(PB[2][0:64, 0:512], ts["ones_1b"][0:1, 0:64],
                         browb[0:1, brow_off + jb * 512:brow_off + (jb + 1) * 512],
                         start=False, stop=True)
                E.fin(t, tag + tagx + "_mm")

            @b.vector
            def _(v, jb=jb):
                v.wait_ge(E.sem(tag + tagx + "_mm")[0], E.at(tag + tagx + "_mm")[1])
                v.tensor_copy(out_sb[0:64, jb * 512:(jb + 1) * 512],
                              PB[2][0:64, 0:512])
                E.fin(v, tag + tagx + "_ev")

    gate_mm("WihT", ts["f_siT"], ts["f_gi"], "_gi", 64)
    gate_mm("WhhT", ts["f_s0T"], ts["f_gh"], "_gh", 64 + 3072)

    @b.vector
    def _(v):
        E.wait_all(v, [tag + "_gi_ev", tag + "_gh_ev"])
        v.tensor_tensor(ts["f_s64a"][0:64, :], ts["f_gi"][0:64, 0:1024],
                        ts["f_gh"][0:64, 0:1024], ALU.add)
        v.tensor_tensor(ts["f_s64b"][0:64, :], ts["f_gi"][0:64, 1024:2048],
                        ts["f_gh"][0:64, 1024:2048], ALU.add)
        E.fin(v, tag + "_g1")

    @b.scalar
    def _(sc):
        E.wait_all(sc, [tag + "_g1"])
        sc.activation(ts["f_s64a"][0:64, :], ts["f_s64a"][0:64, :], AF.Sigmoid)
        sc.activation(ts["f_s64b"][0:64, :], ts["f_s64b"][0:64, :], AF.Sigmoid)
        E.fin(sc, tag + "_g2")

    @b.vector
    def _(v):
        E.wait_all(v, [tag + "_g2"])
        v.tensor_tensor(ts["f_vs"][0:64, :], ts["f_s64a"][0:64, :],
                        ts["f_gh"][0:64, 2048:3072], ALU.mult)
        v.tensor_tensor(ts["f_y"][0:64, :], ts["f_vs"][0:64, :],
                        ts["f_gi"][0:64, 2048:3072], ALU.add)
        E.fin(v, tag + "_g3")

    @b.scalar
    def _(sc):
        E.wait_all(sc, [tag + "_g3"])
        sc.activation(ts["f_y"][0:64, :], ts["f_y"][0:64, :], AF.Tanh)
        E.fin(sc, tag + "_g4")

    @b.vector
    def _(v):
        E.wait_all(v, [tag + "_g4"])
        v.tensor_tensor(ts["f_sn"][0:64, :], ts["f_s0"][0:64, :],
                        ts["f_y"][0:64, :], ALU.subtract)
        v.tensor_tensor(ts["f_sn"][0:64, :], ts["f_sn"][0:64, :],
                        ts["f_s64b"][0:64, :], ALU.mult)
        v.tensor_tensor(ts["f_sn"][0:64, :], ts["f_sn"][0:64, :],
                        ts["f_y"][0:64, :], ALU.add)
        E.fin(v, tag + "_snv")

    for c in range(8):
        @b.tensor
        def _(t, c=c):
            E.wait_all(t, [tag + "_snv"])
            if c >= 1:
                t.wait_ge(E.sem(tag + "_snT_ev")[0], E.at(tag + "_snT_ev")[1])
            t.transpose(PB[6][:, 0:64], ts["f_sn"][0:64, c * 128:(c + 1) * 128],
                        ts["ident_f"][0:64, 0:64])
            E.fin(t, tag + "_snT_tr")

        @b.vector
        def _(v, c=c):
            v.wait_ge(E.sem(tag + "_snT_tr")[0], E.at(tag + "_snT_tr")[1])
            v.tensor_copy(ts["f_snT"][:, c * 64:(c + 1) * 64], PB[6][:, 0:64])
            E.fin(v, tag + "_snT_ev")

    def qkv_mm(wbuf, nout, out_sb, tagx, brow_off, wname=None, prev_ev=None):
        nblk = ceil_div(nout, 512)
        for nb in range(nblk):
            wdt_ = min(512, nout - nb * 512)

            @b.tensor
            def _(t, nb=nb, wdt_=wdt_):
                E.wait_all(t, [tag + "_snT_ev", tag + "_cld"])
                if wname is not None:
                    t.wait_ge(E.sem(tag + tagx + "_ld")[0],
                              E.at(tag + tagx + "_ld")[1])
                if nb >= 1:
                    t.wait_ge(E.sem(tag + tagx + "_ev")[0],
                              E.at(tag + tagx + "_ev")[1])
                elif prev_ev is not None:
                    t.wait_ge(E.sem(tag + prev_ev)[0], E.at(tag + prev_ev)[1])
                for c in range(8):
                    t.matmul(PB[2][0:64, 0:wdt_],
                             ts["f_snT"][:, c * 64:(c + 1) * 64],
                             wbuf[:, c * nout + nb * 512:c * nout + nb * 512 + wdt_],
                             start=(c == 0), stop=False)
                t.matmul(PB[2][0:64, 0:wdt_], ts["ones_1b"][0:1, 0:64],
                         browb[0:1, brow_off + nb * 512:brow_off + nb * 512 + wdt_],
                         start=False, stop=True)
                E.fin(t, tag + tagx + "_mm")

            @b.vector
            def _(v, nb=nb, wdt_=wdt_):
                v.wait_ge(E.sem(tag + tagx + "_mm")[0], E.at(tag + tagx + "_mm")[1])
                v.tensor_copy(out_sb[0:64, nb * 512:nb * 512 + wdt_],
                              PB[2][0:64, 0:wdt_])
                E.fin(v, tag + tagx + "_ev")

    OFF_BQ = 64 + 6144
    qkv_mm(ts["f_wq"], 256, ts["f_qs"], "_q", OFF_BQ, prev_ev="_gh_ev")

    @b.gpsimd
    def _(g):
        g.wait_ge(E.sem(tag + "_q_mm")[0], E.at(tag + "_q_mm")[1])
        g.dma_start(out=ts["f_wq"][:, :].rearrange("p (c o) -> p c o", c=8),
                    in_=w["wkw"].ap().rearrange("(c p) o -> p c o", p=128)
                    ).then_inc(E.inc(tag + "_k_ld", 16)[0], 16)

    qkv_mm(ts["f_wq"], 256, ts["f_ks"], "_k", OFF_BQ + 256, wname="wk",
           prev_ev="_q_ev")

    @b.gpsimd
    def _(g):
        E.wait_all(g, [tag + "_cld", tag + "_sip_mm"])
        g.dma_start(out=ts["f_wbig"][:, :].rearrange("p (c o) -> p c o", c=8),
                    in_=w["wvw"].ap().rearrange("(c p) o -> p c o", p=128)
                    ).then_inc(E.inc(tag + "_v_ld", 16)[0], 16)

    def vvp_mm(outname, tagx, brow_off, out_bf):
        for nb in range(2):
            @b.tensor
            def _(t, nb=nb):
                t.wait_ge(E.sem(tag + "_v_ld")[0], E.at(tag + "_v_ld")[1])
                E.wait_all(t, [tag + ("_snT_ev" if tagx == "_v" else "_ynT_ev")])
                if nb >= 1:
                    t.wait_ge(E.sem(tag + tagx + "_ev")[0],
                              E.at(tag + tagx + "_ev")[1])
                else:
                    pev = tag + ("_k_ev" if tagx == "_v" else "_v_ev")
                    t.wait_ge(E.sem(pev)[0], E.at(pev)[1])
                lhs = ts["f_snT"] if tagx == "_v" else ts["f_ynT"]
                for c in range(8):
                    t.matmul(PB[2][0:64, 0:512],
                             lhs[:, c * 64:(c + 1) * 64],
                             ts["f_wbig"][:, c * 1024 + nb * 512:c * 1024 + (nb + 1) * 512],
                             start=(c == 0), stop=False)
                t.matmul(PB[2][0:64, 0:512], ts["ones_1b"][0:1, 0:64],
                         browb[0:1, brow_off + nb * 512:brow_off + (nb + 1) * 512],
                         start=False, stop=True)
                E.fin(t, tag + tagx + "_mm")

            @b.vector
            def _(v, nb=nb):
                v.wait_ge(E.sem(tag + tagx + "_mm")[0], E.at(tag + tagx + "_mm")[1])
                v.tensor_copy(out_bf[0:64, nb * 512:(nb + 1) * 512],
                              PB[2][0:64, 0:512])
                E.fin(v, tag + tagx + "_ev")

    OFF_BV = OFF_BQ + 512
    vvp_mm("f_vs", "_v", OFF_BV, ts["f_vs"])

    for c in range(2):
        @b.tensor
        def _(t, c=c):
            E.wait_all(t, [tag + "_q_ev", tag + "_k_ev"])
            if c >= 1:
                t.wait_ge(E.sem(tag + "_qkT_ev")[0], E.at(tag + "_qkT_ev")[1])
            t.transpose(PB[6][:, 0:64], ts["f_qs"][0:64, c * 128:(c + 1) * 128],
                        ts["ident_f"][0:64, 0:64])
            t.transpose(PB[5][:, 0:64], ts["f_ks"][0:64, c * 128:(c + 1) * 128],
                        ts["ident_f"][0:64, 0:64])
            E.fin(t, tag + "_qkT_tr")

        @b.vector
        def _(v, c=c):
            v.wait_ge(E.sem(tag + "_qkT_tr")[0], E.at(tag + "_qkT_tr")[1])
            v.tensor_copy(ts["f_qT"][:, c * 64:(c + 1) * 64], PB[6][:, 0:64])
            v.tensor_copy(ts["f_kT"][:, c * 64:(c + 1) * 64], PB[5][:, 0:64])
            E.fin(v, tag + "_qkT_ev")

    @b.tensor
    def _(t):
        E.wait_all(t, [tag + "_qkT_ev"])
        for c in range(2):
            t.matmul(PB[4][0:64, 0:64], ts["f_qT"][:, c * 64:(c + 1) * 64],
                     ts["f_kT"][:, c * 64:(c + 1) * 64],
                     start=(c == 0), stop=(c == 1))
        E.fin(t, tag + "_sc_mm")

    @b.vector
    def _(v):
        E.wait_all(v, [tag + "_sc_mm"])
        v.tensor_scalar(e0[0:64, 0:64], PB[4][0:64, 0:64], 1.0 / 16, None, ALU.mult)
        v.reduce_max(nmx[0:64, 0:1], e0[0:64, 0:64], axis=AX.X, negate=True)
        E.fin(v, tag + "_sc_m")

    @b.scalar
    def _(sc):
        E.wait_all(sc, [tag + "_sc_m"])
        sc.activation(e0[0:64, 0:64], e0[0:64, 0:64], AF.Exp, bias=nmx[0:64, 0:1])
        E.fin(sc, tag + "_sc_e")

    @b.vector
    def _(v):
        E.wait_all(v, [tag + "_sc_e"])
        v.reduce_sum(sre[0:64, 0:1], e0[0:64, 0:64], axis=AX.X)
        v.reciprocal(sre[0:64, 0:1], sre[0:64, 0:1])
        v.tensor_scalar(e0[0:64, 0:64], e0[0:64, 0:64], sre[0:64, 0:1],
                        None, ALU.mult)
        E.fin(v, tag + "_sc_a")

    @b.tensor
    def _(t):
        E.wait_all(t, [tag + "_sc_a"])
        t.transpose(PB[6][0:64, 0:64], e0[0:64, 0:64], ts["ident_f"][0:64, 0:64])
        E.fin(t, tag + "_atr")

    @b.vector
    def _(v):
        E.wait_all(v, [tag + "_atr"])
        v.tensor_copy(ts["f_atT"][0:64, 0:64], PB[6][0:64, 0:64])
        E.fin(v, tag + "_atrv")

    @b.tensor
    def _(t):
        E.wait_all(t, [tag + "_atrv", tag + "_v_ev"])
        for nb in range(2):
            t.matmul(PB[4 + nb][0:64, 0:512], ts["f_atT"][0:64, 0:64],
                     ts["f_vs"][0:64, nb * 512:(nb + 1) * 512],
                     start=True, stop=True)
        E.fin(t, tag + "_scm_mm")

    @b.vector
    def _(v):
        E.wait_all(v, [tag + "_scm_mm"])
        v.tensor_tensor(ts["f_y"][0:64, 0:512], ts["f_sn"][0:64, 0:512],
                        PB[4][0:64, 0:512], ALU.add)
        v.tensor_tensor(ts["f_y"][0:64, 512:1024], ts["f_sn"][0:64, 512:1024],
                        PB[5][0:64, 0:512], ALU.add)
        v.reduce_sum(ts["f_r1"][0:64, 0:1], ts["f_y"][0:64, :], axis=AX.X)
        v.tensor_scalar(ts["f_r1"][0:64, 0:1], ts["f_r1"][0:64, 0:1], 1.0 / 1024,
                        None, ALU.mult)
        E.fin(v, tag + "_y1")

    @b.scalar
    def _(sc):
        E.wait_all(sc, [tag + "_y1"])
        sc.activation(ts["f_s64a"][0:64, :], ts["f_y"][0:64, :], AF.Square)
        E.fin(sc, tag + "_y2")

    @b.vector
    def _(v):
        E.wait_all(v, [tag + "_y2"])
        v.reduce_sum(ts["f_r1"][0:64, 1:2], ts["f_s64a"][0:64, :], axis=AX.X)
        v.tensor_scalar(ts["f_r1"][0:64, 1:2], ts["f_r1"][0:64, 1:2], 1.0 / 1024,
                        None, ALU.mult)
        v.tensor_tensor(ts["f_r1"][0:64, 2:3], ts["f_r1"][0:64, 0:1],
                        ts["f_r1"][0:64, 0:1], ALU.mult)
        v.tensor_tensor(ts["f_r1"][0:64, 1:2], ts["f_r1"][0:64, 1:2],
                        ts["f_r1"][0:64, 2:3], ALU.subtract)
        E.fin(v, tag + "_y3")

    @b.scalar
    def _(sc):
        E.wait_all(sc, [tag + "_y3"])
        sc.activation(ts["f_r1"][0:64, 1:2], ts["f_r1"][0:64, 1:2], AF.Ln,
                      bias=ts["kon"][0:64, 0:1])
        sc.activation(ts["f_r1"][0:64, 1:2], ts["f_r1"][0:64, 1:2], AF.Exp,
                      scale=ts["kon"][0:64, 1:2])
        E.fin(sc, tag + "_y4")

    @b.vector
    def _(v):
        E.wait_all(v, [tag + "_y4"])
        v.tensor_scalar(ts["f_y"][0:64, :], ts["f_y"][0:64, :],
                        ts["f_r1"][0:64, 0:1], ts["f_r1"][0:64, 1:2],
                        ALU.subtract, ALU.mult)
        E.fin(v, tag + "_y5")

    @b.tensor
    def _(t):
        E.wait_all(t, [tag + "_y5"])
        OFF_LNG = 64 + 6144 + 512 + 1024
        for nb in range(2):
            t.matmul(PB[4 + nb][0:64, 0:512], ts["ones_1b"][0:1, 0:64],
                     browb[0:1, OFF_LNG + nb * 512:OFF_LNG + (nb + 1) * 512],
                     start=True, stop=True)
            t.matmul(PB[nb][0:64, 0:512], ts["ones_1b"][0:1, 0:64],
                     browb[0:1, OFF_LNG + 1024 + nb * 512:OFF_LNG + 1024 + (nb + 1) * 512],
                     start=True, stop=True)
        E.fin(t, tag + "_lng_mm")

    @b.vector
    def _(v):
        E.wait_all(v, [tag + "_lng_mm"])
        for nb in range(2):
            v.tensor_tensor(ts["f_y"][0:64, nb * 512:(nb + 1) * 512],
                            ts["f_y"][0:64, nb * 512:(nb + 1) * 512],
                            PB[4 + nb][0:64, 0:512], ALU.mult)
            v.tensor_tensor(ts["f_y"][0:64, nb * 512:(nb + 1) * 512],
                            ts["f_y"][0:64, nb * 512:(nb + 1) * 512],
                            PB[nb][0:64, 0:512], ALU.add)
        E.fin(v, tag + "_y6")

    for c in range(8):
        @b.tensor
        def _(t, c=c):
            E.wait_all(t, [tag + "_y6"])
            if c >= 1:
                t.wait_ge(E.sem(tag + "_ynT_ev")[0], E.at(tag + "_ynT_ev")[1])
            t.transpose(PB[6][:, 0:64], ts["f_y"][0:64, c * 128:(c + 1) * 128],
                        ts["ident_f"][0:64, 0:64])
            E.fin(t, tag + "_ynT_tr")

        @b.vector
        def _(v, c=c):
            v.wait_ge(E.sem(tag + "_ynT_tr")[0], E.at(tag + "_ynT_tr")[1])
            v.tensor_copy(ts["f_ynT"][:, c * 64:(c + 1) * 64], PB[6][:, 0:64])
            E.fin(v, tag + "_ynT_ev")

    @b.gpsimd
    def _(g):
        g.wait_ge(E.sem(tag + "_v_mm")[0], E.at(tag + "_v_mm")[1])
        g.dma_start(out=ts["f_wbig"][:, :].rearrange("p (c o) -> p c o", c=8),
                    in_=w["wvpw"].ap().rearrange("(c p) o -> p c o", p=128)
                    ).then_inc(E.inc(tag + "_v_ld", 16)[0], 16)

    vvp_mm("f_sv", "_vp", 64 + 6144 + 512 + 1024 + 2048, ts["f_sv"])

    @b.gpsimd
    def _(g):
        g.wait_ge(E.sem(tag + "_vp_mm")[0], E.at(tag + "_vp_mm")[1])
        g.dma_start(out=ts["f_wbig"][:, :].rearrange("p (c o) -> p c o", c=8),
                    in_=w["wow"].ap().rearrange("(c p) o -> p c o", p=128)
                    ).then_inc(E.inc(tag + "_v_ld", 16)[0], 16)

    for dc in range(8):
        for tt in range(ntt):
            k = dc * ntt + tt

            @b.tensor
            def _(t, dc=dc, tt=tt, k=k):
                E.wait_all(t, [tag + "_vp_ev", tag + "_at_ev"])
                if k >= 2:
                    t.wait_ge(E.sem(tag + "_ot_ev")[0], E.at(tag + "_ot_ev")[1] - 1)
                t.matmul(PB[k % 2][:, 0:TT],
                         ts["f_sv"][0:64, dc * 128:(dc + 1) * 128],
                         alphaT[0:64, tt * TT:(tt + 1) * TT],
                         start=True, stop=True)
                E.fin(t, tag + "_ot_mm")

            @b.vector
            def _(v, dc=dc, tt=tt, k=k):
                v.wait_ge(E.sem(tag + "_ot_mm")[0], E.at(tag + "_ot_mm")[1])
                v.tensor_copy(ot[:, dc * Ti + tt * TT:dc * Ti + (tt + 1) * TT],
                              PB[k % 2][:, 0:TT])
                E.fin(v, tag + "_ot_ev")

    for oc2 in range(FCH):
        for tt in range(ntt):
            k = oc2 * ntt + tt

            @b.tensor
            def _(t, oc2=oc2, tt=tt, k=k):
                t.wait_ge(E.sem(tag + "_v_ld")[0], E.at(tag + "_v_ld")[1])
                E.wait_all(t, [tag + "_ot_ev"])
                if k >= 1:
                    t.wait_ge(E.sem(tag + "_fo_add")[0], E.at(tag + "_fo_add")[1])
                for dc in range(8):
                    t.matmul(PB[6][:, 0:TT],
                             ts["f_wbig"][:, dc * 1024 + oc2 * 128:dc * 1024 + oc2 * 128 + 128],
                             ot[:, dc * Ti + tt * TT:dc * Ti + (tt + 1) * TT],
                             start=(dc == 0), stop=(dc == 7))
                E.fin(t, tag + "_fo_mm")

            @b.sync
            def _(s, oc2=oc2, tt=tt, k=k):
                stp = tag + ("_fst0" if k % 2 == 0 else "_fst1")
                if k >= 2:
                    s.wait_ge(E.sem(stp)[0], E.at(stp)[1])
                s.dma_start(out=xrl[k % 2][:, 0:TT],
                            in_=w["x_in"].ap()[oc2 * 128:(oc2 + 1) * 128,
                                               tt * TT:(tt + 1) * TT]
                            ).then_inc(E.inc(tag + "_fxr", 16)[0], 16)

            @b.vector
            def _(v, oc2=oc2, tt=tt, k=k):
                v.wait_ge(E.sem(tag + "_fo_mm")[0], E.at(tag + "_fo_mm")[1])
                v.wait_ge(E.sem(tag + "_fxr")[0], E.at(tag + "_fxr")[1])
                stp = tag + ("_fst0" if k % 2 == 0 else "_fst1")
                if k >= 2:
                    v.wait_ge(E.sem(stp)[0], E.at(stp)[1])
                buf = (e1 if k % 2 == 0 else e3)
                v.tensor_scalar(buf[:, 0:TT], PB[6][:, 0:TT],
                                ts["f_bop"][:, oc2:oc2 + 1], None, ALU.add)
                v.tensor_tensor(buf[:, 0:TT], buf[:, 0:TT], xrl[k % 2][:, 0:TT],
                                ALU.add)
                E.fin(v, tag + "_fo_add")

            @b.sync
            def _(s, oc2=oc2, tt=tt, k=k):
                s.wait_ge(E.sem(tag + "_fo_add")[0], E.at(tag + "_fo_add")[1])
                stp = tag + ("_fst0" if k % 2 == 0 else "_fst1")
                s.dma_start(out=w["x_out"].ap()[oc2 * 128:(oc2 + 1) * 128,
                                                tt * TT:(tt + 1) * TT],
                            in_=(e1 if k % 2 == 0 else e3)[:, 0:TT]
                            ).then_inc(E.inc(stp, 16)[0], 16)

    @b.sync
    def _(s):
        s.wait_ge(E.sem(tag + "_fst0")[0], E.at(tag + "_fst0")[1])
        s.wait_ge(E.sem(tag + "_fst1")[0], E.at(tag + "_fst1")[1])
        E.fin(s, tag + "_done")


def build_l1f_program(T):
    nc = bass.Bass("TRN2", target_bir_lowering=False)
    w = {}
    w["x_in"] = nc.dram_tensor("x_in", [D_MODEL, T], F32, kind="ExternalInput")
    w["x_out"] = nc.dram_tensor("x_out", [D_MODEL, T], F32, kind="ExternalOutput")
    for nm, shp in (("w1a", [1024, 512]), ("w1b", [1024, 512]),
                    ("w2w", [512, 64]), ("wqw", [1024, 256]),
                    ("wkw", [1024, 256]), ("wvw", [1024, 1024]),
                    ("wvpw", [1024, 1024]), ("wow", [1024, 1024]),
                    ("WihT", [1024, 3072]), ("WhhT", [1024, 3072]),
                    ("slot_init", [64, 1024]), ("slot_initT", [1024, 64]),
                    ("tau", [1, 1]), ("brows", [1, 10816]),
                    ("b1P", [128, 4]), ("boP", [128, 8]),
                    ("ident", [128, 128]), ("ones_f1", [128, 1]),
                    ("ones_1f", [1, 128]), ("mask_ut", [128, 128]),
                    ("bmask", [1, 8])):
        w[nm] = nc.dram_tensor(nm, shp, F32, kind="ExternalInput")
    w["cv_sc"] = nc.dram_tensor("cv_sc", [1, 512], F32)
    w["cc4_in"] = nc.dram_tensor("cc4_in", [64, 1040], F32)
    w["cc4_out"] = nc.dram_tensor("cc4_out", [NCORES * 64, 1040], F32)
    es = ExitStack()
    with es:
        ts = build_base_tiles(nc, T)
        build_l1f_tiles(nc, ts, T)
        with nc.Block() as block:
            E = Emit(nc, block, es)
            emit_const_init(E, ts, w)
            emit_load_x(E, T, ts, w)
            xn_list = [ts["xn"][:, c * T:(c + 1) * T] for c in range(FCH)]
            emit_ln(E, T, ts["x_fm"], xn_list, "ln",
                    consts=ts,
                    scratch={"sq": ts["ln_sq"], "stat0": ts["ln_s0"],
                             "stat1": ts["ln_s1"], "stat2": ts["ln_s2"],
                             "stat3": ts["ln_s3"]},
                    banks={"A": ts["psA"]},
                    pre=("ldx", "cst", "cst_g"))
            emit_l1_ffn(E, T, ts, w, pre=("ln_done",))

            @block.sync
            def _(s):
                E.wait_all(s, ["l1f_done"])
    return nc


def l1f_in_maps(x_flat, p, T):
    sh = {}
    sh["w1a"] = _bf(p["L1_ffn_w1"][:1024])
    sh["w1b"] = _bf(p["L1_ffn_w1"][1024:])
    sh["w2w"] = _bf(p["L1_ffn_w2"])
    sh["wqw"] = _bf(p["L1_ffn_wq"])
    sh["wkw"] = _bf(p["L1_ffn_wk"])
    sh["wvw"] = _bf(p["L1_ffn_wv"])
    sh["wvpw"] = _bf(p["L1_ffn_wvp"])
    sh["wow"] = _bf(p["L1_ffn_wo"])
    sh["WihT"] = _bf(p["L1_ffn_Wih"].T)
    sh["WhhT"] = _bf(p["L1_ffn_Whh"].T)
    sh["slot_init"] = _bf(p["L1_ffn_slot_init"])
    sh["slot_initT"] = _bf(p["L1_ffn_slot_init"].T)
    sh["tau"] = _bf(p["L1_ffn_tau"].reshape(1, 1))
    br = np.zeros((1, 10816), np.float32)
    br[0, 0:64] = p["L1_ffn_b2"]
    br[0, 64:3136] = p["L1_ffn_bih"]
    br[0, 3136:6208] = p["L1_ffn_bhh"]
    br[0, 6208:6464] = p["L1_ffn_bq"]
    br[0, 6464:6720] = p["L1_ffn_bk"]
    br[0, 6720:7744] = p["L1_ffn_bv"]
    br[0, 7744:8768] = p["L1_ffn_ln_g"]
    br[0, 8768:9792] = p["L1_ffn_ln_b"]
    br[0, 9792:10816] = p["L1_ffn_bvp"]
    sh["brows"] = br
    b1P = np.zeros((128, 4), np.float32)
    for c in range(4):
        b1P[:, c] = p["L1_ffn_b1"][c * 128:(c + 1) * 128]
    sh["b1P"] = b1P
    boP = np.zeros((128, 8), np.float32)
    for c in range(8):
        boP[:, c] = p["L1_ffn_bo"][c * 128:(c + 1) * 128]
    sh["boP"] = boP
    sh["ident"] = np.eye(128, dtype=np.float32)
    sh["ones_f1"] = np.ones((128, 1), np.float32)
    sh["ones_1f"] = np.ones((1, 128), np.float32)
    jj, ii = np.meshgrid(np.arange(128), np.arange(128), indexing="ij")
    sh["mask_ut"] = (jj <= ii).astype(np.float32)
    maps = []
    for c in range(NCORES):
        d = dict(sh)
        d["x_in"] = np.ascontiguousarray(x_flat[c * T:(c + 1) * T].T)
        core = prep_core_inputs(x_flat, c, T)
        d["bmask"] = core["bmask"]
        maps.append(d)
    return maps


def run_l1f(x, p, T=None):
    B, S, D = x.shape
    x_flat = np.ascontiguousarray(x.reshape(B * S, D), dtype=np.float32)
    if T is None:
        T = B * S // NCORES
    nc = build_l1f_program(T)
    maps = l1f_in_maps(x_flat, p, T)
    res = run_bass_kernel_spmd(nc, maps, core_ids=list(range(NCORES)))
    outs = [np.asarray(res.results[c]["x_out"]).T for c in range(NCORES)]
    return np.concatenate(outs, 0).reshape(B, S, D)


def _ab_of(nc):
    used = nc.SBUF_PARTITION_SIZE_BYTES - nc.sbuf_bytes_remaining
    return (used + 1023) // 1024 * 1024


def build_base_tiles(nc, T, arena=True):
    ts = {}
    sb = nc.alloc_sbuf_tensor

    def at(name, shape, dt, off):
        return nc.alloc_sbuf_tensor_at(name, shape, dt, offset=off)

    ts["_at"] = at
    ts["ident_b"] = sb("c_ident_b", [128, 128], BF16)
    ts["ones_f1"] = sb("c_ones_f1", [128, 1], F32)
    ts["ones_1f"] = sb("c_ones_1f", [1, 128], F32)
    ts["ones_b1"] = sb("ones_b1", [128, 1], BF16)
    ts["mask_ut"] = sb("c_mask_ut", [128, 128], F32)
    ts["mn_sb"] = sb("mn_sb", [128, 512], BF16)
    ts["gate"] = sb("gate_t", [16, 1], F32)
    ts["gsg"] = sb("gsg", [128, 8], F32)
    ts["g1m"] = sb("g1m", [128, 8], F32)
    ts["skT"] = sb("skT", [64, 128], BF16)
    ts["sqh"] = sb("sqh", [64, 128], BF16)
    ts["kvr"] = sb("kvr", [64, 1024], F32)
    ts["kvrb"] = sb("kvrb", [64, 1024], BF16)
    ts["kcr"] = sb("kcr", [64, 16], F32)
    ts["kcrb"] = sb("kcrb", [64, 16], BF16)
    ts["den_r"] = sb("den_r", [1, 128], F32)
    ts["dbc"] = sb("dbc", [64, 128], F32)
    ts["sqs"] = sb("sqs", [128, 512], BF16)
    ts["smB"] = sb("smB", [128, 128], BF16)
    ts["e0"] = sb("e0", [128, 512], F32)
    ts["e1"] = sb("e1", [128, 512], F32)
    ts["e2"] = sb("e2", [128, 512], F32)
    ts["e3"] = sb("e3", [128, 512], F32)
    ts["xrl0"] = sb("xrl0", [128, 512], F32)
    ts["xrl1"] = sb("xrl1", [128, 512], F32)
    ts["st_sb"] = sb("st_sb", [128, 528], F32)
    ts["pfx"] = sb("pfx", [128, 528], F32)
    ts["bsm"] = sb("bsm", [128, 528], F32)
    ts["agr"] = sb("agr", [128, 528], F32)
    ts["pmrow"] = sb("pmrow", [1, 8], F32)
    ts["bmrow"] = sb("bmrow", [1, 8], F32)
    ts["pmb"] = sb("pmb", [128, 8], F32)
    ts["bmb"] = sb("bmb", [128, 8], F32)
    ts["wvec"] = sb("wvec", [1, 64], F32)
    ts["rmax"] = sb("rmax", [1, 32], F32)
    ts["wcol"] = sb("wcol", [64, 1], F32)
    ts["rmcol"] = sb("rmcol", [64, 1], F32)
    ts["kon"] = sb("kon", [128, 8], F32)
    ts["ident_f"] = sb("c_ident_f", [128, 128], F32)
    ts["ones_1b"] = sb("c_ones_1b", [1, 128], BF16)

    psA = nc.alloc_psum_tensor("psA", [128, 2048], F32)
    psB = nc.alloc_psum_tensor("psB", [128, 1536], F32)
    psBT = nc.alloc_psum_tensor("psBT", [128, 1024], BF16)
    ts["psA"], ts["psB"] = psA, psB
    PB = [psA[:, 0:512], psA[:, 512:1024], psA[:, 1024:1536], psA[:, 1536:2048],
          psB[:, 0:512], psB[:, 512:1024], psB[:, 1024:1536], psB[:, 1024:1536]]
    ts["PB"] = PB
    ts["PBT"] = psBT[:, :]
    if arena:
        build_base_arena(nc, ts, T)
    return ts


def build_base_arena(nc, ts, T):
    at = ts["_at"]
    AB = _ab_of(nc)
    ts["AB"] = AB
    xn_b = 16 * T
    ts["xn"] = at("xn", [128, 8 * T], BF16, AB)
    ts["loc"] = at("loc_t", [128, 8 * T], BF16, AB)
    ts["v"] = at("v_tm", [128, (T // 128) * 1024], BF16, AB + xn_b)
    ts["w0"] = at("w0", [128, 8192], BF16, AB + 2 * xn_b)
    ts["sq"] = at("sq_t", [128, 8 * T], BF16, AB + 2 * xn_b + 16384)
    skoff = AB + 3 * xn_b + 16384
    ts["sk"] = at("sk_tm", [128, (T // 128) * 1024], BF16, skoff)
    end = AB + 4 * xn_b + 16384
    ts["relT"] = at("relT", [64, T], BF16, end)
    ts["rr"] = ts["relT"]
    end2 = end + 2 * T
    assert end2 <= 206 * 1024, f"arena overflow {end2}"
    if 16 * T >= 16384:
        ts["selrr"] = at("selrr", [64, 4096], BF16, skoff)
        ts["memb"] = at("memb", [128, 4096], BF16, skoff + 8192)
    else:
        assert end2 + 16384 <= 206 * 1024
        ts["selrr"] = at("selrr", [64, 4096], BF16, end2)
        ts["memb"] = at("memb", [128, 4096], BF16, end2 + 8192)
    xt = []
    for c in range(FCH):
        xt.append(at(f"xf{c}", [128, T], F32, AB + c * 4 * T))
    ts["x_fm"] = xt
    ts["ln_sq"] = at("ln_sq", [128, T], F32, AB + 2 * xn_b + 16384)
    ts["ln_s0"] = at("ln_s0", [1, T], F32, skoff)
    ts["ln_s1"] = at("ln_s1", [1, T], F32, skoff + 4 * T)
    ts["ln_s2"] = at("ln_s2", [1, T], F32, skoff + 8 * T)
    ts["ln_s3"] = at("ln_s3", [1, T], F32, skoff + 12 * T)
    return ts


def emit_load_x(E, T, ts, w, tag="ldx"):
    b = E.block

    @b.sync
    def _(s):
        for c in range(FCH):
            s.dma_start(out=ts["x_fm"][c][:, :],
                        in_=w["x_in"].ap()[c * 128:(c + 1) * 128, :]
                        ).then_inc(E.inc(tag, 16)[0], 16)


def emit_const_init(E, ts, w, tag="cst"):
    b = E.block

    @b.vector
    def _(v):
        v.memset(ts["kon"][:, 0:1], 1e-5)
        v.memset(ts["kon"][:, 1:2], -0.5)
        v.memset(ts["kon"][:, 2:3], -1.0)
        v.memset(ts["kon"][:, 3:4], 1.0)
        E.fin(v, "kon")

    @b.gpsimd
    def _(g):
        g.dma_start(out=ts["ident_b"][:, :], in_=w["ident"].ap()
                    ).then_inc(E.inc(tag + "_g", 16)[0], 16)
        g.dma_start(out=ts["ones_b1"][:, :], in_=w["ones_f1"].ap()
                    ).then_inc(E.inc(tag + "_g", 16)[0], 16)
        g.dma_start(out=ts["ones_1b"][:, :], in_=w["ones_1f"].ap()
                    ).then_inc(E.inc(tag + "_g", 16)[0], 16)

    @b.sync
    def _(s):
        s.dma_start(out=ts["ones_f1"][:, :], in_=w["ones_f1"].ap()
                    ).then_inc(E.inc(tag, 16)[0], 16)
        s.dma_start(out=ts["ones_1f"][:, :], in_=w["ones_1f"].ap()
                    ).then_inc(E.inc(tag, 16)[0], 16)
        s.dma_start(out=ts["mask_ut"][:, :], in_=w["mask_ut"].ap()
                    ).then_inc(E.inc(tag, 16)[0], 16)
        s.dma_start(out=ts["ident_f"][:, :], in_=w["ident"].ap()
                    ).then_inc(E.inc(tag, 16)[0], 16)


def build_l0a_program(T):
    nc = bass.Bass("TRN2", target_bir_lowering=False)
    w = {}
    w["x_in"] = nc.dram_tensor("x_in", [D_MODEL, T], F32, kind="ExternalInput")
    w["x_out"] = nc.dram_tensor("x_out", [D_MODEL, T], F32, kind="ExternalOutput")
    for nm, shp in (("wqT", [1024, 1024]), ("wkT", [1024, 1024]),
                    ("wvT", [1024, 1024]), ("woT", [1024, 1024]),
                    ("MN", [1024, 64]), ("selRR", [M_MEM, FCH, 64, 128]),
                    ("MEMB", [M_MEM, FCH, 128, 128]), ("gate", [16, 1]),
                    ("ident", [128, 128]), ("ones_f1", [128, 1]),
                    ("ones_1f", [1, 128]), ("mask_ut", [128, 128]),
                    ("pmask", [1, 8]), ("bmask", [1, 8])):
        w[nm] = nc.dram_tensor(nm, shp, F32, kind="ExternalInput")
    w["gate_sc"] = nc.dram_tensor("gate_sc", [16, 1], F32)
    w["rm_sc"] = nc.dram_tensor("rm_sc", [64, 1], F32)
    w["cc_in"] = nc.dram_tensor("cc_in", [1024, 66], F32)
    w["cc_out"] = nc.dram_tensor("cc_out", [NCORES * 1024, 66], F32)

    es = ExitStack()
    with es:
        ts = build_base_tiles(nc, T)
        with nc.Block() as block:
            E = Emit(nc, block, es)
            emit_const_init(E, ts, w)
            emit_load_x(E, T, ts, w)
            xn_list = [ts["xn"][:, c * T:(c + 1) * T] for c in range(FCH)]
            emit_ln(E, T, ts["x_fm"], xn_list, "ln",
                    consts=ts,
                    scratch={"sq": ts["ln_sq"], "stat0": ts["ln_s0"],
                             "stat1": ts["ln_s1"], "stat2": ts["ln_s2"],
                             "stat3": ts["ln_s3"]},
                    banks={"A": ts["psA"]},
                    pre=("ldx", "cst", "cst_g"))
            xn_tiles = ts["xn"]
            ts2 = dict(ts)
            ts2["xn"] = xn_tiles
            emit_l0_attn(E, T, ts2, w, pre=("ln_done",))
            @block.sync
            def _(s):
                E.wait_all(s, ["l0a_done"])
    return nc


def _bf(x):
    return np.ascontiguousarray(x, dtype=np.float32)


def prep_shared_inputs(w):
    d = {}
    d["wqT"] = _bf(w["L0_attn_w_q"].T)
    d["wkT"] = _bf(w["L0_attn_w_k"].T)
    d["wvT"] = _bf(w["L0_attn_w_v"].T)
    d["woT"] = _bf(w["L0_attn_w_o"].T)
    mn = w["L0_attn_memory_norms"]
    MN = np.zeros((D_MODEL, 64), np.float32)
    for m in range(M_MEM):
        for h in range(NH):
            MN[h * HD:(h + 1) * HD, h * M_MEM + m] = mn[m, h]
    d["MN"] = MN
    mem = w["L0_attn_memories"]
    MEMB = np.zeros((M_MEM, FCH, 128, 128), np.float32)
    for m in range(M_MEM):
        for oc in range(FCH):
            for h2 in range(2):
                h = oc * 2 + h2
                MEMB[m, oc, h2 * 64:(h2 + 1) * 64, h2 * 64:(h2 + 1) * 64] = mem[m, h]
    d["MEMB"] = MEMB
    selRR = np.zeros((M_MEM, FCH, 64, 128), np.float32)
    for m in range(M_MEM):
        for oc in range(FCH):
            for p in range(128):
                h = oc * 2 + p // 64
                selRR[m, oc, h * M_MEM + m, p] = 1.0
    d["selRR"] = selRR
    d["gate"] = _bf(w["L0_attn_gate"].reshape(NH, 1))
    d["ident"] = np.eye(128, dtype=np.float32)
    d["ones_f1"] = np.ones((128, 1), np.float32)
    d["ones_1f"] = np.ones((1, 128), np.float32)
    jj, ii = np.meshgrid(np.arange(128), np.arange(128), indexing="ij")
    d["mask_ut"] = (jj <= ii).astype(np.float32)
    return d


def prep_core_inputs(x_flat, c, T):
    d = {}
    d["x_fm_in"] = np.ascontiguousarray(x_flat[c * T:(c + 1) * T].T)
    cores_per_b = NCORES // 2
    b = c // cores_per_b
    pmask = np.zeros((1, NCORES), np.float32)
    bmask = np.zeros((1, NCORES), np.float32)
    halo = np.zeros((1, NCORES), np.float32)
    for r in range(NCORES):
        if r // cores_per_b == b:
            bmask[0, r] = 1.0
            if r < c:
                pmask[0, r] = 1.0
    if c % cores_per_b != 0:
        halo[0, c - 1] = 1.0
    d["pmask"], d["bmask"], d["halo_sel"] = pmask, bmask, halo
    return d


def l0a_in_maps(x_flat, p, T):
    sh = prep_shared_inputs(p)
    maps = []
    for c in range(NCORES):
        d = {}
        d["x_in"] = np.ascontiguousarray(x_flat[c * T:(c + 1) * T].T)
        core = prep_core_inputs(x_flat, c, T)
        d["pmask"], d["bmask"] = core["pmask"], core["bmask"]
        for k in ("wqT", "wkT", "wvT", "woT", "MN", "selRR", "MEMB", "gate",
                  "ident", "ones_f1", "ones_1f", "mask_ut"):
            d[k] = sh[k]
        maps.append(d)
    return maps


def run_l0a(x, p, T=None):
    B, S, D = x.shape
    x_flat = np.ascontiguousarray(x.reshape(B * S, D), dtype=np.float32)
    if T is None:
        T = B * S // NCORES
    nc = build_l0a_program(T)
    maps = l0a_in_maps(x_flat, p, T)
    res = run_bass_kernel_spmd(nc, maps, core_ids=list(range(NCORES)))
    outs = [np.asarray(res.results[c]["x_out"]).T for c in range(NCORES)]
    return np.concatenate(outs, 0).reshape(B, S, D)


def build_fused_program(T):
    nc = bass.Bass("TRN2", target_bir_lowering=False)
    w = {}

    def ext(nm, shp):
        w[nm] = nc.dram_tensor(nm, shp, F32, kind="ExternalInput")

    w["x_in"] = nc.dram_tensor("x_in", [D_MODEL, T], F32, kind="ExternalInput")
    w["x_out"] = nc.dram_tensor("x_out", [D_MODEL, T], F32, kind="ExternalOutput")
    for nm in ("x1", "x2", "x3"):
        w[nm] = nc.dram_tensor(nm, [D_MODEL, T], F32)
    for nm, shp in (("ident", [128, 128]), ("ones_f1", [128, 1]),
                    ("ones_1f", [1, 128]), ("mask_ut", [128, 128]),
                    ("pmask", [1, 8]), ("bmask", [1, 8]), ("hmask", [1, 8]),
                    ("wqT", [1024, 1024]), ("wkT", [1024, 1024]),
                    ("wvT", [1024, 1024]), ("woT", [1024, 1024]),
                    ("MN", [1024, 64]), ("selRR", [M_MEM, FCH, 64, 128]),
                    ("MEMB", [M_MEM, FCH, 128, 128]), ("gate", [16, 1]),
                    ("rwT", [1024, 8]), ("gwT", [8, 1024, 2816]),
                    ("uwT", [8, 1024, 2816]), ("dwT", [8, 2816, 1024]),
                    ("sel8", [8, 1024]),
                    ("w_in", [1024, 4096]), ("wxp", [2048, 96]),
                    ("wdt", [64, 2048]), ("w_out", [2048, 1024]),
                    ("A_log", [2048, 16]), ("selN", [16, 2048]),
                    ("cwP", [128, 64]), ("cbP", [128, 16]), ("DP", [128, 16]),
                    ("nscP", [128, 8]), ("bdtP", [128, 16]),
                    ("w1a", [1024, 512]), ("w1b", [1024, 512]),
                    ("w2w", [512, 64]), ("wqw", [1024, 256]),
                    ("wkw", [1024, 256]), ("wvw", [1024, 1024]),
                    ("wvpw", [1024, 1024]), ("wow", [1024, 1024]),
                    ("WihT", [1024, 3072]), ("WhhT", [1024, 3072]),
                    ("slot_init", [64, 1024]), ("slot_initT", [1024, 64]),
                    ("tau", [1, 1]), ("brows", [1, 10816]),
                    ("b1P", [128, 4]), ("boP", [128, 8])):
        ext(nm, shp)
    w["gate_sc"] = nc.dram_tensor("gate_sc", [16, 1], F32)
    w["rm_sc"] = nc.dram_tensor("rm_sc", [64, 1], F32)
    w["cc_in"] = nc.dram_tensor("cc_in", [1024, 66], F32)
    w["cc_out"] = nc.dram_tensor("cc_out", [NCORES * 1024, 66], F32)
    w["xn32"] = nc.dram_tensor("xn32", [D_MODEL, T], F32)
    w["res_sp"] = nc.dram_tensor("res_sp", [2048, T], BF16)
    w["cc2_in"] = nc.dram_tensor("cc2_in", [2048, 4], F32)
    w["cc2_out"] = nc.dram_tensor("cc2_out", [NCORES * 2048, 4], F32)
    w["cc3_in"] = nc.dram_tensor("cc3_in", [2048, 32], F32)
    w["cc3_out"] = nc.dram_tensor("cc3_out", [NCORES * 2048, 32], F32)
    w["cv_sc"] = nc.dram_tensor("cv_sc", [1, 512], F32)
    w["cc4_in"] = nc.dram_tensor("cc4_in", [64, 1040], F32)
    w["cc4_out"] = nc.dram_tensor("cc4_out", [NCORES * 64, 1040], F32)

    es = ExitStack()
    with es:
        ts = build_base_tiles(nc, T)
        build_moe_tiles(nc, ts, T)
        build_l1a_tiles(nc, ts, T)
        build_l1f_tiles(nc, ts, T)
        with nc.Block() as block:
            E = Emit(nc, block, es)
            xn_list = [ts["xn"][:, c * T:(c + 1) * T] for c in range(FCH)]
            ln_scr = {"sq": ts["ln_sq"], "stat0": ts["ln_s0"],
                      "stat1": ts["ln_s1"], "stat2": ts["ln_s2"],
                      "stat3": ts["ln_s3"]}
            emit_const_init(E, ts, w)

            wd = dict(w); wd["x_out"] = w["x1"]
            emit_load_x(E, T, ts, wd)
            emit_ln(E, T, ts["x_fm"], xn_list, "ln", consts=ts, scratch=ln_scr,
                    banks={"A": ts["psA"]}, pre=("ldx", "cst", "cst_g"))
            emit_l0_attn(E, T, ts, wd, pre=("ln_done",))

            @block.sync
            def _(s):
                E.wait_all(s, ["l0a_done"])
            barrier(E)
            E.recycle()

            wd = dict(w); wd["x_in"] = w["x1"]; wd["x_out"] = w["x2"]
            emit_load_x(E, T, ts, wd)
            emit_ln(E, T, ts["x_fm"], xn_list, "ln", consts=ts, scratch=ln_scr,
                    banks={"A": ts["psA"]}, pre=("ldx",))
            emit_l0_ffn(E, T, ts, wd, pre=("ln_done",))

            @block.sync
            def _(s):
                E.wait_all(s, ["moe_done"])
            barrier(E)
            E.recycle()

            wd = dict(w); wd["x_in"] = w["x2"]; wd["x_out"] = w["x3"]
            emit_load_x(E, T, ts, wd)

            @block.gpsimd
            def _(g):
                g.dma_start(out=ts["nsc_sb"][:, :], in_=w["nscP"].ap()
                            ).then_inc(E.inc("nscld", 16)[0], 16)
            emit_ln(E, T, ts["x_fm"], None, "ln", xn_f32=ts["x_fm"], consts=ts,
                    scratch=ln_scr, banks={"A": ts["psA"]}, pre=("ldx",))

            @block.sync
            def _(s):
                E.wait_all(s, ["ln_done"])
                for c in range(FCH):
                    s.dma_start(out=wd["xn32"].ap()[c * 128:(c + 1) * 128, :],
                                in_=ts["x_fm"][c][:, :]
                                ).then_inc(E.inc("l1a_xnsp", 16)[0], 16)
            emit_ln(E, T, None, xn_list, "rms", rms_of=ts["x_fm"],
                    nsc=ts["nsc_sb"], consts=ts, scratch=ln_scr,
                    banks={"A": ts["psA"]},
                    pre=("ln_done", "nscld", "l1a_xnsp"))
            emit_l1_attn(E, T, ts, wd, pre=("rms_done",))

            @block.sync
            def _(s):
                E.wait_all(s, ["l1a_done"])
            barrier(E)
            E.recycle()

            wd = dict(w); wd["x_in"] = w["x3"]
            emit_load_x(E, T, ts, wd)
            emit_ln(E, T, ts["x_fm"], xn_list, "ln", consts=ts, scratch=ln_scr,
                    banks={"A": ts["psA"]}, pre=("ldx",))
            emit_l1_ffn(E, T, ts, wd, pre=("ln_done",))

            @block.sync
            def _(s):
                E.wait_all(s, ["l1f_done"])
    return nc


def fused_in_maps(x_flat, p, T):
    maps = []
    parts = [l0a_in_maps(x_flat, p, T), moe_in_maps(x_flat, p, T),
             l1a_in_maps(x_flat, p, T), l1f_in_maps(x_flat, p, T)]
    for c in range(NCORES):
        d = {}
        for m in parts:
            d.update(m[c])
        maps.append(d)
    return maps


_PROG_CACHE = {}


def _get_prog(kind, T):
    key = (kind, T)
    if key not in _PROG_CACHE:
        _PROG_CACHE[key] = {"l0a": build_l0a_program, "moe": build_moe_program,
                            "l1a": build_l1a_program, "l1f": build_l1f_program,
                            "fused": build_fused_program}[kind](T)
    return _PROG_CACHE[key]


def kernel(**inputs):
    p = inputs
    x = np.asarray(p["L0_attn_hidden_states"], dtype=np.float32)
    B, S, D = x.shape
    T = B * S // NCORES
    x_flat = np.ascontiguousarray(x.reshape(B * S, D))
    cores = list(range(NCORES))
    nc = _get_prog("fused", T)
    maps = fused_in_maps(x_flat, p, T)
    res = run_bass_kernel_spmd(nc, maps, core_ids=cores)
    xs = [np.asarray(res.results[c]["x_out"]) for c in cores]
    out = np.concatenate([xc.T for xc in xs], 0).reshape(B, S, D)
    return np.ascontiguousarray(out, dtype=np.float32)


# version 07 (final)
# speedup vs baseline: 1.2496x; 1.2496x over previous
import numpy as np
import ml_dtypes
from contextlib import ExitStack
import concourse.bass as bass
import concourse.mybir as mybir
from concourse.bass_utils import run_bass_kernel_spmd

F32 = mybir.dt.float32
BF16 = mybir.dt.bfloat16
AF = mybir.ActivationFunctionType
ALU = mybir.AluOpType
AX = mybir.AxisListType

NCORES = 8
D_MODEL = 1024
FCH = 8
NH, HD = 16, 64
M_MEM = 4
DI = 2048
DICH = 16
N16 = 16
DTR = 64
EPS0 = 1e-6


_EDGE_ENGINES = {"vector", "scalar"}
_NOEDGE = {"wait_ge", "nop", "sem_inc", "wait_op", "drain"}


class EngW:

    def __init__(self, e, ename, emit):
        object.__setattr__(self, "_e", e)
        object.__setattr__(self, "_last", None)
        object.__setattr__(self, "_n", ename)
        object.__setattr__(self, "_E", emit)

    def __getattr__(self, k):
        e = object.__getattribute__(self, "_e")
        f = getattr(e, k)
        if not callable(f):
            return f

        def g(*a, **kw):
            E = object.__getattribute__(self, "_E")
            nm = object.__getattribute__(self, "_n")
            if nm in _EDGE_ENGINES and k not in _NOEDGE:
                def _nm_of(x):
                    t = getattr(x, "tensor", None)
                    if t is not None and hasattr(t, "name"):
                        return t.name
                    if hasattr(x, "name") and hasattr(x, "dtype"):
                        return x.name
                    return None
                names = {y for y in (_nm_of(x) for x in list(a) + list(kw.values()))
                         if y}
                out = a[0] if a else kw.get("out")
                oname = _nm_of(out)
                dirty = E._dirty.setdefault(nm, set())
                rd = E._rd.setdefault(nm, set())
                if (names & dirty) or (oname in rd) or (oname in dirty):
                    last = E._lasti.get(nm)
                    if last is not None:
                        up = E._lastupd.get(nm)
                        if up is not None and up[0] is last:
                            e.wait_ge(up[1], up[2])
                        else:
                            h, v = E.inc("ae_" + nm)
                            last.then_inc(h, 1)
                            E._lastupd[nm] = (last, h, v)
                            e.wait_ge(h, v)
                    dirty.clear()
                    rd.clear()
                if oname:
                    dirty.add(oname)
                rd |= names - {oname}
            r = f(*a, **kw)
            if hasattr(r, "then_inc"):
                object.__setattr__(self, "_last", r)
                if k != "wait_ge":
                    E = object.__getattribute__(self, "_E")
                    E._lasti[object.__getattribute__(self, "_n")] = r
            return r
        return g


class BlockW:

    def __init__(self, block, emit):
        self._block = block
        self._emit = emit

    def __getattr__(self, name):
        dec = getattr(self._block, name)

        def deco(fn):
            return dec(lambda e, _fn=fn, _n=name: _fn(EngW(e, _n, self._emit)))
        return deco


class Emit:

    def __init__(self, nc, block, es):
        self.nc, self.es = nc, es
        self.block = BlockW(block, self)
        self._sems = {}
        self._dirty = {}
        self._rd = {}
        self._lasti = {}
        self._lastupd = {}

    def recycle(self, keep=("j", "ae_")):
        pool = getattr(self, "_pool", None)
        if pool is None:
            pool = self._pool = []
        for name in list(self._sems):
            if not any(name.startswith(k) for k in keep):
                pool.append(self._sems.pop(name))

    def fin(self, eng, name, n=1):
        h, v = self.inc(name, n)
        e = object.__getattribute__(eng, "_e")
        en = object.__getattribute__(eng, "_n")
        last = object.__getattribute__(eng, "_last")
        up = self._lastupd.get(en)
        if last is None:
            e.nop().then_inc(h, n)
        elif up is not None and up[0] is last:
            e.wait_ge(up[1], up[2])
            e.nop().then_inc(h, n)
        else:
            last.then_inc(h, n)
            if self._lasti.get(en) is last:
                self._lastupd[en] = (last, h, v)

    def sem(self, name):
        if name not in self._sems:
            pool = getattr(self, "_pool", None)
            if pool:
                self._sems[name] = pool.pop()
            else:
                h = self.es.enter_context(self.nc.semaphore(name))
                self._sems[name] = [h, 0]
        return self._sems[name]

    def inc(self, name, n=1):
        s = self.sem(name)
        s[1] += n
        return s[0], s[1]

    def at(self, name):
        s = self.sem(name)
        return s[0], s[1]

    def wait_all(self, eng, names):
        for nm in names:
            h, v = self.at(nm)
            if v > 0:
                eng.wait_ge(h, v)


_ALL_ENGINES = ("sync", "vector", "scalar", "tensor", "gpsimd")


def barrier(E):
    b = E.block
    for eng_name in _ALL_ENGINES:
        @getattr(b, eng_name)
        def _(e):
            h, _ = E.inc("j")
            e.nop().then_inc(h, 1)
    tgt = E.at("j")[1]
    for eng_name in _ALL_ENGINES:
        @getattr(b, eng_name)
        def _(e):
            e.wait_ge(E.sem("j")[0], tgt)


class Tiles:
    def __init__(self, nc, es):
        self.nc, self.es = nc, es
        self.t = {}

    def sb(self, name, shape, dt=F32):
        if name not in self.t:
            self.t[name] = self.es.enter_context(self.nc.sbuf_tensor(name, shape, dt))
        return self.t[name]

    def ps(self, name, shape, dt=F32):
        if name not in self.t:
            self.t[name] = self.es.enter_context(self.nc.psum_tensor(name, shape, dt))
        return self.t[name]


def ceil_div(a, b):
    return (a + b - 1) // b


def emit_ln(E, T, x_fm, xn_bf, ts_name, xn_f32=None, rms_of=None, nsc=None,
            consts=None, scratch=None, banks=None, pre=()):
    b = E.block
    TT = min(512, T)
    ntt = ceil_div(T, TT)
    rms = rms_of is not None
    xin = rms_of if rms else x_fm
    sq = scratch["sq"]
    st_mean, st_var = scratch["stat0"], scratch["stat1"]
    st_rstd, st_negmr = scratch["stat2"], scratch["stat3"]
    A = banks["A"]
    ones_f1 = consts["ones_f1"]
    ones_1f = consts["ones_1f"]
    tag = ts_name

    if not rms:
        @b.tensor
        def _(t):
            E.wait_all(t, list(pre))
            for c in range(FCH):
                for i in range(ntt):
                    sl = slice(i * TT, (i + 1) * TT)
                    t.matmul(A[0:1, sl], ones_f1[:, 0:1], xin[c][:, sl],
                             start=(c == 0), stop=(c == FCH - 1))
            E.fin(t, tag + "_s1mm")

        @b.vector
        def _(v):
            E.wait_all(v, [tag + "_s1mm"])
            v.tensor_scalar(st_mean[0:1, 0:T], A[0:1, 0:T], 1.0 / D_MODEL,
                            None, ALU.mult)
            E.fin(v, tag + "_s1cp")

    bsq = E.at(tag + "_pe_sq")[1]

    @b.scalar
    def _(sc):
        E.wait_all(sc, list(pre))
        for c in range(FCH):
            if c > 0:
                sc.wait_ge(E.sem(tag + "_pe_sq")[0], bsq + c)
            sc.activation(sq[:, 0:T], xin[c][:, 0:T], AF.Square)
            E.fin(sc, tag + "_sqr")

    bsqr = E.at(tag + "_sqr")[1] - FCH

    @b.tensor
    def _(t):
        if not rms:
            E.wait_all(t, [tag + "_s1cp"])
        for c in range(FCH):
            t.wait_ge(E.sem(tag + "_sqr")[0], bsqr + c + 1)
            for i in range(ntt):
                sl = slice(i * TT, (i + 1) * TT)
                t.matmul(A[0:1, sl], ones_f1[:, 0:1], sq[:, sl],
                         start=(c == 0), stop=(c == FCH - 1))
            E.fin(t, tag + "_pe_sq")

    @b.vector
    def _(v):
        E.wait_all(v, [tag + "_pe_sq"])
        v.tensor_scalar(st_var[0:1, 0:T], A[0:1, 0:T], 1.0 / D_MODEL, None, ALU.mult)
        if not rms:
            v.tensor_tensor(st_negmr[0:1, 0:T], st_mean[0:1, 0:T],
                            st_mean[0:1, 0:T], ALU.mult)
            E.fin(v, tag + "_vtmp")
            v.wait_ge(E.sem(tag + "_vtmp")[0], E.at(tag + "_vtmp")[1])
            v.tensor_tensor(st_var[0:1, 0:T], st_var[0:1, 0:T],
                            st_negmr[0:1, 0:T], ALU.subtract)
        E.fin(v, tag + "_dve_var")

    @b.scalar
    def _(sc):
        E.wait_all(sc, [tag + "_dve_var", "kon"])
        kon = consts["kon"]
        sc.activation(st_rstd[0:1, 0:T], st_var[0:1, 0:T], AF.Ln, bias=kon[0:1, 0:1])
        sc.activation(st_rstd[0:1, 0:T], st_rstd[0:1, 0:T], AF.Exp,
                      scale=kon[0:1, 1:2])
        E.fin(sc, tag + "_act_rstd")

    @b.vector
    def _(v):
        E.wait_all(v, [tag + "_act_rstd"])
        if not rms:
            v.scalar_tensor_tensor(st_negmr[0:1, 0:T], st_mean[0:1, 0:T], -1.0,
                                   st_rstd[0:1, 0:T], ALU.mult, ALU.mult)
        E.fin(v, tag + "_negmr")

    for c in range(FCH):
        for i in range(ntt):
            sl = slice(i * TT, (i + 1) * TT)

            @b.tensor
            def _(t, c=c, i=i, sl=sl):
                E.wait_all(t, [tag + "_negmr"])
                if not (c == 0 and i == 0):
                    t.wait_ge(E.sem(tag + "_app")[0], E.at(tag + "_app")[1])
                t.matmul(A[:, 0:TT], ones_1f[0:1, :], st_rstd[0:1, sl],
                         start=True, stop=True)
                if not rms:
                    t.matmul(A[:, TT:2 * TT], ones_1f[0:1, :], st_negmr[0:1, sl],
                             start=True, stop=True)
                E.fin(t, tag + "_bc")

            @b.vector
            def _(v, c=c, i=i, sl=sl):
                v.wait_ge(E.sem(tag + "_bc")[0], E.at(tag + "_bc")[1])
                if rms:
                    v.scalar_tensor_tensor(xn_bf[c][:, sl], xin[c][:, sl],
                                           nsc[:, c:c + 1], A[:, 0:TT],
                                           ALU.mult, ALU.mult)
                else:
                    v.tensor_tensor(sq[:, 0:TT], x_fm[c][:, sl], A[:, 0:TT], ALU.mult)
                    if xn_bf is not None:
                        v.tensor_tensor(xn_bf[c][:, sl], sq[:, 0:TT], A[:, TT:2 * TT],
                                        ALU.add)
                    if xn_f32 is not None:
                        v.tensor_tensor(xn_f32[c][:, sl], sq[:, 0:TT],
                                        A[:, TT:2 * TT], ALU.add)
                E.fin(v, tag + "_app")

    @b.vector
    def _(v):
        v.wait_ge(E.sem(tag + "_app")[0], E.at(tag + "_app")[1])
        E.fin(v, tag + "_done")


def emit_l0_attn(E, Ti, ts, w, pre=(), tag="l0a"):
    b = E.block
    TT = min(512, Ti)
    ntt = ceil_div(Ti, TT)
    ntile = Ti // 128
    CPB = NCORES // 2
    Sq = CPB * Ti
    xn, w0 = ts["xn"], ts["w0"]
    sq, sk_tm, v_tm, loc = ts["sq"], ts["sk"], ts["v"], ts["loc"]
    e0, e1, e2, e3 = ts["e0"], ts["e1"], ts["e2"], ts["e3"]
    relT, rr = ts["relT"], ts["rr"]
    st_sb, pfx, bsm, agr = ts["st_sb"], ts["pfx"], ts["bsm"], ts["agr"]
    mn_sb, selrr, memb = ts["mn_sb"], ts["selrr"], ts["memb"]
    gsg, g1m = ts["gsg"], ts["g1m"]
    skT, kvr, kvrb, kcr, kcrb = ts["skT"], ts["kvr"], ts["kvrb"], ts["kcr"], ts["kcrb"]
    sqh = ts["sqh"]
    den_r, dbc, sqs, smB = ts["den_r"], ts["dbc"], ts["sqs"], ts["smB"]
    xrl = (ts["xrl0"], ts["xrl1"])
    wvec, rmax, wcol, rmcol = ts["wvec"], ts["rmax"], ts["wcol"], ts["rmcol"]
    pmb, bmb = ts["pmb"], ts["bmb"]
    ones_f1, ones_1f, ones_b1 = ts["ones_f1"], ts["ones_1f"], ts["ones_b1"]
    mask_ut, ident_b = ts["mask_ut"], ts["ident_b"]
    PB, PBT = ts["PB"], ts["PBT"]

    def base(name):
        return E.at(tag + name)[1]

    def wld(g, name):
        src = w[name].ap().rearrange("(c p) o -> p c o", p=128)
        ins = g.dma_start(out=w0[:, :].rearrange("p (c o) -> p c o", c=FCH), in_=src)
        ins.then_inc(E.inc(tag + "_wld", 16)[0], 16)

    @b.gpsimd
    def _(g):
        g.dma_start(out=mn_sb[:, 0:512].rearrange("p (c o) -> p c o", c=FCH),
                    in_=w["MN"].ap().rearrange("(c p) o -> p c o", p=128)
                    ).then_inc(E.inc(tag + "_cld", 16)[0], 16)
        g.dma_start(out=ts["gate"][0:16, 0:1], in_=w["gate"].ap()
                    ).then_inc(E.inc(tag + "_cld", 16)[0], 16)
        g.dma_start(out=ts["pmrow"][0:1, 0:8], in_=w["pmask"].ap()
                    ).then_inc(E.inc(tag + "_cld", 16)[0], 16)
        g.dma_start(out=ts["bmrow"][0:1, 0:8], in_=w["bmask"].ap()
                    ).then_inc(E.inc(tag + "_cld", 16)[0], 16)
        wld(g, "wqT")

    @b.scalar
    def _(sc):
        E.wait_all(sc, [tag + "_cld"])
        sc.activation(ts["gate"][0:16, 0:1], ts["gate"][0:16, 0:1], AF.Sigmoid)
        E.fin(sc, tag + "_gsig")

    @b.sync
    def _(s):
        E.wait_all(s, [tag + "_gsig"])
        s.dma_start(out=w["gate_sc"].ap(), in_=ts["gate"][0:16, 0:1]
                    ).then_inc(E.inc(tag + "_gd1", 16)[0], 16)
        E.wait_all(s, [tag + "_gd1"])
        s.dma_start(out=gsg[0:64, 0:8],
                    in_=bass.AP(w["gate_sc"], 0, [[0, 64], [2, 8]]),
                    allow_slow_non_contiguous=True
                    ).then_inc(E.inc(tag + "_gd2", 16)[0], 16)
        s.dma_start(out=gsg[64:128, 0:8],
                    in_=bass.AP(w["gate_sc"], 1, [[0, 64], [2, 8]]),
                    allow_slow_non_contiguous=True
                    ).then_inc(E.inc(tag + "_gd2", 16)[0], 16)

    @b.vector
    def _(v):
        E.wait_all(v, [tag + "_gd2"])
        v.tensor_scalar(g1m[:, 0:8], gsg[:, 0:8], -1.0, 1.0, ALU.mult, ALU.add)
        E.fin(v, tag + "_g1m")

    def elu1_jobs(njobs, bank_of, out_of, tg, W=None):
        b_mm = base(tg + "_mm") - njobs
        b_act = base(tg + "_act")
        b_min = base(tg + "_min")
        W = TT if W is None else W

        @b.vector
        def _(v):
            for j in range(njobs + 1):
                if j < njobs:
                    v.wait_ge(E.sem(tag + tg + "_mm")[0], b_mm + j + 1)
                    v.tensor_scalar((e0 if j % 2 == 0 else e1)[:, 0:W],
                                    bank_of(j)[:, 0:W], 0.0, None, ALU.min)
                    E.fin(v, tag + tg + "_min")
                if j >= 1:
                    v.wait_ge(E.sem(tag + tg + "_act")[0], b_act + j)
                    v.scalar_tensor_tensor(out_of(j - 1), bank_of(j - 1)[:, 0:W],
                                           0.0,
                                           (e2 if (j - 1) % 2 == 0 else e3)[:, 0:W],
                                           ALU.max, ALU.add)
                    E.fin(v, tag + tg + "_done")

        @b.scalar
        def _(sc):
            for j in range(njobs):
                sc.wait_ge(E.sem(tag + tg + "_min")[0], b_min + j + 1)
                if j >= 2:
                    sc.wait_ge(E.sem(tag + tg + "_done")[0],
                               base(tg + "_done") - njobs + j - 1)
                sc.activation((e2 if j % 2 == 0 else e3)[:, 0:W],
                              (e0 if j % 2 == 0 else e1)[:, 0:W], AF.Exp)
                E.fin(sc, tag + tg + "_act")

    def base2(nm):
        return E.at(tag + nm)[1]

    qjobs = [(oc, tt) for oc in range(FCH) for tt in range(ntt)]
    bq_done = base2("_q_done")

    @b.tensor
    def _(t):
        E.wait_all(t, list(pre))
        t.wait_ge(E.sem(tag + "_wld")[0], 16)
        for j, (oc, tt) in enumerate(qjobs):
            if j >= 2:
                t.wait_ge(E.sem(tag + "_q_done")[0], bq_done + j - 1)
            for fc in range(FCH):
                t.matmul(PB[j % 2][:, 0:TT],
                         w0[:, fc * 1024 + oc * 128:fc * 1024 + oc * 128 + 128],
                         xn[:, fc * Ti + tt * TT:fc * Ti + (tt + 1) * TT],
                         start=(fc == 0), stop=(fc == FCH - 1))
            E.fin(t, tag + "_q_mm")

    elu1_jobs(len(qjobs), lambda j: PB[j % 2],
              lambda j: sq[:, qjobs[j][0] * Ti + qjobs[j][1] * TT:
                           qjobs[j][0] * Ti + (qjobs[j][1] + 1) * TT], "_q")

    @b.gpsimd
    def _(g):
        g.wait_ge(E.sem(tag + "_q_mm")[0], E.at(tag + "_q_mm")[1])
        wld(g, "wkT")

    kjobs = [(tI, oh) for tI in range(ntile) for oh in range(2)]

    def tm_proj_section(wneed, bank_base, consname, mmname):
        b_cons = base2(consname)

        @b.tensor
        def _(t):
            t.wait_ge(E.sem(tag + "_wld")[0], wneed)
            for j, (tI, oh) in enumerate(kjobs):
                if j >= 2:
                    t.wait_ge(E.sem(tag + consname)[0], b_cons + j - 1)
                for fc in range(FCH):
                    t.matmul(PB[bank_base + j % 2][:, 0:512],
                             xn[:, fc * Ti + tI * 128:fc * Ti + (tI + 1) * 128],
                             w0[:, fc * 1024 + oh * 512:fc * 1024 + (oh + 1) * 512],
                             start=(fc == 0), stop=(fc == FCH - 1))
                E.fin(t, tag + mmname)

    tm_proj_section(32, 2, "_k_done", "_k_mm")
    elu1_jobs(len(kjobs), lambda j: PB[2 + j % 2],
              lambda j: sk_tm[:, kjobs[j][0] * 1024 + kjobs[j][1] * 512:
                              kjobs[j][0] * 1024 + (kjobs[j][1] + 1) * 512], "_k",
              W=512)

    @b.gpsimd
    def _(g):
        g.wait_ge(E.sem(tag + "_k_mm")[0], E.at(tag + "_k_mm")[1])
        wld(g, "wvT")

    tm_proj_section(48, 4, "_v_cp", "_v_mm")
    bv_mm = base2("_v_mm") - len(kjobs)

    @b.vector
    def _(v):
        for j, (tI, oh) in enumerate(kjobs):
            v.wait_ge(E.sem(tag + "_v_mm")[0], bv_mm + j + 1)
            v.tensor_copy(v_tm[:, tI * 1024 + oh * 512:tI * 1024 + (oh + 1) * 512],
                          PB[4 + j % 2][:, 0:512])
            E.fin(v, tag + "_v_cp")

    brel_cp = base2("_rel_cp")

    @b.tensor
    def _(t):
        t.wait_ge(E.sem(tag + "_q_done")[0], E.at(tag + "_q_done")[1])
        E.wait_all(t, [tag + "_cld"])
        for tt in range(ntt):
            if tt >= 1:
                t.wait_ge(E.sem(tag + "_rel_cp")[0], brel_cp + tt)
            for fc in range(FCH):
                t.matmul(PB[6][0:64, 0:TT], mn_sb[:, fc * 64:(fc + 1) * 64],
                         sq[:, fc * Ti + tt * TT:fc * Ti + (tt + 1) * TT],
                         start=(fc == 0), stop=(fc == FCH - 1))
            E.fin(t, tag + "_rel_mm")

    brel_mm = base2("_rel_mm") - ntt

    @b.vector
    def _(v):
        for tt in range(ntt):
            v.wait_ge(E.sem(tag + "_rel_mm")[0], brel_mm + tt + 1)
            v.tensor_copy(relT[0:64, tt * TT:(tt + 1) * TT], PB[6][0:64, 0:TT])
            E.fin(v, tag + "_rel_cp")

    bkv_cp = base2("_kv_cp")

    @b.tensor
    def _(t):
        for nm in (tag + "_v_cp", tag + "_k_done", tag + "_rel_cp"):
            t.wait_ge(E.sem(nm)[0], E.at(nm)[1])
        for hh in range(NH):
            if hh >= 2:
                t.wait_ge(E.sem(tag + "_kv_cp")[0], bkv_cp + hh - 1)
            for tI in range(ntile):
                t.matmul(PB[hh % 2][0:64, 0:64],
                         sk_tm[:, tI * 1024 + hh * 64:tI * 1024 + (hh + 1) * 64],
                         v_tm[:, tI * 1024 + hh * 64:tI * 1024 + (hh + 1) * 64],
                         start=(tI == 0), stop=(tI == ntile - 1))
            E.fin(t, tag + "_kv_mm")
        for oc in range(FCH):
            for tI in range(ntile):
                t.matmul(PB[2][:, oc:oc + 1],
                         sk_tm[:, tI * 1024 + oc * 128:tI * 1024 + (oc + 1) * 128],
                         ones_b1[:, 0:1], start=(tI == 0), stop=(tI == ntile - 1))
        E.fin(t, tag + "_kc_mm")

    bkv_mm = base2("_kv_mm") - NH

    @b.vector
    def _(v):
        for hh in range(NH):
            v.wait_ge(E.sem(tag + "_kv_mm")[0], bkv_mm + hh + 1)
            oc, half = hh // 2, hh % 2
            v.tensor_copy(st_sb[64 * half:64 * half + 64, oc * 66:oc * 66 + 64],
                          PB[hh % 2][0:64, 0:64])
            E.fin(v, tag + "_kv_cp")
        v.wait_ge(E.sem(tag + "_kc_mm")[0], E.at(tag + "_kc_mm")[1])
        for oc in range(FCH):
            v.tensor_copy(st_sb[:, oc * 66 + 64:oc * 66 + 65], PB[2][:, oc:oc + 1])
        for oc in range(FCH):
            v.reduce_sum(st_sb[:, oc * 66 + 65:oc * 66 + 66],
                         sq[:, oc * Ti:(oc + 1) * Ti], axis=AX.X)
        E.fin(v, tag + "_st_done")

    @b.sync
    def _(s):
        E.wait_all(s, [tag + "_st_done"])
        s.dma_start(out=w["cc_in"].ap().rearrange("(c p) w -> p c w", p=128),
                    in_=st_sb[:, :].rearrange("p (c w) -> p c w", w=66)
                    ).then_inc(E.inc(tag + "_ccin", 16)[0], 16)

    @b.gpsimd
    def _(g):
        E.wait_all(g, [tag + "_ccin"])
        g.collective_compute(
            "AllGather", ALU.bypass, replica_groups=[list(range(NCORES))],
            ins=[w["cc_in"].ap()], outs=[w["cc_out"].ap()],
        ).then_inc(E.inc(tag + "_cc")[0], 1)

    @b.tensor
    def _(t):
        E.wait_all(t, [tag + "_cld", tag + "_kc_mm"])
        t.matmul(PB[3][:, 0:8], ones_1f[0:1, :], ts["pmrow"][0:1, 0:8],
                 start=True, stop=True)
        t.matmul(PB[3][:, 8:16], ones_1f[0:1, :], ts["bmrow"][0:1, 0:8],
                 start=True, stop=True)
        E.fin(t, tag + "_mb_mm")

    @b.vector
    def _(v):
        E.wait_all(v, [tag + "_mb_mm"])
        v.tensor_copy(pmb[:, 0:8], PB[3][:, 0:8])
        v.tensor_copy(bmb[:, 0:8], PB[3][:, 8:16])
        E.fin(v, tag + "_mb_cp")

    bagacc = base2("_agacc")
    for r in range(NCORES):
        @b.sync
        def _(s, r=r):
            E.wait_all(s, [tag + "_cc"])
            if r >= 1:
                s.wait_ge(E.sem(tag + "_agacc")[0], bagacc + r)
            s.dma_start(out=agr[:, 0:528].rearrange("p (c w) -> p c w", w=66),
                        in_=w["cc_out"].ap()[r * 8 * 128:(r + 1) * 8 * 128, :]
                        .rearrange("(c p) w -> p c w", p=128)
                        ).then_inc(E.inc(tag + "_agld", 16)[0], 16)

        @b.vector
        def _(v, r=r):
            v.wait_ge(E.sem(tag + "_agld")[0], E.at(tag + "_agld")[1])
            E.wait_all(v, [tag + "_mb_cp"])
            if r == 0:
                v.tensor_scalar(pfx[:, 0:528], agr[:, 0:528], pmb[:, 0:1], None, ALU.mult)
                v.tensor_scalar(bsm[:, 0:528], agr[:, 0:528], bmb[:, 0:1], None, ALU.mult)
            else:
                v.scalar_tensor_tensor(pfx[:, 0:528], agr[:, 0:528], pmb[:, r:r + 1],
                                       pfx[:, 0:528], ALU.mult, ALU.add)
                v.scalar_tensor_tensor(bsm[:, 0:528], agr[:, 0:528], bmb[:, r:r + 1],
                                       bsm[:, 0:528], ALU.mult, ALU.add)
            E.fin(v, tag + "_agacc")

    @b.vector
    def _(v):
        E.wait_all(v, [tag + "_agacc"])
        for c in range(FCH):
            v.tensor_copy(sqs[:, c:c + 1], bsm[:, c * 66 + 65:c * 66 + 66])
        E.fin(v, tag + "_sqb")

    @b.tensor
    def _(t):
        E.wait_all(t, [tag + "_sqb"])
        for c in range(FCH):
            t.matmul(PB[4][0:64, 0:1], mn_sb[:, c * 64:(c + 1) * 64],
                     sqs[:, c:c + 1], start=(c == 0), stop=(c == FCH - 1))
        E.fin(t, tag + "_rm_mm")

    @b.vector
    def _(v):
        E.wait_all(v, [tag + "_rm_mm"])
        v.tensor_scalar(rmcol[0:64, 0:1], PB[4][0:64, 0:1], 1.0 / Sq, None, ALU.mult)
        E.fin(v, tag + "_rmc")

    @b.sync
    def _(s):
        E.wait_all(s, [tag + "_rmc"])
        s.dma_start(out=w["rm_sc"].ap(), in_=rmcol[0:64, 0:1]
                    ).then_inc(E.inc(tag + "_rmd1", 16)[0], 16)
        E.wait_all(s, [tag + "_rmd1"])
        s.dma_start(out=wvec[0:1, 0:64], in_=w["rm_sc"].ap().rearrange("a b -> b a")
                    ).then_inc(E.inc(tag + "_rmtr", 16)[0], 16)

    @b.vector
    def _(v):
        E.wait_all(v, [tag + "_rmtr"])
        wv3 = wvec[0:1, 0:64].rearrange("a (h m) -> a h m", m=4)
        v.reduce_max(bass.AP(rmax, 0, [[32, 1], [1, 16], [1, 1]]), wv3, axis=AX.X)
        E.fin(v, tag + "_sm0")
        v.wait_ge(E.sem(tag + "_sm0")[0], E.at(tag + "_sm0")[1])
        mxb = bass.AP(rmax, 0, [[32, 1], [1, 16], [0, 4]])
        v.tensor_tensor(wv3, wv3, mxb, ALU.subtract)
        E.fin(v, tag + "_sm1")

    @b.scalar
    def _(sc):
        E.wait_all(sc, [tag + "_sm1"])
        sc.activation(wvec[0:1, 0:64], wvec[0:1, 0:64], AF.Exp)
        E.fin(sc, tag + "_sm2")

    @b.vector
    def _(v):
        E.wait_all(v, [tag + "_sm2"])
        wv3 = wvec[0:1, 0:64].rearrange("a (h m) -> a h m", m=4)
        v.reduce_sum(bass.AP(rmax, 0, [[32, 1], [1, 16], [1, 1]]), wv3, axis=AX.X)
        E.fin(v, tag + "_sm3")
        v.wait_ge(E.sem(tag + "_sm3")[0], E.at(tag + "_sm3")[1])
        v.reciprocal(bass.AP(rmax, 16, [[32, 1], [1, 16]]),
                     bass.AP(rmax, 0, [[32, 1], [1, 16]]))
        E.fin(v, tag + "_sm4")
        v.wait_ge(E.sem(tag + "_sm4")[0], E.at(tag + "_sm4")[1])
        rcb = bass.AP(rmax, 16, [[32, 1], [1, 16], [0, 4]])
        v.tensor_tensor(wv3, wv3, rcb, ALU.mult)
        E.fin(v, tag + "_smw")

    @b.sync
    def _(s):
        E.wait_all(s, [tag + "_smw"])
        s.dma_start(out=w["rm_sc"].ap().rearrange("a b -> b a"),
                    in_=wvec[0:1, 0:64]).then_inc(E.inc(tag + "_wcd1", 16)[0], 16)
        E.wait_all(s, [tag + "_wcd1"])
        s.dma_start(out=wcol[0:64, 0:1], in_=w["rm_sc"].ap()
                    ).then_inc(E.inc(tag + "_wcol", 16)[0], 16)

    @b.vector
    def _(v):
        E.wait_all(v, [tag + "_wcol"])
        for tt in range(ntt):
            v.tensor_scalar(e0[0:64, 0:TT], relT[0:64, tt * TT:(tt + 1) * TT],
                            EPS0, None, ALU.max)
            v.reciprocal(e1[0:64, 0:TT], e0[0:64, 0:TT])
            v.tensor_scalar(rr[0:64, tt * TT:(tt + 1) * TT], e1[0:64, 0:TT],
                            wcol[0:64, 0:1], None, ALU.mult)
        E.fin(v, tag + "_rr")

    @b.vector
    def _(v):
        E.wait_all(v, [tag + "_agacc"])
        for hh in range(NH):
            oc, half = hh // 2, hh % 2
            src = pfx[64 * half:64 * half + 64, oc * 66:oc * 66 + 64]
            v.tensor_copy(kvr[0:64, hh * 64:(hh + 1) * 64], src)
            v.tensor_copy(kvrb[0:64, hh * 64:(hh + 1) * 64], src)
            v.tensor_copy(kcr[0:64, hh:hh + 1],
                          pfx[64 * half:64 * half + 64, oc * 66 + 64:oc * 66 + 65])
        v.tensor_copy(kcrb[0:64, 0:16], kcr[0:64, 0:16])
        E.fin(v, tag + "_kvinit")

    for ci in range(ntile):
        for hh in range(NH):
            j = ci * NH + hh
            oc, half = hh // 2, hh % 2
            sq_sl = sq[64 * half:64 * half + 64,
                       oc * Ti + ci * 128:oc * Ti + (ci + 1) * 128]
            sk_sl = sk_tm[:, ci * 1024 + hh * 64:ci * 1024 + (hh + 1) * 64]
            v_sl = v_tm[:, ci * 1024 + hh * 64:ci * 1024 + (hh + 1) * 64]

            @b.tensor
            def _(t, j=j, sk_sl=sk_sl):
                if j == 0:
                    E.wait_all(t, [tag + "_kvinit", tag + "_rr", tag + "_g1m"])
                else:
                    t.wait_ge(E.sem(tag + "_lp_d")[0], E.at(tag + "_lp_d")[1])
                t.transpose(PBT[0:64, 0:128], sk_sl, ident_b[:, 0:128])
                E.fin(t, tag + "_lp_tr")

            @b.vector
            def _(v, j=j, sq_sl=sq_sl):
                v.wait_ge(E.sem(tag + "_lp_tr")[0], E.at(tag + "_lp_tr")[1])
                v.tensor_copy(skT[0:64, 0:128], PBT[0:64, 0:128])
                v.tensor_copy(sqh[0:64, 0:128], sq_sl)
                E.fin(v, tag + "_lp_skT")

            @b.tensor
            def _(t, j=j):
                t.wait_ge(E.sem(tag + "_lp_skT")[0], E.at(tag + "_lp_skT")[1])
                t.matmul(PB[0][:, 0:128], skT[0:64, 0:128], sqh[0:64, 0:128],
                         start=True, stop=True)
                E.fin(t, tag + "_lp_sc")

            @b.vector
            def _(v, j=j):
                v.wait_ge(E.sem(tag + "_lp_sc")[0], E.at(tag + "_lp_sc")[1])
                v.tensor_tensor(smB[:, 0:128], PB[0][:, 0:128], mask_ut[:, 0:128],
                                ALU.mult)
                E.fin(v, tag + "_lp_sm")

            @b.tensor
            def _(t, j=j, hh=hh, v_sl=v_sl, sk_sl=sk_sl):
                t.wait_ge(E.sem(tag + "_lp_sm")[0], E.at(tag + "_lp_sm")[1])
                t.matmul(PB[1][0:1, 0:128], ones_b1[:, 0:1], smB[:, 0:128],
                         start=True, stop=False)
                t.matmul(PB[1][0:1, 0:128], kcrb[0:64, hh:hh + 1], sqh[0:64, 0:128],
                         start=False, stop=True)
                t.matmul(PB[2][0:64, 0:128], v_sl, smB[:, 0:128],
                         start=True, stop=False)
                t.matmul(PB[2][0:64, 0:128], kvrb[0:64, hh * 64:(hh + 1) * 64],
                         sqh[0:64, 0:128], start=False, stop=True)
                t.matmul(PB[3][0:64, 0:64], sk_sl, v_sl, start=True, stop=True)
                t.matmul(PB[3][0:64, 64:65], sk_sl, ones_b1[:, 0:1],
                         start=True, stop=True)
                E.fin(t, tag + "_lp_mm")

            @b.vector
            def _(v, j=j, hh=hh):
                v.wait_ge(E.sem(tag + "_lp_mm")[0], E.at(tag + "_lp_mm")[1])
                v.tensor_scalar(den_r[0:1, 0:128], PB[1][0:1, 0:128], EPS0,
                                None, ALU.max)
                E.fin(v, tag + "_lp_dr")
                v.wait_ge(E.sem(tag + "_lp_dr")[0], E.at(tag + "_lp_dr")[1])
                v.reciprocal(den_r[0:1, 0:128], den_r[0:1, 0:128])
                v.tensor_tensor(kvr[0:64, hh * 64:(hh + 1) * 64],
                                kvr[0:64, hh * 64:(hh + 1) * 64],
                                PB[3][0:64, 0:64], ALU.add)
                v.tensor_tensor(kcr[0:64, hh:hh + 1], kcr[0:64, hh:hh + 1],
                                PB[3][0:64, 64:65], ALU.add)
                v.tensor_copy(kvrb[0:64, hh * 64:(hh + 1) * 64],
                              kvr[0:64, hh * 64:(hh + 1) * 64])
                v.tensor_copy(kcrb[0:64, hh:hh + 1], kcr[0:64, hh:hh + 1])
                E.fin(v, tag + "_lp_upd")

            @b.tensor
            def _(t, j=j):
                t.wait_ge(E.sem(tag + "_lp_upd")[0], E.at(tag + "_lp_upd")[1])
                t.matmul(PB[4][0:64, 0:128], ones_1f[0:1, 0:64], den_r[0:1, 0:128],
                         start=True, stop=True)
                E.fin(t, tag + "_lp_bc")

            @b.vector
            def _(v, j=j, ci=ci, oc=oc, half=half):
                v.wait_ge(E.sem(tag + "_lp_bc")[0], E.at(tag + "_lp_bc")[1])
                v.tensor_copy(dbc[0:64, 0:128], PB[4][0:64, 0:128])
                E.fin(v, tag + "_lp_ev")
                v.wait_ge(E.sem(tag + "_lp_ev")[0], E.at(tag + "_lp_ev")[1])
                v.tensor_tensor(loc[64 * half:64 * half + 64,
                                    oc * Ti + ci * 128:oc * Ti + (ci + 1) * 128],
                                PB[2][0:64, 0:128], dbc[0:64, 0:128], ALU.mult)
                E.fin(v, tag + "_lp_d")

    @b.gpsimd
    def _(g):
        g.wait_ge(E.sem(tag + "_v_mm")[0], E.at(tag + "_v_mm")[1])
        wld(g, "woT")
        g.wait_ge(E.sem(tag + "_lp_d")[0], E.at(tag + "_lp_d")[1])
        g.dma_start(out=selrr[:, :].rearrange("p (m c o) -> p m c o", m=M_MEM, c=FCH),
                    in_=w["selRR"].ap().rearrange("m c p o -> p m c o")
                    ).then_inc(E.inc(tag + "_cld2", 16)[0], 16)
        g.dma_start(out=memb[:, :].rearrange("p (m c o) -> p m c o", m=M_MEM, c=FCH),
                    in_=w["MEMB"].ap().rearrange("m c p o -> p m c o")
                    ).then_inc(E.inc(tag + "_cld2", 16)[0], 16)

    for tt in range(ntt):
        for oc in range(FCH):
            idx = tt * FCH + oc
            b_bc = base2("_mp_bc")
            b_sq2 = base2("_mp_sq2")
            b_mm = base2("_mp_mm")

            @b.tensor
            def _(t, tt=tt, oc=oc, b_sq2=b_sq2):
                if tt == 0 and oc == 0:
                    E.wait_all(t, [tag + "_lp_d", tag + "_rr", tag + "_cld2"])
                    t.wait_ge(E.sem(tag + "_wld")[0], E.at(tag + "_wld")[1])
                else:
                    t.wait_ge(E.sem(tag + "_mp_done")[0], E.at(tag + "_mp_done")[1])
                if tt > 0 and oc == 0:
                    t.wait_ge(E.sem(tag + "_wo_add")[0], E.at(tag + "_wo_add")[1])
                for m in range(M_MEM):
                    t.matmul(PB[5][:, 0:TT], selrr[:, (m * FCH + oc) * 128:
                                                   (m * FCH + oc + 1) * 128],
                             rr[0:64, tt * TT:(tt + 1) * TT], start=True, stop=True)
                    E.fin(t, tag + "_mp_bc")
                    t.wait_ge(E.sem(tag + "_mp_sq2")[0], b_sq2 + m + 1)
                    t.matmul(PB[6][:, 0:TT], memb[:, (m * FCH + oc) * 128:
                                                  (m * FCH + oc + 1) * 128],
                             sqs[:, 0:TT], start=(m == 0), stop=(m == M_MEM - 1))
                    E.fin(t, tag + "_mp_mm")

            @b.vector
            def _(v, tt=tt, oc=oc, b_bc=b_bc, b_mm=b_mm):
                if tt == 0 and oc == 0:
                    E.wait_all(v, [tag + "_g1m"])
                for m in range(M_MEM):
                    v.wait_ge(E.sem(tag + "_mp_bc")[0], b_bc + m + 1)
                    if m >= 1:
                        v.wait_ge(E.sem(tag + "_mp_mm")[0], b_mm + m)
                    v.tensor_tensor(sqs[:, 0:TT],
                                    sq[:, oc * Ti + tt * TT:oc * Ti + (tt + 1) * TT],
                                    PB[5][:, 0:TT], ALU.mult)
                    E.fin(v, tag + "_mp_sq2")
                v.wait_ge(E.sem(tag + "_mp_mm")[0], b_mm + M_MEM)
                v.tensor_scalar(e0[:, 0:TT], PB[6][:, 0:TT], gsg[:, oc:oc + 1],
                                None, ALU.mult)
                E.fin(v, tag + "_mp_gm")
                v.wait_ge(E.sem(tag + "_mp_gm")[0], E.at(tag + "_mp_gm")[1])
                v.scalar_tensor_tensor(
                    loc[:, oc * Ti + tt * TT:oc * Ti + (tt + 1) * TT],
                    loc[:, oc * Ti + tt * TT:oc * Ti + (tt + 1) * TT],
                    g1m[:, oc:oc + 1], e0[:, 0:TT], ALU.mult, ALU.add)
                E.fin(v, tag + "_mp_done")

        for oc2 in range(FCH):
            k = tt * FCH + oc2

            @b.tensor
            def _(t, tt=tt, oc2=oc2, k=k):
                t.wait_ge(E.sem(tag + "_mp_done")[0], E.at(tag + "_mp_done")[1])
                if k >= 1:
                    t.wait_ge(E.sem(tag + "_wo_add")[0], E.at(tag + "_wo_add")[1])
                for oc in range(FCH):
                    t.matmul(PB[6][:, 0:TT],
                             w0[:, oc * 1024 + oc2 * 128:oc * 1024 + oc2 * 128 + 128],
                             loc[:, oc * Ti + tt * TT:oc * Ti + (tt + 1) * TT],
                             start=(oc == 0), stop=(oc == FCH - 1))
                E.fin(t, tag + "_wo_mm")

            @b.sync
            def _(s, tt=tt, oc2=oc2, k=k):
                if k >= 2:
                    s.wait_ge(E.sem(tag + "_wo_add")[0], E.at(tag + "_wo_add")[1] - 1)
                s.dma_start(out=xrl[k % 2][:, 0:TT],
                            in_=w["x_in"].ap()[oc2 * 128:(oc2 + 1) * 128,
                                               tt * TT:(tt + 1) * TT]
                            ).then_inc(E.inc(tag + "_xrl", 16)[0], 16)

            @b.vector
            def _(v, tt=tt, oc2=oc2, k=k):
                v.wait_ge(E.sem(tag + "_wo_mm")[0], E.at(tag + "_wo_mm")[1])
                v.wait_ge(E.sem(tag + "_xrl")[0], E.at(tag + "_xrl")[1])
                stp = tag + ("_wo_st0" if k % 2 == 0 else "_wo_st1")
                if k >= 2:
                    v.wait_ge(E.sem(stp)[0], E.at(stp)[1])
                v.tensor_tensor((e1 if k % 2 == 0 else e3)[:, 0:TT], PB[6][:, 0:TT],
                                xrl[k % 2][:, 0:TT], ALU.add)
                E.fin(v, tag + "_wo_add")

            @b.sync
            def _(s, tt=tt, oc2=oc2, k=k):
                s.wait_ge(E.sem(tag + "_wo_add")[0], E.at(tag + "_wo_add")[1])
                stp = tag + ("_wo_st0" if k % 2 == 0 else "_wo_st1")
                s.dma_start(out=w["x_out"].ap()[oc2 * 128:(oc2 + 1) * 128,
                                                tt * TT:(tt + 1) * TT],
                            in_=(e1 if k % 2 == 0 else e3)[:, 0:TT]
                            ).then_inc(E.inc(stp, 16)[0], 16)

    @b.sync
    def _(s):
        s.wait_ge(E.sem(tag + "_wo_st0")[0], E.at(tag + "_wo_st0")[1])
        s.wait_ge(E.sem(tag + "_wo_st1")[0], E.at(tag + "_wo_st1")[1])
        E.fin(s, tag + "_done")


def build_moe_tiles(nc, ts, T):
    at = ts["_at"]
    AB = ts["AB"]
    xn_b = 16 * T
    TQ = min(512, T)
    ts["moe_Y"] = at("moe_Y", [128, 8 * T], BF16, AB + xn_b)
    ts["moe_H"] = at("moe_H", [128, 22 * TQ], BF16, AB + 2 * xn_b)
    ts["moe_dw"] = at("moe_dw", [128, 22 * 1024], BF16, AB + 2 * xn_b + 22 * TQ * 2)
    goff = AB + 2 * xn_b + 22 * TQ * 2 + 45056
    ts["moe_gw"] = at("moe_gw", [128, 2 * 1024], BF16, goff)
    ts["moe_uw"] = at("moe_uw", [128, 2 * 1024], BF16, goff + 4096)
    o = max(goff + 8192, AB + 66 * T + 16384)
    ts["rw_sb"] = at("rw_sb", [128, 64], BF16, o); o += 128
    ts["wte"] = at("wte", [8, T], BF16, o); o += 2 * T
    ts["sel8"] = at("c_sel8", [8, 1024], BF16, o); o += 2048
    ts["m8"] = at("m8", [128, 8], F32, o); o += 32
    ts["nmx"] = at("nmx", [128, 1], F32, o); o += 32
    ts["sre"] = at("sre", [128, 1], F32, o); o += 32
    ts["wbc"] = at("wbc", [128, 512], F32, o); o += 2048
    ts["prb"] = at("prb", [128, 8], F32, o); o += 32
    ts["wtm"] = at("wtm", [128, 8], BF16, o); o += 32
    assert o <= 206 * 1024, f"moe arena overflow {o}"
    return ts


def emit_l0_ffn(E, Ti, ts, w, pre=(), tag="moe"):
    b = E.block
    TQ = min(512, Ti)
    ntq = Ti // TQ
    ntile = Ti // 128
    NE, NIC = 8, 22
    xn, Y, H = ts["xn"], ts["moe_Y"], ts["moe_H"]
    dw, gw, uw = ts["moe_dw"], ts["moe_gw"], ts["moe_uw"]
    rw_sb, wte, sel8 = ts["rw_sb"], ts["wte"], ts["sel8"]
    m8, nmx, sre, wbc, prb, wtm = ts["m8"], ts["nmx"], ts["sre"], ts["wbc"], ts["prb"], ts["wtm"]
    e0, e1, e2 = ts["e0"], ts["e1"], ts["e2"]
    xrl = (ts["xrl0"], ts["xrl1"])
    ident_b = ts["ident_b"]
    PB, PBT = ts["PB"], ts["PBT"]

    @b.gpsimd
    def _(g):
        g.dma_start(out=rw_sb[:, 0:64].rearrange("p (c o) -> p c o", c=FCH),
                    in_=w["rwT"].ap().rearrange("(c p) o -> p c o", p=128)
                    ).then_inc(E.inc(tag + "_cld", 16)[0], 16)
        g.dma_start(out=sel8[:, :], in_=w["sel8"].ap()
                    ).then_inc(E.inc(tag + "_cld", 16)[0], 16)

    @b.tensor
    def _(t):
        E.wait_all(t, list(pre) + [tag + "_cld"])
        for tI in range(ntile):
            for fc in range(FCH):
                t.matmul(PB[6][:, tI * 8:(tI + 1) * 8],
                         xn[:, fc * Ti + tI * 128:fc * Ti + (tI + 1) * 128],
                         rw_sb[:, fc * 8:(fc + 1) * 8],
                         start=(fc == 0), stop=(fc == FCH - 1))
        E.fin(t, tag + "_rt_mm")

    for tI in range(ntile):
        @b.vector
        def _(v, tI=tI):
            E.wait_all(v, [tag + "_rt_mm"])
            if tI >= 1:
                v.wait_ge(E.sem(tag + "_rt_ex")[0], E.at(tag + "_rt_ex")[1])
            lg = PB[6][:, tI * 8:(tI + 1) * 8]
            v.reduce_max(nmx[:, 0:1], lg, axis=AX.X, negate=True)
            E.fin(v, tag + "_rt_nm")

        @b.scalar
        def _(sc, tI=tI):
            sc.wait_ge(E.sem(tag + "_rt_nm")[0], E.at(tag + "_rt_nm")[1])
            sc.activation(prb[:, 0:8], PB[6][:, tI * 8:(tI + 1) * 8], AF.Exp,
                          bias=nmx[:, 0:1])
            E.fin(sc, tag + "_rt_ex")

        @b.vector
        def _(v, tI=tI):
            v.wait_ge(E.sem(tag + "_rt_ex")[0], E.at(tag + "_rt_ex")[1])
            if tI >= 1:
                v.wait_ge(E.sem(tag + "_rt_tr")[0], E.at(tag + "_rt_tr")[1])
            v.reduce_sum(sre[:, 0:1], prb[:, 0:8], axis=AX.X)
            v.reciprocal(sre[:, 0:1], sre[:, 0:1])
            v.tensor_scalar(prb[:, 0:8], prb[:, 0:8], sre[:, 0:1], None, ALU.mult)
            v.max(m8[:, 0:8], prb[:, 0:8])
            v.tensor_scalar(e0[:, 0:8], prb[:, 0:8], m8[:, 1:2], None, ALU.is_ge)
            v.tensor_tensor(wtm[:, 0:8], prb[:, 0:8], e0[:, 0:8], ALU.mult)
            E.fin(v, tag + "_rt_w")

        @b.tensor
        def _(t, tI=tI):
            t.wait_ge(E.sem(tag + "_rt_w")[0], E.at(tag + "_rt_w")[1])
            t.transpose(PBT[0:8, 0:128], wtm[:, 0:8], ident_b[:, 0:128])
            E.fin(t, tag + "_rt_tr")

        @b.vector
        def _(v, tI=tI):
            v.wait_ge(E.sem(tag + "_rt_tr")[0], E.at(tag + "_rt_tr")[1])
            v.tensor_copy(wte[0:8, tI * 128:(tI + 1) * 128], PBT[0:8, 0:128])
            E.fin(v, tag + "_rt_cp")

    for e in range(NE):
        @b.gpsimd
        def _(g, e=e):
            if e == 0:
                E.wait_all(g, list(pre))
            if e >= 1:
                g.wait_ge(E.sem(tag + "_dn_mm")[0], E.at(tag + "_dn_mm")[1])
            g.dma_start(out=dw[:, :].rearrange("p (ic h) -> p ic h", ic=NIC),
                        in_=w["dwT"].ap()[e].rearrange("(ic p) h -> p ic h", p=128)
                        ).then_inc(E.inc(tag + "_dwld", 16)[0], 16)

        for tq in range(ntq):
            for ic in range(NIC):
                j = ic
                @b.gpsimd
                def _(g, e=e, tq=tq, ic=ic):
                    k = (tq * NIC + ic)
                    if tq == 0 and ic == 0 and e == 0:
                        E.wait_all(g, list(pre))
                    if k >= 2 or e > 0:
                        g.wait_ge(E.sem(tag + "_h_mm")[0],
                                  max(0, E.at(tag + "_h_mm")[1] - 1))
                    gp = tag + ("_gld0" if ic % 2 == 0 else "_gld1")
                    for nm, buf in (("gwT", gw), ("uwT", uw)):
                        g.dma_start(
                            out=buf[:, (ic % 2) * 1024:(ic % 2) * 1024 + 1024]
                            .rearrange("p (c i) -> p c i", c=FCH),
                            in_=w[nm].ap()[e][:, ic * 128:(ic + 1) * 128]
                            .rearrange("(c p) i -> p c i", p=128)
                        ).then_inc(E.inc(gp, 16)[0], 16)

                @b.tensor
                def _(t, e=e, tq=tq, ic=ic):
                    gp = tag + ("_gld0" if ic % 2 == 0 else "_gld1")
                    t.wait_ge(E.sem(gp)[0], E.at(gp)[1])
                    if not (e == 0 and tq == 0 and ic == 0):
                        t.wait_ge(E.sem(tag + "_h_dve")[0],
                                  max(0, E.at(tag + "_h_dve")[1] - 1))
                    for fc in range(FCH):
                        t.matmul(PB[0 + (ic % 2)][:, 0:TQ],
                                 gw[:, (ic % 2) * 1024 + fc * 128:
                                    (ic % 2) * 1024 + (fc + 1) * 128],
                                 xn[:, fc * Ti + tq * TQ:fc * Ti + (tq + 1) * TQ],
                                 start=(fc == 0), stop=(fc == FCH - 1))
                    for fc in range(FCH):
                        t.matmul(PB[2 + (ic % 2)][:, 0:TQ],
                                 uw[:, (ic % 2) * 1024 + fc * 128:
                                    (ic % 2) * 1024 + (fc + 1) * 128],
                                 xn[:, fc * Ti + tq * TQ:fc * Ti + (tq + 1) * TQ],
                                 start=(fc == 0), stop=(fc == FCH - 1))
                    E.fin(t, tag + "_h_mm")

                @b.scalar
                def _(sc, e=e, tq=tq, ic=ic):
                    sc.wait_ge(E.sem(tag + "_h_mm")[0], E.at(tag + "_h_mm")[1])
                    sc.wait_ge(E.sem(tag + "_h_dve")[0],
                               max(0, E.at(tag + "_h_dve")[1] - 1))
                    sc.activation(e0[:, 0:TQ] if ic % 2 == 0 else e1[:, 0:TQ],
                                  PB[0 + (ic % 2)][:, 0:TQ], AF.Silu)
                    E.fin(sc, tag + "_h_act")

                @b.vector
                def _(v, e=e, tq=tq, ic=ic):
                    v.wait_ge(E.sem(tag + "_h_act")[0], E.at(tag + "_h_act")[1])
                    v.tensor_tensor(H[:, ic * TQ:(ic + 1) * TQ],
                                    PB[2 + (ic % 2)][:, 0:TQ],
                                    e0[:, 0:TQ] if ic % 2 == 0 else e1[:, 0:TQ],
                                    ALU.mult)
                    E.fin(v, tag + "_h_dve")

            @b.tensor
            def _(t, e=e, tq=tq):
                t.wait_ge(E.sem(tag + "_h_dve")[0], E.at(tag + "_h_dve")[1])
                t.wait_ge(E.sem(tag + "_rt_cp")[0], E.at(tag + "_rt_cp")[1])
                if not (e == 0 and tq == 0):
                    t.wait_ge(E.sem(tag + "_wbc_ev")[0], E.at(tag + "_wbc_ev")[1])
                t.matmul(PB[6][:, 0:TQ], sel8[:, e * 128:(e + 1) * 128],
                         wte[0:8, tq * TQ:(tq + 1) * TQ], start=True, stop=True)
                E.fin(t, tag + "_wbc_mm")

            @b.vector
            def _(v, e=e, tq=tq):
                v.wait_ge(E.sem(tag + "_wbc_mm")[0], E.at(tag + "_wbc_mm")[1])
                v.tensor_copy(wbc[:, 0:TQ], PB[6][:, 0:TQ])
                E.fin(v, tag + "_wbc_ev")

            for oc2 in range(FCH):
                @b.tensor
                def _(t, e=e, tq=tq, oc2=oc2):
                    t.wait_ge(E.sem(tag + "_dwld")[0], E.at(tag + "_dwld")[1])
                    k = oc2
                    if not (e == 0 and tq == 0 and oc2 <= 1):
                        t.wait_ge(E.sem(tag + "_dn_y")[0],
                                  max(0, E.at(tag + "_dn_y")[1] - 1))
                    for ic in range(NIC):
                        t.matmul(PB[4 + oc2 % 2][:, 0:TQ],
                                 dw[:, ic * 1024 + oc2 * 128:ic * 1024 + oc2 * 128 + 128],
                                 H[:, ic * TQ:(ic + 1) * TQ],
                                 start=(ic == 0), stop=(ic == NIC - 1))
                    E.fin(t, tag + "_dn_mm")

                @b.vector
                def _(v, e=e, tq=tq, oc2=oc2):
                    v.wait_ge(E.sem(tag + "_dn_mm")[0], E.at(tag + "_dn_mm")[1])
                    v.wait_ge(E.sem(tag + "_wbc_ev")[0], E.at(tag + "_wbc_ev")[1])
                    ysl = Y[:, oc2 * Ti + tq * TQ:oc2 * Ti + (tq + 1) * TQ]
                    if e == 0:
                        v.tensor_tensor(ysl, PB[4 + oc2 % 2][:, 0:TQ], wbc[:, 0:TQ],
                                        ALU.mult)
                    else:
                        v.tensor_tensor(e2[:, 0:TQ], PB[4 + oc2 % 2][:, 0:TQ],
                                        wbc[:, 0:TQ], ALU.mult)
                        v.tensor_tensor(ysl, ysl, e2[:, 0:TQ], ALU.add)
                    E.fin(v, tag + "_dn_y")

    TT = min(512, Ti)
    ntt = Ti // TT
    for tt in range(ntt):
        for oc2 in range(FCH):
            k = tt * FCH + oc2

            @b.sync
            def _(s, tt=tt, oc2=oc2, k=k):
                E.wait_all(s, [tag + "_dn_y"])
                stp = tag + ("_st0" if k % 2 == 0 else "_st1")
                if k >= 2:
                    s.wait_ge(E.sem(stp)[0], E.at(stp)[1])
                s.dma_start(out=xrl[k % 2][:, 0:TT],
                            in_=w["x_in"].ap()[oc2 * 128:(oc2 + 1) * 128,
                                               tt * TT:(tt + 1) * TT]
                            ).then_inc(E.inc(tag + "_xrl", 16)[0], 16)

            @b.vector
            def _(v, tt=tt, oc2=oc2, k=k):
                v.wait_ge(E.sem(tag + "_xrl")[0], E.at(tag + "_xrl")[1])
                stp = tag + ("_st0" if k % 2 == 0 else "_st1")
                if k >= 2:
                    v.wait_ge(E.sem(stp)[0], E.at(stp)[1])
                v.tensor_tensor((e1 if k % 2 == 0 else e2)[:, 0:TT],
                                Y[:, oc2 * Ti + tt * TT:oc2 * Ti + (tt + 1) * TT],
                                xrl[k % 2][:, 0:TT], ALU.add)
                E.fin(v, tag + "_add")

            @b.sync
            def _(s, tt=tt, oc2=oc2, k=k):
                s.wait_ge(E.sem(tag + "_add")[0], E.at(tag + "_add")[1])
                stp = tag + ("_st0" if k % 2 == 0 else "_st1")
                s.dma_start(out=w["x_out"].ap()[oc2 * 128:(oc2 + 1) * 128,
                                                tt * TT:(tt + 1) * TT],
                            in_=(e1 if k % 2 == 0 else e2)[:, 0:TT]
                            ).then_inc(E.inc(stp, 16)[0], 16)

    @b.sync
    def _(s):
        s.wait_ge(E.sem(tag + "_st0")[0], E.at(tag + "_st0")[1])
        s.wait_ge(E.sem(tag + "_st1")[0], E.at(tag + "_st1")[1])
        E.fin(s, tag + "_done")


def build_moe_program(T):
    nc = bass.Bass("TRN2", target_bir_lowering=False)
    w = {}
    w["x_in"] = nc.dram_tensor("x_in", [D_MODEL, T], F32, kind="ExternalInput")
    w["x_out"] = nc.dram_tensor("x_out", [D_MODEL, T], F32, kind="ExternalOutput")
    for nm, shp in (("rwT", [1024, 8]), ("gwT", [8, 1024, 2816]),
                    ("uwT", [8, 1024, 2816]), ("dwT", [8, 2816, 1024]),
                    ("sel8", [8, 1024]),
                    ("ident", [128, 128]), ("ones_f1", [128, 1]),
                    ("ones_1f", [1, 128]), ("mask_ut", [128, 128])):
        w[nm] = nc.dram_tensor(nm, shp, F32, kind="ExternalInput")
    es = ExitStack()
    with es:
        ts = build_base_tiles(nc, T)
        build_moe_tiles(nc, ts, T)
        with nc.Block() as block:
            E = Emit(nc, block, es)
            emit_const_init(E, ts, w)
            emit_load_x(E, T, ts, w)
            xn_list = [ts["xn"][:, c * T:(c + 1) * T] for c in range(FCH)]
            emit_ln(E, T, ts["x_fm"], xn_list, "ln",
                    consts=ts,
                    scratch={"sq": ts["ln_sq"], "stat0": ts["ln_s0"],
                             "stat1": ts["ln_s1"], "stat2": ts["ln_s2"],
                             "stat3": ts["ln_s3"]},
                    banks={"A": ts["psA"]},
                    pre=("ldx", "cst", "cst_g"))
            emit_l0_ffn(E, T, ts, w, pre=("ln_done",))

            @block.sync
            def _(s):
                E.wait_all(s, ["moe_done"])
    return nc


def moe_in_maps(x_flat, p, T):
    sh = {}
    sh["rwT"] = _bf(p["L0_ffn_router_w"].T)
    sh["gwT"] = _bf(p["L0_ffn_gate_w"].transpose(0, 2, 1))
    sh["uwT"] = _bf(p["L0_ffn_up_w"].transpose(0, 2, 1))
    sh["dwT"] = _bf(p["L0_ffn_down_w"].transpose(0, 2, 1))
    sel8 = np.zeros((8, 1024), np.float32)
    for e in range(8):
        sel8[e, e * 128:(e + 1) * 128] = 1.0
    sh["sel8"] = sel8
    sh["ident"] = np.eye(128, dtype=np.float32)
    sh["ones_f1"] = np.ones((128, 1), np.float32)
    sh["ones_1f"] = np.ones((1, 128), np.float32)
    jj, ii = np.meshgrid(np.arange(128), np.arange(128), indexing="ij")
    sh["mask_ut"] = (jj <= ii).astype(np.float32)
    maps = []
    for c in range(NCORES):
        d = dict(sh)
        d["x_in"] = np.ascontiguousarray(x_flat[c * T:(c + 1) * T].T)
        maps.append(d)
    return maps


def run_moe(x, p, T=None):
    B, S, D = x.shape
    x_flat = np.ascontiguousarray(x.reshape(B * S, D), dtype=np.float32)
    if T is None:
        T = B * S // NCORES
    nc = build_moe_program(T)
    maps = moe_in_maps(x_flat, p, T)
    res = run_bass_kernel_spmd(nc, maps, core_ids=list(range(NCORES)))
    outs = [np.asarray(res.results[c]["x_out"]).T for c in range(NCORES)]
    return np.concatenate(outs, 0).reshape(B, S, D)


def build_l1a_tiles(nc, ts, T):
    at = ts["_at"]
    AB = ts["AB"]
    LC = 256
    TE = T + 8
    big = 16 * T >= 30 * 1024
    lo = AB if big else None

    def lat(name, shape, dt, sz):
        nonlocal lo
        h = at(name, shape, dt, lo)
        lo += (sz + 31) // 32 * 32
        return h

    xse_off = AB + 16 * T
    ts["xse"] = at("xse", [128, 16 * TE], BF16, xse_off)
    scro = xse_off + 32 * T + 256
    BL = 32 * LC
    ts["cvacc"] = at("cvacc", [128, T], F32, scro)
    ts["dA"] = (at("dA0", [128, 16 * LC], BF16, scro),
                at("dA1", [128, 16 * LC], BF16, scro + BL))
    ts["dbx"] = (at("dbx0", [128, 16 * LC], BF16, scro + 2 * BL),
                 at("dbx1", [128, 16 * LC], BF16, scro + 3 * BL),
                 at("dbxc", [128, 16 * LC], BF16, scro + 4 * BL))
    ts["hsc"] = ts["dbx"]
    ts["Bbc"] = at("Bbc", [128, 16 * LC], BF16, scro + 5 * BL)
    ts["Cbc"] = at("Cbc", [128, 16 * LC], BF16, scro + 6 * BL)
    ts["dsl16"] = at("dsl16", [128, 16 * LC], BF16, scro + 7 * BL)
    o = scro + 8 * BL
    ts["spc1"] = at("spc1", [128, 4 * LC], BF16, o)
    ts["spna"] = at("spna", [128, 4 * LC], BF16, o + 8 * LC)
    ts["wxp"] = at("wxp_t", [128, 16 * 96], BF16, o)
    o += 16 * LC
    ts["u2"] = (at("u0", [128, LC], BF16, o),
                at("u1", [128, LC], BF16, o + 2 * LC))
    ts["sd16"] = at("sd16", [128, 16], F32, o + 4 * LC)
    o += 4 * LC + 64
    ts["wsl"] = at("wsl", [128, 2048], BF16, o); o += 4096
    ts["wdt"] = at("wdt_t", [64, 2048], BF16, o); o += 4096
    ts["nsc_sb"] = at("nsc_sb", [128, 8], F32, o); o += 32
    ts["res0"] = at("res0", [128, 512], BF16, scro)
    ts["res1"] = at("res1", [128, 512], BF16, scro + 1024)
    ts["xr2a"] = at("xr2a", [128, 512], F32, scro + 2048)
    ts["xr2b"] = at("xr2b", [128, 512], F32, scro + 4096)
    if not big:
        lo = o
        o += 30 * 1024
    ts["dtr"] = lat("dtr", [64, T], BF16, 2 * T)
    ts["BT"] = lat("BT", [16, T], BF16, 2 * T)
    ts["CT"] = lat("CT", [16, T], BF16, 2 * T)
    ts["A_sb"] = lat("A_sb", [128, 256], F32, 1024)
    ts["Ap_sb"] = lat("Ap_sb", [128, 256], F32, 1024)
    ts["states"] = lat("states", [128, 256], F32, 1024)
    ts["hacc"] = lat("hacc", [128, 256], F32, 1024)
    ts["t256"] = lat("t256", [128, 256], F32, 1024)
    ts["cw_sb"] = lat("cw_sb", [128, 64], F32, 256)
    ts["cb_sb"] = lat("cb_sb", [128, 16], F32, 64)
    ts["D_sb"] = lat("D_sb", [128, 16], F32, 64)
    ts["bdt_sb"] = lat("bdt_sb", [128, 16], F32, 64)
    ts["bdt_rT"] = lat("bdt_rT", [1, 2048], BF16, 4096)
    ts["seln"] = lat("seln", [16, 2048], BF16, 4096)
    ts["st3"] = lat("st3", [128, 512], F32, 2048)
    ts["y_sc"] = lat("y_sc", [128, LC], F32, 4 * LC)
    ts["dsl_sc"] = lat("dsl_sc", [128, LC], F32, 4 * LC)
    ts["dslb"] = lat("dslb", [128, LC], BF16, 2 * LC)
    ts["sd_sc"] = lat("sd_sc", [128, 16], F32, 64)
    ts["rl0"] = lat("rl0", [128, LC], BF16, 2 * LC)
    ts["rl1"] = lat("rl1", [128, LC], BF16, 2 * LC)
    if big:
        assert lo <= AB + 16 * T, f"late smalls overflow xn region {lo}"
        end = o
    else:
        end = max(o, lo)
    lim = nc.SBUF_PARTITION_SIZE_BYTES
    assert end <= lim, f"l1a arena overflow {end} > {lim}"
    return ts


def emit_l1_attn(E, Ti, ts, w, pre=(), tag="l1a"):
    b = E.block
    LC = 256
    nlc = Ti // LC
    TT = min(512, Ti)
    ntt = Ti // TT
    TE = Ti + 8
    xse = ts["xse"]
    dA2, dbx2, hsc2 = ts["dA"], ts["dbx"], ts["hsc"]
    dsl16, spc1, spna = ts["dsl16"], ts["spc1"], ts["spna"]
    u2, sd16 = ts["u2"], ts["sd16"]
    Bbc, Cbc, cvacc = ts["Bbc"], ts["Cbc"], ts["cvacc"]
    wsl, wxp, wdt = ts["wsl"], ts["wxp"], ts["wdt"]
    dtr, BT, CT = ts["dtr"], ts["BT"], ts["CT"]
    A_sb, Ap_sb, states, hacc, t256 = (ts["A_sb"], ts["Ap_sb"], ts["states"],
                                       ts["hacc"], ts["t256"])
    cw_sb, cb_sb, D_sb, bdt_sb = ts["cw_sb"], ts["cb_sb"], ts["D_sb"], ts["bdt_sb"]
    seln, st3, y_sc, sd_sc = ts["seln"], ts["st3"], ts["y_sc"], ts["sd_sc"]
    dsl_sc, dslb = ts["dsl_sc"], ts["dslb"]
    xr2 = (ts["xr2a"], ts["xr2b"])
    rl = (ts["rl0"], ts["rl1"])
    e0, e1, e3 = ts["e0"], ts["e1"], ts["e3"]
    xn = ts["xn"]
    agr, pmb, bmb = ts["agr"], ts["pmb"], ts["bmb"]
    xrl = (ts["xrl0"], ts["xrl1"])
    ones_1f = ts["ones_1f"]
    PB, PBT = ts["PB"], ts["PBT"]

    @b.gpsimd
    def _(g):
        E.wait_all(g, ["ln_done"])
        g.dma_start(out=wxp[:, :].rearrange("p (c o) -> p c o", c=16),
                    in_=w["wxp"].ap().rearrange("(c p) o -> p c o", p=128)
                    ).then_inc(E.inc(tag + "_cld", 16)[0], 16)
        g.dma_start(out=wdt[0:64, 0:2048], in_=w["wdt"].ap()
                    ).then_inc(E.inc(tag + "_cld", 16)[0], 16)

    wjobs = [(oc, tt) for oc in range(32) for tt in range(ntt)]
    cons_toks = []
    for j, (oc, tt) in enumerate(wjobs):
        if tt == 0:
            @b.gpsimd
            def _(g, j=j, oc=oc):
                gp = tag + ("_wld0" if oc % 2 == 0 else "_wld1")
                if oc < 2:
                    E.wait_all(g, ["ln_done"])
                if oc >= 2:
                    g.wait_ge(E.sem(tag + "_wi_mm")[0],
                              max(0, E.at(tag + "_wi_mm")[1] - ntt))
                g.dma_start(out=wsl[:, (oc % 2) * 1024:(oc % 2) * 1024 + 1024]
                            .rearrange("p (c i) -> p c i", c=FCH),
                            in_=w["w_in"].ap()[:, oc * 128:(oc + 1) * 128]
                            .rearrange("(c p) i -> p c i", p=128)
                            ).then_inc(E.inc(gp, 16)[0], 16)

        @b.tensor
        def _(t, j=j, oc=oc, tt=tt):
            gp = tag + ("_wld0" if oc % 2 == 0 else "_wld1")
            t.wait_ge(E.sem(gp)[0], E.at(gp)[1])
            if j == 0:
                E.wait_all(t, list(pre))
            if j >= 2:
                t.wait_ge(*cons_toks[j - 2])
            for fc in range(FCH):
                t.matmul(PB[j % 2][:, 0:TT],
                         wsl[:, (oc % 2) * 1024 + fc * 128:(oc % 2) * 1024 + (fc + 1) * 128],
                         xn[:, fc * Ti + tt * TT:fc * Ti + (tt + 1) * TT],
                         start=(fc == 0), stop=(fc == FCH - 1))
            E.fin(t, tag + "_wi_mm")

        if oc < 16:
            @b.vector
            def _(v, j=j, oc=oc, tt=tt):
                v.wait_ge(E.sem(tag + "_wi_mm")[0], E.at(tag + "_wi_mm")[1])
                v.tensor_copy(xse[:, oc * TE + 8 + tt * TT:oc * TE + 8 + (tt + 1) * TT],
                              PB[j % 2][:, 0:TT])
                E.fin(v, tag + "_wi_cv")
            cons_toks.append((E.sem(tag + "_wi_cv")[0], E.at(tag + "_wi_cv")[1]))
        else:
            j2 = (oc - 16) * ntt + tt

            @b.scalar
            def _(sc, j=j, oc=oc, tt=tt, j2=j2):
                sc.wait_ge(E.sem(tag + "_wi_mm")[0], E.at(tag + "_wi_mm")[1])
                stp = tag + ("_rs0" if j2 % 2 == 0 else "_rs1")
                if j2 >= 2:
                    sc.wait_ge(E.sem(stp)[0], E.at(stp)[1])
                sc.activation((ts["res0"] if j2 % 2 == 0 else ts["res1"])[:, 0:TT],
                              PB[j % 2][:, 0:TT], AF.Silu)
                E.fin(sc, tag + "_wi_ca")
            cons_toks.append((E.sem(tag + "_wi_ca")[0], E.at(tag + "_wi_ca")[1]))

            @b.sync
            def _(s, oc=oc, tt=tt, j2=j2):
                s.wait_ge(E.sem(tag + "_wi_ca")[0], E.at(tag + "_wi_ca")[1])
                stp = tag + ("_rs0" if j2 % 2 == 0 else "_rs1")
                s.dma_start(out=w["res_sp"].ap()[(oc - 16) * 128:(oc - 15) * 128,
                                                 tt * TT:(tt + 1) * TT],
                            in_=(ts["res0"] if j2 % 2 == 0 else ts["res1"])[:, 0:TT]
                            ).then_inc(E.inc(stp, 16)[0], 16)

    @b.gpsimd
    def _(g):
        g.wait_ge(E.sem(tag + "_wi_mm")[0], E.at(tag + "_wi_mm")[1])
        g.dma_start(out=seln[0:16, :], in_=w["selN"].ap()
                    ).then_inc(E.inc(tag + "_cld2", 16)[0], 16)
        for nm, buf in (("cwP", cw_sb), ("cbP", cb_sb), ("DP", D_sb),
                        ("bdtP", bdt_sb)):
            g.dma_start(out=buf[:, :], in_=w[nm].ap()
                        ).then_inc(E.inc(tag + "_cld2", 16)[0], 16)
        g.dma_start(out=ts["bdt_rT"][0:1, 0:2048], in_=w["bdt_row"].ap()
                    ).then_inc(E.inc(tag + "_cld2", 16)[0], 16)
        g.dma_start(out=ts["pmrow"][0:1, 0:8], in_=w["pmask"].ap()
                    ).then_inc(E.inc(tag + "_cld2", 16)[0], 16)
        g.dma_start(out=ts["bmrow"][0:1, 0:8], in_=w["hmask"].ap()
                    ).then_inc(E.inc(tag + "_cld2", 16)[0], 16)
        g.dma_start(out=A_sb[:, :].rearrange("p (c n) -> p c n", c=16),
                    in_=w["A_log"].ap().rearrange("(c p) n -> p c n", p=128)
                    ).then_inc(E.inc(tag + "_cld2", 16)[0], 16)

    @b.scalar
    def _(sc):
        E.wait_all(sc, [tag + "_cld2"])
        sc.activation(A_sb[:, :], A_sb[:, :], AF.Exp)
        E.fin(sc, tag + "_aexp")

    @b.vector
    def _(v):
        E.wait_all(v, [tag + "_aexp"])
        v.tensor_scalar(A_sb[:, :], A_sb[:, :], -1.0, None, ALU.mult)
        E.fin(v, tag + "_aneg")

    @b.tensor
    def _(t):
        E.wait_all(t, [tag + "_cld2"])
        t.matmul(PB[3][:, 0:8], ones_1f[0:1, :], ts["pmrow"][0:1, 0:8],
                 start=True, stop=True)
        t.matmul(PB[3][:, 8:16], ones_1f[0:1, :], ts["bmrow"][0:1, 0:8],
                 start=True, stop=True)
        E.fin(t, tag + "_mb_mm")

    @b.vector
    def _(v):
        E.wait_all(v, [tag + "_mb_mm"])
        v.tensor_copy(pmb[:, 0:8], PB[3][:, 0:8])
        v.tensor_copy(bmb[:, 0:8], PB[3][:, 8:16])
        E.fin(v, tag + "_mb_cp")

    @b.vector
    def _(v):
        E.wait_all(v, [tag + "_wi_cv", tag + "_wi_ca", tag + "_cld2"])
        v.memset(st3[:, 0:64], 0.0)
        for ic in range(16):
            v.tensor_copy(st3[:, ic * 4:ic * 4 + 3],
                          xse[:, ic * TE + 8 + Ti - 3:ic * TE + 8 + Ti])
        E.fin(v, tag + "_halo_st")

    @b.sync
    def _(s):
        E.wait_all(s, [tag + "_halo_st"])
        s.dma_start(out=w["cc2_in"].ap().rearrange("(c p) wd -> p c wd", p=128),
                    in_=st3[:, 0:64].rearrange("p (c wd) -> p c wd", wd=4)
                    ).then_inc(E.inc(tag + "_cc2in", 16)[0], 16)

    @b.gpsimd
    def _(g):
        E.wait_all(g, [tag + "_cc2in"])
        g.collective_compute(
            "AllGather", ALU.bypass, replica_groups=[list(range(NCORES))],
            ins=[w["cc2_in"].ap()], outs=[w["cc2_out"].ap()],
        ).then_inc(E.inc(tag + "_cc2")[0], 1)

    for r in range(NCORES):
        @b.sync
        def _(s, r=r):
            E.wait_all(s, [tag + "_cc2"])
            if r >= 1:
                s.wait_ge(E.sem(tag + "_hacc2")[0], E.at(tag + "_hacc2")[1])
            s.dma_start(out=agr[:, 0:64].rearrange("p (c wd) -> p c wd", wd=4),
                        in_=w["cc2_out"].ap()[r * 2048:(r + 1) * 2048, :]
                        .rearrange("(c p) wd -> p c wd", p=128)
                        ).then_inc(E.inc(tag + "_agl2", 16)[0], 16)

        @b.vector
        def _(v, r=r):
            v.wait_ge(E.sem(tag + "_agl2")[0], E.at(tag + "_agl2")[1])
            E.wait_all(v, [tag + "_mb_cp"])
            if r == 0:
                v.tensor_scalar(st3[:, 64:128], agr[:, 0:64], bmb[:, 0:1],
                                None, ALU.mult)
            else:
                v.scalar_tensor_tensor(st3[:, 64:128], agr[:, 0:64],
                                       bmb[:, r:r + 1], st3[:, 64:128],
                                       ALU.mult, ALU.add)
            E.fin(v, tag + "_hacc2")

    @b.vector
    def _(v):
        E.wait_all(v, [tag + "_hacc2"])
        for ic in range(16):
            v.tensor_copy(xse[:, ic * TE + 5:ic * TE + 8],
                          st3[:, 64 + ic * 4:64 + ic * 4 + 3])
        E.fin(v, tag + "_halo")

    for ic in range(16):
        @b.vector
        def _(v, ic=ic):
            E.wait_all(v, [tag + "_halo"])
            if ic == 0:
                E.wait_all(v, [tag + "_rs0", tag + "_rs1"])
            if ic >= 1:
                v.wait_ge(E.sem(tag + "_cvs")[0], E.at(tag + "_cvs")[1])
            base = ic * TE + 8
            v.tensor_scalar(cvacc[:, 0:Ti], xse[:, base - 3:base - 3 + Ti],
                            cw_sb[:, ic * 4:ic * 4 + 1], None, ALU.mult)
            for kk in range(1, 4):
                v.scalar_tensor_tensor(cvacc[:, 0:Ti],
                                       xse[:, base - 3 + kk:base - 3 + kk + Ti],
                                       cw_sb[:, ic * 4 + kk:ic * 4 + kk + 1],
                                       cvacc[:, 0:Ti], ALU.mult, ALU.add)
            E.fin(v, tag + "_cv")

        @b.scalar
        def _(sc, ic=ic):
            sc.wait_ge(E.sem(tag + "_cv")[0], E.at(tag + "_cv")[1])
            sc.activation(xse[:, ic * TE + 8:ic * TE + 8 + Ti], cvacc[:, 0:Ti],
                          AF.Silu, bias=cb_sb[:, ic:ic + 1])
            E.fin(sc, tag + "_cvs")

    for tt in range(ntt):
        for grp, (m0, m1) in enumerate(((0, 80), (80, 96))):
            @b.tensor
            def _(t, tt=tt, grp=grp, m0=m0, m1=m1):
                E.wait_all(t, [tag + "_cvs", tag + "_cld"])
                if not (tt == 0 and grp == 0):
                    t.wait_ge(E.sem(tag + "_xp_ev")[0], E.at(tag + "_xp_ev")[1])
                for ic in range(16):
                    t.matmul(PB[4][0:m1 - m0, 0:TT],
                             wxp[:, ic * 96 + m0:ic * 96 + m1],
                             xse[:, ic * TE + 8 + tt * TT:ic * TE + 8 + (tt + 1) * TT],
                             start=(ic == 0), stop=(ic == 15))
                E.fin(t, tag + "_xp_mm")

            @b.vector
            def _(v, tt=tt, grp=grp):
                v.wait_ge(E.sem(tag + "_xp_mm")[0], E.at(tag + "_xp_mm")[1])
                if grp == 0:
                    v.tensor_copy(dtr[0:64, tt * TT:(tt + 1) * TT], PB[4][0:64, 0:TT])
                    v.tensor_copy(BT[0:16, tt * TT:(tt + 1) * TT], PB[4][64:80, 0:TT])
                else:
                    v.tensor_copy(CT[0:16, tt * TT:(tt + 1) * TT], PB[4][0:16, 0:TT])
                E.fin(v, tag + "_xp_ev")

    def scan_phase(ph):
        K = nlc * 16
        psA_ = ts["psA"]
        base = {nm: E.at(tag + nm)[1] for nm in
                ("_sE", "_scan", "_yv", "_spd", "_dlcp",
                 "_sdacc", "_rl_ld0", "_rl_ld1")}

        def bw(eng, nm, v):
            if v > 0:
                eng.wait_ge(E.sem(tag + nm)[0], base[nm] + v)

        @b.vector
        def _(v):
            if ph == 1:
                E.wait_all(v, [tag + "_xp_ev", tag + "_aneg"])
                v.memset(states[:, :], 0.0)
                v.memset(sd_sc[:, :], 0.0)
            else:
                E.wait_all(v, [tag + "_st_ld"])
                v.tensor_copy(states[:, :], hacc[:, :])
            E.fin(v, tag + "_st_init")

        def emit_bc(lc):
            for n in range(16):
                @b.tensor
                def _(t, lc=lc, n=n, ph=ph):
                    if lc == 0 and n == 0:
                        E.wait_all(t, [tag + "_st_init"])
                    if not (lc == 0 and n <= (1 if ph == 1 else 0)):
                        t.wait_ge(E.sem(tag + "_bc_ev")[0],
                                  E.at(tag + "_bc_ev")[1] - (0 if ph == 2 else 1))
                    t.matmul((PB[6] if n % 2 == 0 else PB[5])[:, 0:LC],
                             seln[0:16, n * 128:(n + 1) * 128],
                             BT[0:16, lc * LC:(lc + 1) * LC], start=True, stop=True)
                    if ph == 2:
                        t.matmul(PB[4][:, 0:LC],
                                 seln[0:16, n * 128:(n + 1) * 128],
                                 CT[0:16, lc * LC:(lc + 1) * LC],
                                 start=True, stop=True)
                    E.fin(t, tag + "_bc_mm")

                @b.vector
                def _(v, lc=lc, n=n, ph=ph):
                    v.wait_ge(E.sem(tag + "_bc_mm")[0], E.at(tag + "_bc_mm")[1])
                    v.tensor_copy(Bbc[:, n * LC:(n + 1) * LC],
                                  (PB[6] if n % 2 == 0 else PB[5])[:, 0:LC])
                    if ph == 2:
                        v.tensor_copy(Cbc[:, n * LC:(n + 1) * LC],
                                      PB[4][:, 0:LC])
                    E.fin(v, tag + "_bc_ev")

        def emit_delta(lc):
            for bb in range(4):
                @b.tensor
                def _(t, lc=lc, bb=bb):
                    if lc == 0 and bb == 0:
                        E.wait_all(t, [tag + "_cld", tag + "_cvs", tag + "_cld2",
                                       tag + "_xp_ev", "cst_g"])
                    bw(t, "_dlcp", 4 * lc + bb)
                    for j in range(4):
                        dc = bb * 4 + j
                        sl = psA_[:, j * LC:(j + 1) * LC]
                        t.matmul(sl, wdt[0:64, dc * 128:(dc + 1) * 128],
                                 dtr[0:64, lc * LC:(lc + 1) * LC],
                                 start=True, stop=False)
                        t.matmul(sl[:, 0:128], ts["bdt_rT"][0:1, dc * 128:(dc + 1) * 128],
                                 ts["ones_1b"][0:1, 0:128], start=False, stop=False)
                        t.matmul(sl[:, 128:256], ts["bdt_rT"][0:1, dc * 128:(dc + 1) * 128],
                                 ts["ones_1b"][0:1, 0:128], start=False, stop=True)
                    E.fin(t, tag + "_dlb")

                @b.vector
                def _(v, lc=lc, bb=bb, ph=ph):
                    v.wait_ge(E.sem(tag + "_dlb")[0], E.at(tag + "_dlb")[1])
                    v.tensor_scalar(spc1[:, :], psA_[:, 0:4 * LC], 0.0, None,
                                    ALU.max)
                    v.scalar_tensor_tensor(spna[:, :], spc1[:, :], -2.0,
                                           psA_[:, 0:4 * LC], ALU.mult, ALU.add)
                    E.fin(v, tag + "_dlcp")

                @b.scalar
                def _(sc, lc=lc, bb=bb):
                    sc.wait_ge(E.sem(tag + "_dlcp")[0], E.at(tag + "_dlcp")[1])
                    sc.activation(spna[:, :], spna[:, :], AF.Exp)
                    sc.activation(spna[:, :], spna[:, :], AF.Ln,
                                  bias=ts["kon"][:, 3:4])
                    E.fin(sc, tag + "_spe")

                @b.vector
                def _(v, lc=lc, bb=bb, ph=ph):
                    v.wait_ge(E.sem(tag + "_spe")[0], E.at(tag + "_spe")[1])
                    if bb == 0:
                        bw(v, "_sE", lc * 16)
                    v.tensor_tensor(dsl16[:, bb * 4 * LC:(bb + 1) * 4 * LC],
                                    spc1[:, :], spna[:, :], ALU.add)
                    E.fin(v, tag + "_spd")
                    if ph == 1 and bb == 3:
                        v.reduce_sum(sd16[:, 0:16],
                                     bass.AP(dsl16, 0, [[16 * LC, 128], [LC, 16], [1, LC]]),
                                     axis=AX.X)
                        v.tensor_tensor(sd_sc[:, 0:16], sd_sc[:, 0:16],
                                        sd16[:, 0:16], ALU.add)
                        E.fin(v, tag + "_sdacc")

        def emit_tail(k):
            i = k - 1
            if 0 <= i < K:
                dci, lci = i % 16, i // 16
                pa3, p3 = i % 2, i % 2
                xsli = xse[:, dci * TE + 8 + lci * LC:dci * TE + 8 + (lci + 1) * LC]

                @b.vector
                def _(v, i=i, dci=dci, pa3=pa3, p3=p3, xsli=xsli):
                    bw(v, "_sE", i + 1)
                    if ph == 2:
                        rp = "_rl_ld0" if dci % 2 == 0 else "_rl_ld1"
                        v.wait_ge(E.sem(tag + rp)[0], base[rp] + 16 * (i // 2 + 1))
                    ve = object.__getattribute__(v, "_e")
                    for n in range(16):
                        eng = v if n in (0, 15) else ve
                        eng.tensor_tensor_scan(
                            hsc2[p3][:, n * LC:(n + 1) * LC],
                            dA2[pa3][:, n * LC:(n + 1) * LC],
                            dbx2[p3][:, n * LC:(n + 1) * LC],
                            states[:, dci * 16 + n:dci * 16 + n + 1],
                            ALU.mult, ALU.add)
                    v.tensor_copy(states[:, dci * 16:(dci + 1) * 16],
                                  bass.AP(hsc2[p3], LC - 1, [[16 * LC, 128], [LC, 16]]))
                    E.fin(v, tag + "_scan")
                    if ph == 2:
                        h = hsc2[p3]
                        v.tensor_tensor(h[:, :], h[:, :], Cbc[:, :], ALU.mult)
                        v.tensor_tensor(h[:, 0:8 * LC], h[:, 0:8 * LC],
                                        h[:, 8 * LC:16 * LC], ALU.add)
                        v.tensor_tensor(h[:, 0:4 * LC], h[:, 0:4 * LC],
                                        h[:, 4 * LC:8 * LC], ALU.add)
                        v.tensor_tensor(h[:, 0:2 * LC], h[:, 0:2 * LC],
                                        h[:, 2 * LC:4 * LC], ALU.add)
                        v.tensor_tensor(y_sc[:, 0:LC], h[:, 0:LC],
                                        h[:, LC:2 * LC], ALU.add)
                        v.scalar_tensor_tensor(y_sc[:, 0:LC], xsli,
                                               D_sb[:, dci:dci + 1], y_sc[:, 0:LC],
                                               ALU.mult, ALU.add)
                        v.tensor_tensor(xsli, y_sc[:, 0:LC],
                                        rl[dci % 2][:, 0:LC], ALU.mult)
                        E.fin(v, tag + "_yv")

        for k in range(K + 2):
            boundary = (k % 16 == 0)
            if boundary or k >= K:
                emit_tail(k)
            if k < K and boundary:
                emit_bc(k // 16)
                emit_delta(k // 16)
            if k < K:
                dc = k % 16
                lc = k // 16
                pa = k % 2
                p3 = k % 3
                xsl = xse[:, dc * TE + 8 + lc * LC:dc * TE + 8 + (lc + 1) * LC]

                @b.scalar
                def _(sc, k=k, dc=dc, lc=lc, pa=pa):
                    bw(sc, "_spd", 4 * (lc + 1))
                    bw(sc, "_scan", k - 1)
                    sce = object.__getattribute__(sc, "_e")
                    for n in range(16):
                        eng = sc if n in (0, 15) else sce
                        eng.activation(dA2[pa][:, n * LC:(n + 1) * LC],
                                       dsl16[:, dc * LC:(dc + 1) * LC], AF.Exp,
                                       scale=A_sb[:, dc * 16 + n:dc * 16 + n + 1])
                    E.fin(sc, tag + "_sE")

                if ph == 2:
                    @b.sync
                    def _(s, k=k, dc=dc, lc=lc):
                        if k == 0:
                            E.wait_all(s, [tag + "_rs0", tag + "_rs1"])
                        bw(s, "_yv", k - 1)
                        rp = tag + ("_rl_ld0" if dc % 2 == 0 else "_rl_ld1")
                        s.dma_start(out=rl[dc % 2][:, 0:LC],
                                    in_=w["res_sp"].ap()[dc * 128:(dc + 1) * 128,
                                                         lc * LC:(lc + 1) * LC]
                                    ).then_inc(E.inc(rp, 16)[0], 16)

                @b.vector
                def _(v, k=k, dc=dc, lc=lc, pa=pa, xsl=xsl):
                    bw(v, "_spd", 4 * (lc + 1))
                    v.tensor_tensor(u2[pa][:, 0:LC], dsl16[:, dc * LC:(dc + 1) * LC],
                                    xsl, ALU.mult)
                    ub = bass.AP(u2[pa], 0, [[LC, 128], [0, 16], [1, LC]])
                    v.tensor_tensor(dbx2[k % 2][:, :].rearrange("p (n l) -> p n l", n=16),
                                    ub, Bbc[:, :].rearrange("p (n l) -> p n l", n=16),
                                    ALU.mult)

            if not boundary and k < K:
                emit_tail(k)

    scan_phase(1)

    @b.vector
    def _(v):
        E.wait_all(v, [tag + "_sdacc", tag + "_aneg"])
        for dc in range(16):
            v.tensor_scalar(Ap_sb[:, dc * 16:(dc + 1) * 16],
                            A_sb[:, dc * 16:(dc + 1) * 16],
                            sd_sc[:, dc:dc + 1], None, ALU.mult)
        E.fin(v, tag + "_ap_s")

    @b.scalar
    def _(sc):
        E.wait_all(sc, [tag + "_ap_s"])
        sc.activation(Ap_sb[:, :], Ap_sb[:, :], AF.Exp)
        E.fin(sc, tag + "_ap_e")

    @b.vector
    def _(v):
        E.wait_all(v, [tag + "_scan", tag + "_ap_e"])
        v.tensor_copy(bass.AP(st3, 0, [[512, 128], [32, 16], [1, 16]]),
                      Ap_sb[:, :].rearrange("p (c n) -> p c n", n=16))
        v.tensor_copy(bass.AP(st3, 16, [[512, 128], [32, 16], [1, 16]]),
                      states[:, :].rearrange("p (c n) -> p c n", n=16))
        E.fin(v, tag + "_st3")

    @b.sync
    def _(s):
        E.wait_all(s, [tag + "_st3"])
        s.dma_start(out=w["cc3_in"].ap().rearrange("(c p) wd -> p c wd", p=128),
                    in_=st3[:, 0:512].rearrange("p (c wd) -> p c wd", wd=32)
                    ).then_inc(E.inc(tag + "_cc3in", 16)[0], 16)

    @b.gpsimd
    def _(g):
        E.wait_all(g, [tag + "_cc3in"])
        g.collective_compute(
            "AllGather", ALU.bypass, replica_groups=[list(range(NCORES))],
            ins=[w["cc3_in"].ap()], outs=[w["cc3_out"].ap()],
        ).then_inc(E.inc(tag + "_cc3")[0], 1)

    @b.vector
    def _(v):
        E.wait_all(v, [tag + "_cc3"])
        v.memset(hacc[:, :], 0.0)
        E.fin(v, tag + "_hk")

    for r in range(NCORES):
        @b.sync
        def _(s, r=r):
            E.wait_all(s, [tag + "_cc3"])
            s.wait_ge(E.sem(tag + "_hk")[0], E.at(tag + "_hk")[1])
            s.dma_start(out=agr[:, 0:512].rearrange("p (c wd) -> p c wd", wd=32),
                        in_=w["cc3_out"].ap()[r * 2048:(r + 1) * 2048, :]
                        .rearrange("(c p) wd -> p c wd", p=128)
                        ).then_inc(E.inc(tag + "_agl3", 16)[0], 16)

        @b.vector
        def _(v, r=r):
            v.wait_ge(E.sem(tag + "_agl3")[0], E.at(tag + "_agl3")[1])
            E.wait_all(v, [tag + "_mb_cp"])
            apv = bass.AP(agr, 0, [[528, 128], [32, 16], [1, 16]])
            blv = bass.AP(agr, 16, [[528, 128], [32, 16], [1, 16]])
            hv = hacc[:, :].rearrange("p (c n) -> p c n", n=16)
            t1 = t256[:, :].rearrange("p (c n) -> p c n", n=16)
            v.tensor_tensor(t1, apv, hv, ALU.mult)
            v.tensor_tensor(t1, t1, blv, ALU.add)
            v.tensor_tensor(t1, t1, hv, ALU.subtract)
            v.scalar_tensor_tensor(hv, t1, pmb[:, r:r + 1], hv, ALU.mult, ALU.add)
            E.fin(v, tag + "_hk")

    @b.vector
    def _(v):
        E.wait_all(v, [tag + "_hk"])
        E.fin(v, tag + "_st_ld")

    scan_phase(2)

    for oc2 in range(FCH):
        for tt in range(ntt):
            k = oc2 * ntt + tt
            if tt == 0:
                @b.gpsimd
                def _(g, oc2=oc2):
                    if oc2 >= 1:
                        g.wait_ge(E.sem(tag + "_ou_mm")[0], E.at(tag + "_ou_mm")[1])
                    g.dma_start(out=wsl[:, 0:2048]
                                .rearrange("p (c i) -> p c i", c=16),
                                in_=w["w_out"].ap()[:, oc2 * 128:(oc2 + 1) * 128]
                                .rearrange("(c p) i -> p c i", p=128)
                                ).then_inc(E.inc(tag + "_owld", 16)[0], 16)

            @b.tensor
            def _(t, tt=tt, oc2=oc2, k=k):
                t.wait_ge(E.sem(tag + "_owld")[0], E.at(tag + "_owld")[1])
                E.wait_all(t, [tag + "_yv"])
                if k >= 1:
                    t.wait_ge(E.sem(tag + "_ou_add")[0], E.at(tag + "_ou_add")[1])
                for ic in range(16):
                    t.matmul(PB[6][:, 0:TT],
                             wsl[:, ic * 128:(ic + 1) * 128],
                             xse[:, ic * TE + 8 + tt * TT:ic * TE + 8 + (tt + 1) * TT],
                             start=(ic == 0), stop=(ic == 15))
                E.fin(t, tag + "_ou_mm")

            @b.sync
            def _(s, tt=tt, oc2=oc2, k=k):
                stp = tag + ("_ost0" if k % 2 == 0 else "_ost1")
                if k < 2:
                    E.wait_all(s, [tag + "_yv"])
                if k >= 2:
                    s.wait_ge(E.sem(stp)[0], E.at(stp)[1])
                s.dma_start(out=xrl[k % 2][:, 0:TT],
                            in_=w["x_in"].ap()[oc2 * 128:(oc2 + 1) * 128,
                                               tt * TT:(tt + 1) * TT]
                            ).then_inc(E.inc(tag + "_oxr", 16)[0], 16)
                s.dma_start(out=xr2[k % 2][:, 0:TT],
                            in_=w["xn32"].ap()[oc2 * 128:(oc2 + 1) * 128,
                                               tt * TT:(tt + 1) * TT]
                            ).then_inc(E.inc(tag + "_oxn", 16)[0], 16)

            @b.vector
            def _(v, tt=tt, oc2=oc2, k=k):
                v.wait_ge(E.sem(tag + "_ou_mm")[0], E.at(tag + "_ou_mm")[1])
                v.wait_ge(E.sem(tag + "_oxr")[0], E.at(tag + "_oxr")[1])
                v.wait_ge(E.sem(tag + "_oxn")[0], E.at(tag + "_oxn")[1])
                stp = tag + ("_ost0" if k % 2 == 0 else "_ost1")
                if k >= 2:
                    v.wait_ge(E.sem(stp)[0], E.at(stp)[1])
                buf = (e1 if k % 2 == 0 else e3)
                v.tensor_tensor(buf[:, 0:TT], PB[6][:, 0:TT], xrl[k % 2][:, 0:TT],
                                ALU.add)
                v.tensor_tensor(buf[:, 0:TT], buf[:, 0:TT], xr2[k % 2][:, 0:TT],
                                ALU.add)
                E.fin(v, tag + "_ou_add")

            @b.sync
            def _(s, tt=tt, oc2=oc2, k=k):
                s.wait_ge(E.sem(tag + "_ou_add")[0], E.at(tag + "_ou_add")[1])
                stp = tag + ("_ost0" if k % 2 == 0 else "_ost1")
                s.dma_start(out=w["x_out"].ap()[oc2 * 128:(oc2 + 1) * 128,
                                                tt * TT:(tt + 1) * TT],
                            in_=(e1 if k % 2 == 0 else e3)[:, 0:TT]
                            ).then_inc(E.inc(stp, 16)[0], 16)

    @b.sync
    def _(s):
        s.wait_ge(E.sem(tag + "_ost0")[0], E.at(tag + "_ost0")[1])
        s.wait_ge(E.sem(tag + "_ost1")[0], E.at(tag + "_ost1")[1])
        E.fin(s, tag + "_done")


def build_l1a_program(T):
    nc = bass.Bass("TRN2", target_bir_lowering=False)
    w = {}
    w["x_in"] = nc.dram_tensor("x_in", [D_MODEL, T], F32, kind="ExternalInput")
    w["x_out"] = nc.dram_tensor("x_out", [D_MODEL, T], F32, kind="ExternalOutput")
    for nm, shp in (("w_in", [1024, 4096]), ("wxp", [2048, 96]),
                    ("wdt", [64, 2048]), ("w_out", [2048, 1024]),
                    ("A_log", [2048, 16]), ("selN", [16, 2048]),
                    ("cwP", [128, 64]), ("cbP", [128, 16]), ("DP", [128, 16]),
                    ("nscP", [128, 8]), ("bdtP", [128, 16]),
                    ("bdt_row", [1, 2048]),
                    ("ident", [128, 128]), ("ones_f1", [128, 1]),
                    ("ones_1f", [1, 128]), ("mask_ut", [128, 128]),
                    ("pmask", [1, 8]), ("hmask", [1, 8])):
        w[nm] = nc.dram_tensor(nm, shp, F32, kind="ExternalInput")
    w["xn32"] = nc.dram_tensor("xn32", [D_MODEL, T], F32)
    w["res_sp"] = nc.dram_tensor("res_sp", [2048, T], BF16)
    w["cc2_in"] = nc.dram_tensor("cc2_in", [2048, 4], F32)
    w["cc2_out"] = nc.dram_tensor("cc2_out", [NCORES * 2048, 4], F32)
    w["cc3_in"] = nc.dram_tensor("cc3_in", [2048, 32], F32)
    w["cc3_out"] = nc.dram_tensor("cc3_out", [NCORES * 2048, 32], F32)
    es = ExitStack()
    with es:
        ts = build_base_tiles(nc, T)
        build_l1a_tiles(nc, ts, T)
        with nc.Block() as block:
            E = Emit(nc, block, es)
            emit_const_init(E, ts, w)
            emit_load_x(E, T, ts, w)
            @block.gpsimd
            def _(g):
                g.dma_start(out=ts["nsc_sb"][:, :], in_=w["nscP"].ap()
                            ).then_inc(E.inc("nscld", 16)[0], 16)
            xn_list = [ts["xn"][:, c * T:(c + 1) * T] for c in range(FCH)]
            x32 = ts["x_fm"]
            emit_ln(E, T, ts["x_fm"], None, "ln", xn_f32=x32,
                    consts=ts,
                    scratch={"sq": ts["ln_sq"], "stat0": ts["ln_s0"],
                             "stat1": ts["ln_s1"], "stat2": ts["ln_s2"],
                             "stat3": ts["ln_s3"]},
                    banks={"A": ts["psA"]},
                    pre=("ldx", "cst", "cst_g"))

            @block.sync
            def _(s):
                E.wait_all(s, ["ln_done"])
                for c in range(FCH):
                    s.dma_start(out=w["xn32"].ap()[c * 128:(c + 1) * 128, :],
                                in_=ts["x_fm"][c][:, :]
                                ).then_inc(E.inc("l1a_xnsp", 16)[0], 16)

            emit_ln(E, T, None, xn_list, "rms", rms_of=x32, nsc=ts["nsc_sb"],
                    consts=ts,
                    scratch={"sq": ts["ln_sq"], "stat0": ts["ln_s0"],
                             "stat1": ts["ln_s1"], "stat2": ts["ln_s2"],
                             "stat3": ts["ln_s3"]},
                    banks={"A": ts["psA"]},
                    pre=("ln_done", "nscld", "l1a_xnsp"))
            emit_l1_attn(E, T, ts, w, pre=("rms_done",))

            @block.sync
            def _(s):
                E.wait_all(s, ["l1a_done"])
    return nc


def l1a_in_maps(x_flat, p, T):
    sh = {}
    sh["w_in"] = _bf(p["L1_attn_w_in"])
    sh["wxp"] = _bf(p["L1_attn_w_xproj"])
    sh["wdt"] = _bf(p["L1_attn_w_dt"])
    sh["w_out"] = _bf(p["L1_attn_w_out"])
    sh["A_log"] = _bf(p["L1_attn_A_log"])
    seln = np.zeros((16, 2048), np.float32)
    for n in range(16):
        seln[n, n * 128:(n + 1) * 128] = 1.0
    sh["selN"] = seln
    cw = p["L1_attn_conv_w"][:, 0, :]
    cwP = np.zeros((128, 64), np.float32)
    cbP = np.zeros((128, 16), np.float32)
    DP = np.zeros((128, 16), np.float32)
    bdtP = np.zeros((128, 16), np.float32)
    for c in range(16):
        for kk in range(4):
            cwP[:, c * 4 + kk] = cw[kk, c * 128:(c + 1) * 128]
        cbP[:, c] = p["L1_attn_conv_b"][c * 128:(c + 1) * 128]
        DP[:, c] = p["L1_attn_D"][c * 128:(c + 1) * 128]
        bdtP[:, c] = p["L1_attn_b_dt"][c * 128:(c + 1) * 128]
    sh["cwP"], sh["cbP"], sh["DP"], sh["bdtP"] = cwP, cbP, DP, bdtP
    sh["bdt_row"] = _bf(p["L1_attn_b_dt"].reshape(1, 2048))
    nscP = np.zeros((128, 8), np.float32)
    for c in range(8):
        nscP[:, c] = p["L1_attn_norm_scale"][c * 128:(c + 1) * 128]
    sh["nscP"] = nscP
    sh["ident"] = np.eye(128, dtype=np.float32)
    sh["ones_f1"] = np.ones((128, 1), np.float32)
    sh["ones_1f"] = np.ones((1, 128), np.float32)
    jj, ii = np.meshgrid(np.arange(128), np.arange(128), indexing="ij")
    sh["mask_ut"] = (jj <= ii).astype(np.float32)
    maps = []
    CPB = NCORES // 2
    for c in range(NCORES):
        d = dict(sh)
        d["x_in"] = np.ascontiguousarray(x_flat[c * T:(c + 1) * T].T)
        core = prep_core_inputs(x_flat, c, T)
        d["pmask"] = core["pmask"]
        d["hmask"] = core["halo_sel"]
        maps.append(d)
    return maps


def run_l1a(x, p, T=None):
    B, S, D = x.shape
    x_flat = np.ascontiguousarray(x.reshape(B * S, D), dtype=np.float32)
    if T is None:
        T = B * S // NCORES
    nc = build_l1a_program(T)
    maps = l1a_in_maps(x_flat, p, T)
    res = run_bass_kernel_spmd(nc, maps, core_ids=list(range(NCORES)))
    outs = [np.asarray(res.results[c]["x_out"]).T for c in range(NCORES)]
    return np.concatenate(outs, 0).reshape(B, S, D)


def build_l1f_tiles(nc, ts, T):
    at = ts["_at"]
    AB = ts["AB"]
    ntile = T // 128

    def mk_cursor(start):
        pos = [start]

        def nt(name, shape, dt, sz):
            h = at(name, shape, dt, pos[0])
            pos[0] += (sz + 31) // 32 * 32
            return h
        return nt, pos

    ts["f_xtm"] = at("f_xtm", [128, ntile * 1024], BF16, AB + 16 * T)
    ts["f_ot"] = at("f_ot", [128, 8 * T], BF16, AB + 16 * T)

    nt, p1 = mk_cursor(AB + 32 * T)
    ts["f_alphaT"] = nt("f_alphaT", [64, T], BF16, 2 * T)
    ts["f_browb"] = nt("f_browb", [1, 10816], BF16, 21632)
    ts["f_st4"] = nt("f_st4", [64, 1040], F32, 4160)
    ts["f_acc"] = nt("f_acc", [64, 1040], F32, 4160)
    ts["f_s0T"] = nt("f_s0T", [128, 8 * 64], BF16, 1024)
    ts["f_s0"] = nt("f_s0", [64, 1024], F32, 4096)
    ts["f_rows"] = nt("f_rows", [1, 1024], F32, 4096)
    ts["f_smc"] = nt("f_smc", [128, 8], BF16, 16)
    ts["f_smf"] = nt("f_smf", [128, 8], F32, 32)
    ts["f_tsc"] = nt("f_tsc", [128, 1], F32, 32)
    ts["f_bias"] = nt("f_bias", [128, 4], F32, 32)
    ts["f_b1p"] = nt("f_b1p", [128, 4], F32, 32)
    ts["f_bop"] = nt("f_bop", [128, 8], F32, 32)
    ts["f_r1"] = nt("f_r1", [64, 4], F32, 16)
    if "m8" not in ts:
        ts["m8"] = nt("m8", [128, 8], F32, 32)
        ts["nmx"] = nt("nmx", [128, 1], F32, 32)
        ts["sre"] = nt("sre", [128, 1], F32, 32)
    ts["f_wq"] = nt("f_wq", [128, 8 * 256], BF16, 4096)
    ntd, p2 = mk_cursor(p1[0])
    ts["f_h1"] = ntd("f_h1", [128, 4 * T], BF16, 8 * T)
    ts["f_alpha"] = ntd("f_alpha", [128, ntile * 64], BF16, T)
    ts["f_w1a"] = ntd("f_w1a", [128, 8 * 512], BF16, 8192)
    ts["f_w1b"] = ntd("f_w1b", [128, 8 * 512], BF16, 8192)
    ts["f_w2"] = ntd("f_w2", [128, 4 * 64], BF16, 512)
    adead0, adead1 = p1[0], p2[0]

    ntc, p4 = mk_cursor(adead0)
    ts["f_gi"] = ntc("f_gi", [64, 3072], BF16, 6144)
    ts["f_gh"] = ntc("f_gh", [64, 3072], BF16, 6144)
    ts["f_sn"] = ntc("f_sn", [64, 1024], F32, 4096)
    ts["f_snT"] = ntc("f_snT", [128, 8 * 64], BF16, 1024)
    ts["f_qs"] = ntc("f_qs", [64, 256], F32, 1024)
    ts["f_ks"] = ntc("f_ks", [64, 256], F32, 1024)
    ts["f_vs"] = ntc("f_vs", [64, 1024], BF16, 2048)
    ts["f_qT"] = ntc("f_qT", [128, 2 * 64], BF16, 256)
    ts["f_kT"] = ntc("f_kT", [128, 2 * 64], BF16, 256)
    ts["f_atT"] = ntc("f_atT", [64, 64], BF16, 128)
    ts["f_y"] = ntc("f_y", [64, 1024], F32, 4096)
    ts["f_ynT"] = ntc("f_ynT", [128, 8 * 64], BF16, 1024)
    ts["f_sv"] = ntc("f_sv", [64, 1024], BF16, 2048)
    ts["f_s64a"] = ntc("f_s64a", [64, 1024], BF16, 2048)
    ts["f_s64b"] = ntc("f_s64b", [64, 1024], F32, 4096)
    bp1 = AB if 16 * T >= 30 * 1024 else max(p2[0], p4[0])
    ntb, p3 = mk_cursor(bp1)
    ts["f_wbig"] = ntb("f_wbig", [128, 8 * 1024], BF16, 16384)
    ts["f_wih"] = ntb("f_wih", [128, 8 * 512], BF16, 8192)
    ts["f_si"] = ntb("f_si", [64, 1024], F32, 4096)
    ts["f_siT"] = ntb("f_siT", [128, 8 * 64], BF16, 1024)
    if bp1 == AB:
        assert p3[0] <= AB + 16 * T, f"l1f B1 overflow {p3[0]}"
    end = max(p2[0], p4[0], p3[0] if bp1 != AB else 0)
    assert end <= 206 * 1024, f"l1f arena overflow {end}"
    return ts


def emit_l1_ffn(E, Ti, ts, w, pre=(), tag="l1f"):
    b = E.block
    TT = min(512, Ti)
    ntt = Ti // TT
    ntile = Ti // 128
    xn = ts["xn"]
    xtm, h1, alpha, alphaT, ot = (ts["f_xtm"], ts["f_h1"], ts["f_alpha"],
                                  ts["f_alphaT"], ts["f_ot"])
    rows, browb = ts["f_rows"], ts["f_browb"]
    e0, e1, e3 = ts["e0"], ts["e1"], ts["e3"]
    xrl = (ts["xrl0"], ts["xrl1"])
    ones_1f, ones_b1, ident_b = ts["ones_1f"], ts["ones_b1"], ts["ident_b"]
    PB, PBT = ts["PB"], ts["PBT"]
    bmb = ts["bmb"]
    m8, nmx, sre = ts["m8"], ts["nmx"], ts["sre"]

    @b.gpsimd
    def _(g):
        E.wait_all(g, list(pre))
        for nm, buf, c in (("w1a", ts["f_w1a"], 8), ("w1b", ts["f_w1b"], 8),
                           ("w2w", ts["f_w2"], 4), ("wqw", ts["f_wq"], 8)):
            g.dma_start(out=buf[:, :].rearrange("p (c o) -> p c o", c=c),
                        in_=w[nm].ap().rearrange("(c p) o -> p c o", p=128)
                        ).then_inc(E.inc(tag + "_cld", 16)[0], 16)
        g.dma_start(out=ts["f_s0T"][:, :].rearrange("p (c o) -> p c o", c=8),
                    in_=w["slot_initT"].ap().rearrange("(c p) o -> p c o", p=128)
                    ).then_inc(E.inc(tag + "_cld", 16)[0], 16)
        g.dma_start(out=ts["f_s0"][0:64, :], in_=w["slot_init"].ap()
                    ).then_inc(E.inc(tag + "_cld", 16)[0], 16)
        g.dma_start(out=rows[0:1, 0:1], in_=w["tau"].ap()
                    ).then_inc(E.inc(tag + "_cld", 16)[0], 16)
        g.dma_start(out=ts["bmrow"][0:1, 0:8], in_=w["bmask"].ap()
                    ).then_inc(E.inc(tag + "_cld", 16)[0], 16)
        g.dma_start(out=ts["f_b1p"][:, 0:4], in_=w["b1P"].ap()
                    ).then_inc(E.inc(tag + "_cld", 16)[0], 16)
        g.dma_start(out=ts["f_bop"][:, 0:8], in_=w["boP"].ap()
                    ).then_inc(E.inc(tag + "_cld", 16)[0], 16)
        g.dma_start(out=browb[0:1, 0:10816], in_=w["brows"].ap()
                    ).then_inc(E.inc(tag + "_cld", 16)[0], 16)

    @b.tensor
    def _(t):
        E.wait_all(t, [tag + "_cld"])
        t.matmul(PB[3][:, 0:8], ones_1f[0:1, :], ts["bmrow"][0:1, 0:8],
                 start=True, stop=True)
        E.fin(t, tag + "_mb_mm")

    @b.vector
    def _(v):
        E.wait_all(v, [tag + "_mb_mm"])
        v.tensor_copy(bmb[:, 0:8], PB[3][:, 0:8])
        E.fin(v, tag + "_mb_cp")

    @b.scalar
    def _(sc):
        E.wait_all(sc, [tag + "_cld"])
        sc.activation(rows[0:1, 1:2], rows[0:1, 0:1], AF.Abs)
        E.fin(sc, tag + "_tau1")

    @b.vector
    def _(v):
        E.wait_all(v, [tag + "_tau1"])
        v.tensor_scalar(rows[0:1, 1:2], rows[0:1, 1:2], 0.1, None, ALU.add)
        v.reciprocal(rows[0:1, 1:2], rows[0:1, 1:2])
        E.fin(v, tag + "_tau2")

    @b.tensor
    def _(t):
        E.wait_all(t, [tag + "_tau2"])
        t.matmul(PB[3][:, 8:9], ones_1f[0:1, :], rows[0:1, 1:2],
                 start=True, stop=True)
        E.fin(t, tag + "_tau3")

    @b.vector
    def _(v):
        E.wait_all(v, [tag + "_tau3"])
        v.tensor_copy(ts["f_tsc"][:, 0:1], PB[3][:, 8:9])
        E.fin(v, tag + "_tau")

    @b.vector
    def _(v):
        E.wait_all(v, [tag + "_cld"])
        for c in range(8):
            v.reduce_sum(ts["f_smf"][:, c:c + 1],
                         ts["f_s0T"][:, c * 64:(c + 1) * 64], axis=AX.X)
        v.tensor_scalar(ts["f_smc"][:, 0:8], ts["f_smf"][:, 0:8], 1.0 / 64,
                        None, ALU.mult)
        E.fin(v, tag + "_smc")

    @b.tensor
    def _(t):
        E.wait_all(t, [tag + "_smc"])
        for c in range(8):
            t.matmul(PB[4][0:1, 0:512], ts["f_smc"][:, c:c + 1],
                     ts["f_w1b"][:, c * 512:(c + 1) * 512],
                     start=(c == 0), stop=(c == 7))
        E.fin(t, tag + "_cv_mm")

    @b.vector
    def _(v):
        E.wait_all(v, [tag + "_cv_mm"])
        v.tensor_copy(rows[0:1, 512:1024], PB[4][0:1, 0:512])
        E.fin(v, tag + "_cv_ev")

    @b.sync
    def _(s):
        E.wait_all(s, [tag + "_cv_ev"])
        s.dma_start(out=w["cv_sc"].ap(), in_=rows[0:1, 512:1024]
                    ).then_inc(E.inc(tag + "_cvd1", 16)[0], 16)
        E.wait_all(s, [tag + "_cvd1"])
        s.dma_start(out=ts["f_bias"][:, 0:4],
                    in_=bass.AP(w["cv_sc"], 0, [[1, 128], [128, 4]]),
                    allow_slow_non_contiguous=True
                    ).then_inc(E.inc(tag + "_cvd2", 16)[0], 16)

    @b.vector
    def _(v):
        E.wait_all(v, [tag + "_cvd2"])
        v.tensor_tensor(ts["f_bias"][:, 0:4], ts["f_bias"][:, 0:4],
                        ts["f_b1p"][:, 0:4], ALU.add)
        E.fin(v, tag + "_bias")

    for jc in range(4):
        for tt in range(ntt):
            k = jc * ntt + tt

            @b.tensor
            def _(t, jc=jc, tt=tt, k=k):
                if k == 0:
                    E.wait_all(t, list(pre) + [tag + "_cld"])
                if k >= 2:
                    t.wait_ge(E.sem(tag + "_h1_act")[0],
                              E.at(tag + "_h1_act")[1] - 1)
                for fc in range(FCH):
                    t.matmul(PB[k % 2][:, 0:TT],
                             ts["f_w1a"][:, fc * 512 + jc * 128:fc * 512 + (jc + 1) * 128],
                             xn[:, fc * Ti + tt * TT:fc * Ti + (tt + 1) * TT],
                             start=(fc == 0), stop=(fc == FCH - 1))
                E.fin(t, tag + "_h1_mm")

            @b.scalar
            def _(sc, jc=jc, tt=tt, k=k):
                sc.wait_ge(E.sem(tag + "_h1_mm")[0], E.at(tag + "_h1_mm")[1])
                E.wait_all(sc, [tag + "_bias"])
                sc.activation(h1[:, jc * Ti + tt * TT:jc * Ti + (tt + 1) * TT],
                              PB[k % 2][:, 0:TT], AF.Gelu,
                              bias=ts["f_bias"][:, jc:jc + 1])
                E.fin(sc, tag + "_h1_act")

    for tI in range(ntile):
        for fc in range(FCH):
            kk = tI * FCH + fc

            @b.tensor
            def _(t, tI=tI, fc=fc, kk=kk):
                if kk == 0:
                    E.wait_all(t, list(pre))
                if kk >= 1:
                    t.wait_ge(E.sem(tag + "_xt_ev")[0], E.at(tag + "_xt_ev")[1])
                t.transpose(PBT[:, 0:128],
                            xn[:, fc * Ti + tI * 128:fc * Ti + (tI + 1) * 128],
                            ident_b[:, 0:128])
                E.fin(t, tag + "_xt_tr")

            @b.vector
            def _(v, tI=tI, fc=fc, kk=kk):
                v.wait_ge(E.sem(tag + "_xt_tr")[0], E.at(tag + "_xt_tr")[1])
                v.tensor_copy(xtm[:, tI * 1024 + fc * 128:tI * 1024 + (fc + 1) * 128],
                              PBT[:, 0:128])
                E.fin(v, tag + "_xt_ev")

    for tI in range(ntile):
        @b.tensor
        def _(t, tI=tI):
            E.wait_all(t, [tag + "_h1_act"])
            if tI >= 1:
                t.wait_ge(E.sem(tag + "_rt_w")[0], E.at(tag + "_rt_w")[1])
            for q in range(4):
                t.matmul(PB[6][:, 0:64],
                         h1[:, q * Ti + tI * 128:q * Ti + (tI + 1) * 128],
                         ts["f_w2"][:, q * 64:(q + 1) * 64],
                         start=(q == 0), stop=False)
            t.matmul(PB[6][:, 0:64], ts["ones_1b"][0:1, 0:128], browb[0:1, 0:64],
                     start=False, stop=True)
            E.fin(t, tag + "_rt_mm")

        @b.vector
        def _(v, tI=tI):
            v.wait_ge(E.sem(tag + "_rt_mm")[0], E.at(tag + "_rt_mm")[1])
            E.wait_all(v, [tag + "_tau"])
            v.tensor_scalar(e0[:, 0:64], PB[6][:, 0:64], ts["f_tsc"][:, 0:1],
                            None, ALU.mult)
            v.max(m8[:, 0:8], e0[:, 0:64])
            v.reduce_max(nmx[:, 0:1], e0[:, 0:64], axis=AX.X, negate=True)
            E.fin(v, tag + "_rt_m")

        @b.scalar
        def _(sc, tI=tI):
            sc.wait_ge(E.sem(tag + "_rt_m")[0], E.at(tag + "_rt_m")[1])
            sc.activation(e1[:, 0:64], e0[:, 0:64], AF.Exp, bias=nmx[:, 0:1])
            E.fin(sc, tag + "_rt_e")

        @b.vector
        def _(v, tI=tI):
            v.wait_ge(E.sem(tag + "_rt_e")[0], E.at(tag + "_rt_e")[1])
            v.tensor_scalar(e3[:, 0:64], e0[:, 0:64], m8[:, 7:8], None, ALU.is_ge)
            v.tensor_tensor(e1[:, 0:64], e1[:, 0:64], e3[:, 0:64], ALU.mult)
            v.reduce_sum(sre[:, 0:1], e1[:, 0:64], axis=AX.X)
            v.reciprocal(sre[:, 0:1], sre[:, 0:1])
            v.tensor_scalar(alpha[:, tI * 64:(tI + 1) * 64], e1[:, 0:64],
                            sre[:, 0:1], None, ALU.mult)
            E.fin(v, tag + "_rt_w")

        @b.tensor
        def _(t, tI=tI):
            t.wait_ge(E.sem(tag + "_rt_w")[0], E.at(tag + "_rt_w")[1])
            t.transpose(PBT[0:64, 0:128], alpha[:, tI * 64:(tI + 1) * 64],
                        ident_b[:, 0:128])
            E.fin(t, tag + "_at_tr")

        @b.vector
        def _(v, tI=tI):
            v.wait_ge(E.sem(tag + "_at_tr")[0], E.at(tag + "_at_tr")[1])
            v.tensor_copy(alphaT[0:64, tI * 128:(tI + 1) * 128], PBT[0:64, 0:128])
            E.fin(v, tag + "_at_ev")

    @b.tensor
    def _(t):
        E.wait_all(t, [tag + "_rt_w", tag + "_xt_ev"])
        for half in range(2):
            for tI in range(ntile):
                t.matmul(PB[4 + half][0:64, 0:512],
                         alpha[:, tI * 64:(tI + 1) * 64],
                         xtm[:, tI * 1024 + half * 512:tI * 1024 + (half + 1) * 512],
                         start=(tI == 0), stop=(tI == ntile - 1))
        for tI in range(ntile):
            t.matmul(PB[3][0:64, 0:1], alpha[:, tI * 64:(tI + 1) * 64],
                     ones_b1[:, 0:1], start=(tI == 0), stop=(tI == ntile - 1))
        E.fin(t, tag + "_sip_mm")

    @b.vector
    def _(v):
        E.wait_all(v, [tag + "_sip_mm"])
        v.tensor_copy(ts["f_st4"][0:64, 0:512], PB[4][0:64, 0:512])
        v.tensor_copy(ts["f_st4"][0:64, 512:1024], PB[5][0:64, 0:512])
        v.tensor_copy(ts["f_st4"][0:64, 1024:1025], PB[3][0:64, 0:1])
        v.memset(ts["f_st4"][0:64, 1025:1040], 0.0)
        E.fin(v, tag + "_sip_ev")

    @b.sync
    def _(s):
        E.wait_all(s, [tag + "_sip_ev"])
        s.dma_start(out=w["cc4_in"].ap(), in_=ts["f_st4"][0:64, 0:1040]
                    ).then_inc(E.inc(tag + "_cc4in", 16)[0], 16)

    @b.gpsimd
    def _(g):
        E.wait_all(g, [tag + "_cc4in"])
        g.collective_compute(
            "AllGather", ALU.bypass, replica_groups=[list(range(NCORES))],
            ins=[w["cc4_in"].ap()], outs=[w["cc4_out"].ap()],
        ).then_inc(E.inc(tag + "_cc4")[0], 1)

    for r in range(NCORES):
        @b.sync
        def _(s, r=r):
            E.wait_all(s, [tag + "_cc4"])
            if r >= 1:
                s.wait_ge(E.sem(tag + "_sac")[0], E.at(tag + "_sac")[1])
            s.dma_start(out=ts["f_st4"][0:64, 0:1040],
                        in_=w["cc4_out"].ap()[r * 64:(r + 1) * 64, :]
                        ).then_inc(E.inc(tag + "_agl4", 16)[0], 16)

        @b.vector
        def _(v, r=r):
            v.wait_ge(E.sem(tag + "_agl4")[0], E.at(tag + "_agl4")[1])
            E.wait_all(v, [tag + "_mb_cp"])
            if r == 0:
                v.tensor_scalar(ts["f_acc"][0:64, :], ts["f_st4"][0:64, :],
                                bmb[0:64, 0:1], None, ALU.mult)
            else:
                v.scalar_tensor_tensor(ts["f_acc"][0:64, :], ts["f_st4"][0:64, :],
                                       bmb[0:64, r:r + 1], ts["f_acc"][0:64, :],
                                       ALU.mult, ALU.add)
            E.fin(v, tag + "_sac")

    @b.vector
    def _(v):
        E.wait_all(v, [tag + "_sac"])
        v.tensor_scalar(ts["f_r1"][0:64, 0:1], ts["f_acc"][0:64, 1024:1025],
                        1e-8, None, ALU.add)
        v.reciprocal(ts["f_r1"][0:64, 0:1], ts["f_r1"][0:64, 0:1])
        v.tensor_scalar(ts["f_si"][0:64, :], ts["f_acc"][0:64, 0:1024],
                        ts["f_r1"][0:64, 0:1], None, ALU.mult)
        E.fin(v, tag + "_si")

    for c in range(8):
        @b.tensor
        def _(t, c=c):
            E.wait_all(t, [tag + "_si"])
            if c >= 1:
                t.wait_ge(E.sem(tag + "_siT_ev")[0], E.at(tag + "_siT_ev")[1])
            t.transpose(PB[6][:, 0:64], ts["f_si"][0:64, c * 128:(c + 1) * 128],
                        ts["ident_f"][0:64, 0:64])
            E.fin(t, tag + "_siT_tr")

        @b.vector
        def _(v, c=c):
            v.wait_ge(E.sem(tag + "_siT_tr")[0], E.at(tag + "_siT_tr")[1])
            v.tensor_copy(ts["f_siT"][:, c * 64:(c + 1) * 64], PB[6][:, 0:64])
            E.fin(v, tag + "_siT_ev")

    def gate_mm(nmW, lhs_sb, out_sb, tagx, brow_off):
        for jb in range(6):
            @b.gpsimd
            def _(g, jb=jb):
                E.wait_all(g, [tag + "_cld", tag + "_sip_mm"])
                if jb >= 1:
                    g.wait_ge(E.sem(tag + tagx + "_mm")[0],
                              E.at(tag + tagx + "_mm")[1])
                elif nmW == "WhhT":
                    g.wait_ge(E.sem(tag + "_gi_mm")[0], E.at(tag + "_gi_mm")[1])
                g.dma_start(out=ts["f_wih"][:, :].rearrange("p (c o) -> p c o", c=8),
                            in_=w[nmW].ap()[:, jb * 512:(jb + 1) * 512]
                            .rearrange("(c p) o -> p c o", p=128)
                            ).then_inc(E.inc(tag + tagx + "_ld", 16)[0], 16)

            @b.tensor
            def _(t, jb=jb):
                t.wait_ge(E.sem(tag + tagx + "_ld")[0], E.at(tag + tagx + "_ld")[1])
                E.wait_all(t, [tag + "_siT_ev"])
                if jb >= 1:
                    t.wait_ge(E.sem(tag + tagx + "_ev")[0],
                              E.at(tag + tagx + "_ev")[1])
                elif nmW == "WhhT":
                    t.wait_ge(E.sem(tag + "_gi_ev")[0], E.at(tag + "_gi_ev")[1])
                for c in range(8):
                    t.matmul(PB[2][0:64, 0:512],
                             lhs_sb[:, c * 64:(c + 1) * 64],
                             ts["f_wih"][:, c * 512:(c + 1) * 512],
                             start=(c == 0), stop=False)
                t.matmul(PB[2][0:64, 0:512], ts["ones_1b"][0:1, 0:64],
                         browb[0:1, brow_off + jb * 512:brow_off + (jb + 1) * 512],
                         start=False, stop=True)
                E.fin(t, tag + tagx + "_mm")

            @b.vector
            def _(v, jb=jb):
                v.wait_ge(E.sem(tag + tagx + "_mm")[0], E.at(tag + tagx + "_mm")[1])
                v.tensor_copy(out_sb[0:64, jb * 512:(jb + 1) * 512],
                              PB[2][0:64, 0:512])
                E.fin(v, tag + tagx + "_ev")

    gate_mm("WihT", ts["f_siT"], ts["f_gi"], "_gi", 64)
    gate_mm("WhhT", ts["f_s0T"], ts["f_gh"], "_gh", 64 + 3072)

    @b.vector
    def _(v):
        E.wait_all(v, [tag + "_gi_ev", tag + "_gh_ev"])
        v.tensor_tensor(ts["f_s64a"][0:64, :], ts["f_gi"][0:64, 0:1024],
                        ts["f_gh"][0:64, 0:1024], ALU.add)
        v.tensor_tensor(ts["f_s64b"][0:64, :], ts["f_gi"][0:64, 1024:2048],
                        ts["f_gh"][0:64, 1024:2048], ALU.add)
        E.fin(v, tag + "_g1")

    @b.scalar
    def _(sc):
        E.wait_all(sc, [tag + "_g1"])
        sc.activation(ts["f_s64a"][0:64, :], ts["f_s64a"][0:64, :], AF.Sigmoid)
        sc.activation(ts["f_s64b"][0:64, :], ts["f_s64b"][0:64, :], AF.Sigmoid)
        E.fin(sc, tag + "_g2")

    @b.vector
    def _(v):
        E.wait_all(v, [tag + "_g2"])
        v.tensor_tensor(ts["f_vs"][0:64, :], ts["f_s64a"][0:64, :],
                        ts["f_gh"][0:64, 2048:3072], ALU.mult)
        v.tensor_tensor(ts["f_y"][0:64, :], ts["f_vs"][0:64, :],
                        ts["f_gi"][0:64, 2048:3072], ALU.add)
        E.fin(v, tag + "_g3")

    @b.scalar
    def _(sc):
        E.wait_all(sc, [tag + "_g3"])
        sc.activation(ts["f_y"][0:64, :], ts["f_y"][0:64, :], AF.Tanh)
        E.fin(sc, tag + "_g4")

    @b.vector
    def _(v):
        E.wait_all(v, [tag + "_g4"])
        v.tensor_tensor(ts["f_sn"][0:64, :], ts["f_s0"][0:64, :],
                        ts["f_y"][0:64, :], ALU.subtract)
        v.tensor_tensor(ts["f_sn"][0:64, :], ts["f_sn"][0:64, :],
                        ts["f_s64b"][0:64, :], ALU.mult)
        v.tensor_tensor(ts["f_sn"][0:64, :], ts["f_sn"][0:64, :],
                        ts["f_y"][0:64, :], ALU.add)
        E.fin(v, tag + "_snv")

    for c in range(8):
        @b.tensor
        def _(t, c=c):
            E.wait_all(t, [tag + "_snv"])
            if c >= 1:
                t.wait_ge(E.sem(tag + "_snT_ev")[0], E.at(tag + "_snT_ev")[1])
            t.transpose(PB[6][:, 0:64], ts["f_sn"][0:64, c * 128:(c + 1) * 128],
                        ts["ident_f"][0:64, 0:64])
            E.fin(t, tag + "_snT_tr")

        @b.vector
        def _(v, c=c):
            v.wait_ge(E.sem(tag + "_snT_tr")[0], E.at(tag + "_snT_tr")[1])
            v.tensor_copy(ts["f_snT"][:, c * 64:(c + 1) * 64], PB[6][:, 0:64])
            E.fin(v, tag + "_snT_ev")

    def qkv_mm(wbuf, nout, out_sb, tagx, brow_off, wname=None, prev_ev=None):
        nblk = ceil_div(nout, 512)
        for nb in range(nblk):
            wdt_ = min(512, nout - nb * 512)

            @b.tensor
            def _(t, nb=nb, wdt_=wdt_):
                E.wait_all(t, [tag + "_snT_ev", tag + "_cld"])
                if wname is not None:
                    t.wait_ge(E.sem(tag + tagx + "_ld")[0],
                              E.at(tag + tagx + "_ld")[1])
                if nb >= 1:
                    t.wait_ge(E.sem(tag + tagx + "_ev")[0],
                              E.at(tag + tagx + "_ev")[1])
                elif prev_ev is not None:
                    t.wait_ge(E.sem(tag + prev_ev)[0], E.at(tag + prev_ev)[1])
                for c in range(8):
                    t.matmul(PB[2][0:64, 0:wdt_],
                             ts["f_snT"][:, c * 64:(c + 1) * 64],
                             wbuf[:, c * nout + nb * 512:c * nout + nb * 512 + wdt_],
                             start=(c == 0), stop=False)
                t.matmul(PB[2][0:64, 0:wdt_], ts["ones_1b"][0:1, 0:64],
                         browb[0:1, brow_off + nb * 512:brow_off + nb * 512 + wdt_],
                         start=False, stop=True)
                E.fin(t, tag + tagx + "_mm")

            @b.vector
            def _(v, nb=nb, wdt_=wdt_):
                v.wait_ge(E.sem(tag + tagx + "_mm")[0], E.at(tag + tagx + "_mm")[1])
                v.tensor_copy(out_sb[0:64, nb * 512:nb * 512 + wdt_],
                              PB[2][0:64, 0:wdt_])
                E.fin(v, tag + tagx + "_ev")

    OFF_BQ = 64 + 6144
    qkv_mm(ts["f_wq"], 256, ts["f_qs"], "_q", OFF_BQ, prev_ev="_gh_ev")

    @b.gpsimd
    def _(g):
        g.wait_ge(E.sem(tag + "_q_mm")[0], E.at(tag + "_q_mm")[1])
        g.dma_start(out=ts["f_wq"][:, :].rearrange("p (c o) -> p c o", c=8),
                    in_=w["wkw"].ap().rearrange("(c p) o -> p c o", p=128)
                    ).then_inc(E.inc(tag + "_k_ld", 16)[0], 16)

    qkv_mm(ts["f_wq"], 256, ts["f_ks"], "_k", OFF_BQ + 256, wname="wk",
           prev_ev="_q_ev")

    @b.gpsimd
    def _(g):
        E.wait_all(g, [tag + "_cld", tag + "_sip_mm"])
        g.dma_start(out=ts["f_wbig"][:, :].rearrange("p (c o) -> p c o", c=8),
                    in_=w["wvw"].ap().rearrange("(c p) o -> p c o", p=128)
                    ).then_inc(E.inc(tag + "_v_ld", 16)[0], 16)

    def vvp_mm(outname, tagx, brow_off, out_bf):
        for nb in range(2):
            @b.tensor
            def _(t, nb=nb):
                t.wait_ge(E.sem(tag + "_v_ld")[0], E.at(tag + "_v_ld")[1])
                E.wait_all(t, [tag + ("_snT_ev" if tagx == "_v" else "_ynT_ev")])
                if nb >= 1:
                    t.wait_ge(E.sem(tag + tagx + "_ev")[0],
                              E.at(tag + tagx + "_ev")[1])
                else:
                    pev = tag + ("_k_ev" if tagx == "_v" else "_v_ev")
                    t.wait_ge(E.sem(pev)[0], E.at(pev)[1])
                lhs = ts["f_snT"] if tagx == "_v" else ts["f_ynT"]
                for c in range(8):
                    t.matmul(PB[2][0:64, 0:512],
                             lhs[:, c * 64:(c + 1) * 64],
                             ts["f_wbig"][:, c * 1024 + nb * 512:c * 1024 + (nb + 1) * 512],
                             start=(c == 0), stop=False)
                t.matmul(PB[2][0:64, 0:512], ts["ones_1b"][0:1, 0:64],
                         browb[0:1, brow_off + nb * 512:brow_off + (nb + 1) * 512],
                         start=False, stop=True)
                E.fin(t, tag + tagx + "_mm")

            @b.vector
            def _(v, nb=nb):
                v.wait_ge(E.sem(tag + tagx + "_mm")[0], E.at(tag + tagx + "_mm")[1])
                v.tensor_copy(out_bf[0:64, nb * 512:(nb + 1) * 512],
                              PB[2][0:64, 0:512])
                E.fin(v, tag + tagx + "_ev")

    OFF_BV = OFF_BQ + 512
    vvp_mm("f_vs", "_v", OFF_BV, ts["f_vs"])

    for c in range(2):
        @b.tensor
        def _(t, c=c):
            E.wait_all(t, [tag + "_q_ev", tag + "_k_ev"])
            if c >= 1:
                t.wait_ge(E.sem(tag + "_qkT_ev")[0], E.at(tag + "_qkT_ev")[1])
            t.transpose(PB[6][:, 0:64], ts["f_qs"][0:64, c * 128:(c + 1) * 128],
                        ts["ident_f"][0:64, 0:64])
            t.transpose(PB[5][:, 0:64], ts["f_ks"][0:64, c * 128:(c + 1) * 128],
                        ts["ident_f"][0:64, 0:64])
            E.fin(t, tag + "_qkT_tr")

        @b.vector
        def _(v, c=c):
            v.wait_ge(E.sem(tag + "_qkT_tr")[0], E.at(tag + "_qkT_tr")[1])
            v.tensor_copy(ts["f_qT"][:, c * 64:(c + 1) * 64], PB[6][:, 0:64])
            v.tensor_copy(ts["f_kT"][:, c * 64:(c + 1) * 64], PB[5][:, 0:64])
            E.fin(v, tag + "_qkT_ev")

    @b.tensor
    def _(t):
        E.wait_all(t, [tag + "_qkT_ev"])
        for c in range(2):
            t.matmul(PB[4][0:64, 0:64], ts["f_qT"][:, c * 64:(c + 1) * 64],
                     ts["f_kT"][:, c * 64:(c + 1) * 64],
                     start=(c == 0), stop=(c == 1))
        E.fin(t, tag + "_sc_mm")

    @b.vector
    def _(v):
        E.wait_all(v, [tag + "_sc_mm"])
        v.tensor_scalar(e0[0:64, 0:64], PB[4][0:64, 0:64], 1.0 / 16, None, ALU.mult)
        v.reduce_max(nmx[0:64, 0:1], e0[0:64, 0:64], axis=AX.X, negate=True)
        E.fin(v, tag + "_sc_m")

    @b.scalar
    def _(sc):
        E.wait_all(sc, [tag + "_sc_m"])
        sc.activation(e0[0:64, 0:64], e0[0:64, 0:64], AF.Exp, bias=nmx[0:64, 0:1])
        E.fin(sc, tag + "_sc_e")

    @b.vector
    def _(v):
        E.wait_all(v, [tag + "_sc_e"])
        v.reduce_sum(sre[0:64, 0:1], e0[0:64, 0:64], axis=AX.X)
        v.reciprocal(sre[0:64, 0:1], sre[0:64, 0:1])
        v.tensor_scalar(e0[0:64, 0:64], e0[0:64, 0:64], sre[0:64, 0:1],
                        None, ALU.mult)
        E.fin(v, tag + "_sc_a")

    @b.tensor
    def _(t):
        E.wait_all(t, [tag + "_sc_a"])
        t.transpose(PB[6][0:64, 0:64], e0[0:64, 0:64], ts["ident_f"][0:64, 0:64])
        E.fin(t, tag + "_atr")

    @b.vector
    def _(v):
        E.wait_all(v, [tag + "_atr"])
        v.tensor_copy(ts["f_atT"][0:64, 0:64], PB[6][0:64, 0:64])
        E.fin(v, tag + "_atrv")

    @b.tensor
    def _(t):
        E.wait_all(t, [tag + "_atrv", tag + "_v_ev"])
        for nb in range(2):
            t.matmul(PB[4 + nb][0:64, 0:512], ts["f_atT"][0:64, 0:64],
                     ts["f_vs"][0:64, nb * 512:(nb + 1) * 512],
                     start=True, stop=True)
        E.fin(t, tag + "_scm_mm")

    @b.vector
    def _(v):
        E.wait_all(v, [tag + "_scm_mm"])
        v.tensor_tensor(ts["f_y"][0:64, 0:512], ts["f_sn"][0:64, 0:512],
                        PB[4][0:64, 0:512], ALU.add)
        v.tensor_tensor(ts["f_y"][0:64, 512:1024], ts["f_sn"][0:64, 512:1024],
                        PB[5][0:64, 0:512], ALU.add)
        v.reduce_sum(ts["f_r1"][0:64, 0:1], ts["f_y"][0:64, :], axis=AX.X)
        v.tensor_scalar(ts["f_r1"][0:64, 0:1], ts["f_r1"][0:64, 0:1], 1.0 / 1024,
                        None, ALU.mult)
        E.fin(v, tag + "_y1")

    @b.scalar
    def _(sc):
        E.wait_all(sc, [tag + "_y1"])
        sc.activation(ts["f_s64a"][0:64, :], ts["f_y"][0:64, :], AF.Square)
        E.fin(sc, tag + "_y2")

    @b.vector
    def _(v):
        E.wait_all(v, [tag + "_y2"])
        v.reduce_sum(ts["f_r1"][0:64, 1:2], ts["f_s64a"][0:64, :], axis=AX.X)
        v.tensor_scalar(ts["f_r1"][0:64, 1:2], ts["f_r1"][0:64, 1:2], 1.0 / 1024,
                        None, ALU.mult)
        v.tensor_tensor(ts["f_r1"][0:64, 2:3], ts["f_r1"][0:64, 0:1],
                        ts["f_r1"][0:64, 0:1], ALU.mult)
        v.tensor_tensor(ts["f_r1"][0:64, 1:2], ts["f_r1"][0:64, 1:2],
                        ts["f_r1"][0:64, 2:3], ALU.subtract)
        E.fin(v, tag + "_y3")

    @b.scalar
    def _(sc):
        E.wait_all(sc, [tag + "_y3"])
        sc.activation(ts["f_r1"][0:64, 1:2], ts["f_r1"][0:64, 1:2], AF.Ln,
                      bias=ts["kon"][0:64, 0:1])
        sc.activation(ts["f_r1"][0:64, 1:2], ts["f_r1"][0:64, 1:2], AF.Exp,
                      scale=ts["kon"][0:64, 1:2])
        E.fin(sc, tag + "_y4")

    @b.vector
    def _(v):
        E.wait_all(v, [tag + "_y4"])
        v.tensor_scalar(ts["f_y"][0:64, :], ts["f_y"][0:64, :],
                        ts["f_r1"][0:64, 0:1], ts["f_r1"][0:64, 1:2],
                        ALU.subtract, ALU.mult)
        E.fin(v, tag + "_y5")

    @b.tensor
    def _(t):
        E.wait_all(t, [tag + "_y5"])
        OFF_LNG = 64 + 6144 + 512 + 1024
        for nb in range(2):
            t.matmul(PB[4 + nb][0:64, 0:512], ts["ones_1b"][0:1, 0:64],
                     browb[0:1, OFF_LNG + nb * 512:OFF_LNG + (nb + 1) * 512],
                     start=True, stop=True)
            t.matmul(PB[nb][0:64, 0:512], ts["ones_1b"][0:1, 0:64],
                     browb[0:1, OFF_LNG + 1024 + nb * 512:OFF_LNG + 1024 + (nb + 1) * 512],
                     start=True, stop=True)
        E.fin(t, tag + "_lng_mm")

    @b.vector
    def _(v):
        E.wait_all(v, [tag + "_lng_mm"])
        for nb in range(2):
            v.tensor_tensor(ts["f_y"][0:64, nb * 512:(nb + 1) * 512],
                            ts["f_y"][0:64, nb * 512:(nb + 1) * 512],
                            PB[4 + nb][0:64, 0:512], ALU.mult)
            v.tensor_tensor(ts["f_y"][0:64, nb * 512:(nb + 1) * 512],
                            ts["f_y"][0:64, nb * 512:(nb + 1) * 512],
                            PB[nb][0:64, 0:512], ALU.add)
        E.fin(v, tag + "_y6")

    for c in range(8):
        @b.tensor
        def _(t, c=c):
            E.wait_all(t, [tag + "_y6"])
            if c >= 1:
                t.wait_ge(E.sem(tag + "_ynT_ev")[0], E.at(tag + "_ynT_ev")[1])
            t.transpose(PB[6][:, 0:64], ts["f_y"][0:64, c * 128:(c + 1) * 128],
                        ts["ident_f"][0:64, 0:64])
            E.fin(t, tag + "_ynT_tr")

        @b.vector
        def _(v, c=c):
            v.wait_ge(E.sem(tag + "_ynT_tr")[0], E.at(tag + "_ynT_tr")[1])
            v.tensor_copy(ts["f_ynT"][:, c * 64:(c + 1) * 64], PB[6][:, 0:64])
            E.fin(v, tag + "_ynT_ev")

    @b.gpsimd
    def _(g):
        g.wait_ge(E.sem(tag + "_v_mm")[0], E.at(tag + "_v_mm")[1])
        g.dma_start(out=ts["f_wbig"][:, :].rearrange("p (c o) -> p c o", c=8),
                    in_=w["wvpw"].ap().rearrange("(c p) o -> p c o", p=128)
                    ).then_inc(E.inc(tag + "_v_ld", 16)[0], 16)

    vvp_mm("f_sv", "_vp", 64 + 6144 + 512 + 1024 + 2048, ts["f_sv"])

    @b.gpsimd
    def _(g):
        g.wait_ge(E.sem(tag + "_vp_mm")[0], E.at(tag + "_vp_mm")[1])
        g.dma_start(out=ts["f_wbig"][:, :].rearrange("p (c o) -> p c o", c=8),
                    in_=w["wow"].ap().rearrange("(c p) o -> p c o", p=128)
                    ).then_inc(E.inc(tag + "_v_ld", 16)[0], 16)

    for dc in range(8):
        for tt in range(ntt):
            k = dc * ntt + tt

            @b.tensor
            def _(t, dc=dc, tt=tt, k=k):
                E.wait_all(t, [tag + "_vp_ev", tag + "_at_ev"])
                if k >= 2:
                    t.wait_ge(E.sem(tag + "_ot_ev")[0], E.at(tag + "_ot_ev")[1] - 1)
                t.matmul(PB[k % 2][:, 0:TT],
                         ts["f_sv"][0:64, dc * 128:(dc + 1) * 128],
                         alphaT[0:64, tt * TT:(tt + 1) * TT],
                         start=True, stop=True)
                E.fin(t, tag + "_ot_mm")

            @b.vector
            def _(v, dc=dc, tt=tt, k=k):
                v.wait_ge(E.sem(tag + "_ot_mm")[0], E.at(tag + "_ot_mm")[1])
                v.tensor_copy(ot[:, dc * Ti + tt * TT:dc * Ti + (tt + 1) * TT],
                              PB[k % 2][:, 0:TT])
                E.fin(v, tag + "_ot_ev")

    for oc2 in range(FCH):
        for tt in range(ntt):
            k = oc2 * ntt + tt

            @b.tensor
            def _(t, oc2=oc2, tt=tt, k=k):
                t.wait_ge(E.sem(tag + "_v_ld")[0], E.at(tag + "_v_ld")[1])
                E.wait_all(t, [tag + "_ot_ev"])
                if k >= 1:
                    t.wait_ge(E.sem(tag + "_fo_add")[0], E.at(tag + "_fo_add")[1])
                for dc in range(8):
                    t.matmul(PB[6][:, 0:TT],
                             ts["f_wbig"][:, dc * 1024 + oc2 * 128:dc * 1024 + oc2 * 128 + 128],
                             ot[:, dc * Ti + tt * TT:dc * Ti + (tt + 1) * TT],
                             start=(dc == 0), stop=(dc == 7))
                E.fin(t, tag + "_fo_mm")

            @b.sync
            def _(s, oc2=oc2, tt=tt, k=k):
                stp = tag + ("_fst0" if k % 2 == 0 else "_fst1")
                if k >= 2:
                    s.wait_ge(E.sem(stp)[0], E.at(stp)[1])
                s.dma_start(out=xrl[k % 2][:, 0:TT],
                            in_=w["x_in"].ap()[oc2 * 128:(oc2 + 1) * 128,
                                               tt * TT:(tt + 1) * TT]
                            ).then_inc(E.inc(tag + "_fxr", 16)[0], 16)

            @b.vector
            def _(v, oc2=oc2, tt=tt, k=k):
                v.wait_ge(E.sem(tag + "_fo_mm")[0], E.at(tag + "_fo_mm")[1])
                v.wait_ge(E.sem(tag + "_fxr")[0], E.at(tag + "_fxr")[1])
                stp = tag + ("_fst0" if k % 2 == 0 else "_fst1")
                if k >= 2:
                    v.wait_ge(E.sem(stp)[0], E.at(stp)[1])
                buf = (e1 if k % 2 == 0 else e3)
                v.tensor_scalar(buf[:, 0:TT], PB[6][:, 0:TT],
                                ts["f_bop"][:, oc2:oc2 + 1], None, ALU.add)
                v.tensor_tensor(buf[:, 0:TT], buf[:, 0:TT], xrl[k % 2][:, 0:TT],
                                ALU.add)
                E.fin(v, tag + "_fo_add")

            @b.sync
            def _(s, oc2=oc2, tt=tt, k=k):
                s.wait_ge(E.sem(tag + "_fo_add")[0], E.at(tag + "_fo_add")[1])
                stp = tag + ("_fst0" if k % 2 == 0 else "_fst1")
                s.dma_start(out=w["x_out"].ap()[oc2 * 128:(oc2 + 1) * 128,
                                                tt * TT:(tt + 1) * TT],
                            in_=(e1 if k % 2 == 0 else e3)[:, 0:TT]
                            ).then_inc(E.inc(stp, 16)[0], 16)

    @b.sync
    def _(s):
        s.wait_ge(E.sem(tag + "_fst0")[0], E.at(tag + "_fst0")[1])
        s.wait_ge(E.sem(tag + "_fst1")[0], E.at(tag + "_fst1")[1])
        E.fin(s, tag + "_done")


def build_l1f_program(T):
    nc = bass.Bass("TRN2", target_bir_lowering=False)
    w = {}
    w["x_in"] = nc.dram_tensor("x_in", [D_MODEL, T], F32, kind="ExternalInput")
    w["x_out"] = nc.dram_tensor("x_out", [D_MODEL, T], F32, kind="ExternalOutput")
    for nm, shp in (("w1a", [1024, 512]), ("w1b", [1024, 512]),
                    ("w2w", [512, 64]), ("wqw", [1024, 256]),
                    ("wkw", [1024, 256]), ("wvw", [1024, 1024]),
                    ("wvpw", [1024, 1024]), ("wow", [1024, 1024]),
                    ("WihT", [1024, 3072]), ("WhhT", [1024, 3072]),
                    ("slot_init", [64, 1024]), ("slot_initT", [1024, 64]),
                    ("tau", [1, 1]), ("brows", [1, 10816]),
                    ("b1P", [128, 4]), ("boP", [128, 8]),
                    ("ident", [128, 128]), ("ones_f1", [128, 1]),
                    ("ones_1f", [1, 128]), ("mask_ut", [128, 128]),
                    ("bmask", [1, 8])):
        w[nm] = nc.dram_tensor(nm, shp, F32, kind="ExternalInput")
    w["cv_sc"] = nc.dram_tensor("cv_sc", [1, 512], F32)
    w["cc4_in"] = nc.dram_tensor("cc4_in", [64, 1040], F32)
    w["cc4_out"] = nc.dram_tensor("cc4_out", [NCORES * 64, 1040], F32)
    es = ExitStack()
    with es:
        ts = build_base_tiles(nc, T)
        build_l1f_tiles(nc, ts, T)
        with nc.Block() as block:
            E = Emit(nc, block, es)
            emit_const_init(E, ts, w)
            emit_load_x(E, T, ts, w)
            xn_list = [ts["xn"][:, c * T:(c + 1) * T] for c in range(FCH)]
            emit_ln(E, T, ts["x_fm"], xn_list, "ln",
                    consts=ts,
                    scratch={"sq": ts["ln_sq"], "stat0": ts["ln_s0"],
                             "stat1": ts["ln_s1"], "stat2": ts["ln_s2"],
                             "stat3": ts["ln_s3"]},
                    banks={"A": ts["psA"]},
                    pre=("ldx", "cst", "cst_g"))
            emit_l1_ffn(E, T, ts, w, pre=("ln_done",))

            @block.sync
            def _(s):
                E.wait_all(s, ["l1f_done"])
    return nc


def l1f_in_maps(x_flat, p, T):
    sh = {}
    sh["w1a"] = _bf(p["L1_ffn_w1"][:1024])
    sh["w1b"] = _bf(p["L1_ffn_w1"][1024:])
    sh["w2w"] = _bf(p["L1_ffn_w2"])
    sh["wqw"] = _bf(p["L1_ffn_wq"])
    sh["wkw"] = _bf(p["L1_ffn_wk"])
    sh["wvw"] = _bf(p["L1_ffn_wv"])
    sh["wvpw"] = _bf(p["L1_ffn_wvp"])
    sh["wow"] = _bf(p["L1_ffn_wo"])
    sh["WihT"] = _bf(p["L1_ffn_Wih"].T)
    sh["WhhT"] = _bf(p["L1_ffn_Whh"].T)
    sh["slot_init"] = _bf(p["L1_ffn_slot_init"])
    sh["slot_initT"] = _bf(p["L1_ffn_slot_init"].T)
    sh["tau"] = _bf(p["L1_ffn_tau"].reshape(1, 1))
    br = np.zeros((1, 10816), np.float32)
    br[0, 0:64] = p["L1_ffn_b2"]
    br[0, 64:3136] = p["L1_ffn_bih"]
    br[0, 3136:6208] = p["L1_ffn_bhh"]
    br[0, 6208:6464] = p["L1_ffn_bq"]
    br[0, 6464:6720] = p["L1_ffn_bk"]
    br[0, 6720:7744] = p["L1_ffn_bv"]
    br[0, 7744:8768] = p["L1_ffn_ln_g"]
    br[0, 8768:9792] = p["L1_ffn_ln_b"]
    br[0, 9792:10816] = p["L1_ffn_bvp"]
    sh["brows"] = br
    b1P = np.zeros((128, 4), np.float32)
    for c in range(4):
        b1P[:, c] = p["L1_ffn_b1"][c * 128:(c + 1) * 128]
    sh["b1P"] = b1P
    boP = np.zeros((128, 8), np.float32)
    for c in range(8):
        boP[:, c] = p["L1_ffn_bo"][c * 128:(c + 1) * 128]
    sh["boP"] = boP
    sh["ident"] = np.eye(128, dtype=np.float32)
    sh["ones_f1"] = np.ones((128, 1), np.float32)
    sh["ones_1f"] = np.ones((1, 128), np.float32)
    jj, ii = np.meshgrid(np.arange(128), np.arange(128), indexing="ij")
    sh["mask_ut"] = (jj <= ii).astype(np.float32)
    maps = []
    for c in range(NCORES):
        d = dict(sh)
        d["x_in"] = np.ascontiguousarray(x_flat[c * T:(c + 1) * T].T)
        core = prep_core_inputs(x_flat, c, T)
        d["bmask"] = core["bmask"]
        maps.append(d)
    return maps


def run_l1f(x, p, T=None):
    B, S, D = x.shape
    x_flat = np.ascontiguousarray(x.reshape(B * S, D), dtype=np.float32)
    if T is None:
        T = B * S // NCORES
    nc = build_l1f_program(T)
    maps = l1f_in_maps(x_flat, p, T)
    res = run_bass_kernel_spmd(nc, maps, core_ids=list(range(NCORES)))
    outs = [np.asarray(res.results[c]["x_out"]).T for c in range(NCORES)]
    return np.concatenate(outs, 0).reshape(B, S, D)


def _ab_of(nc):
    used = nc.SBUF_PARTITION_SIZE_BYTES - nc.sbuf_bytes_remaining
    return (used + 1023) // 1024 * 1024


def build_base_tiles(nc, T, arena=True):
    ts = {}
    sb = nc.alloc_sbuf_tensor

    def at(name, shape, dt, off):
        return nc.alloc_sbuf_tensor_at(name, shape, dt, offset=off)

    ts["_at"] = at
    ts["ident_b"] = sb("c_ident_b", [128, 128], BF16)
    ts["ones_f1"] = sb("c_ones_f1", [128, 1], F32)
    ts["ones_1f"] = sb("c_ones_1f", [1, 128], F32)
    ts["ones_b1"] = sb("ones_b1", [128, 1], BF16)
    ts["mask_ut"] = sb("c_mask_ut", [128, 128], F32)
    ts["mn_sb"] = sb("mn_sb", [128, 512], BF16)
    ts["gate"] = sb("gate_t", [16, 1], F32)
    ts["gsg"] = sb("gsg", [128, 8], F32)
    ts["g1m"] = sb("g1m", [128, 8], F32)
    ts["skT"] = sb("skT", [64, 128], BF16)
    ts["sqh"] = sb("sqh", [64, 128], BF16)
    ts["kvr"] = sb("kvr", [64, 1024], F32)
    ts["kvrb"] = sb("kvrb", [64, 1024], BF16)
    ts["kcr"] = sb("kcr", [64, 16], F32)
    ts["kcrb"] = sb("kcrb", [64, 16], BF16)
    ts["den_r"] = sb("den_r", [1, 128], F32)
    ts["dbc"] = sb("dbc", [64, 128], F32)
    ts["sqs"] = sb("sqs", [128, 512], BF16)
    ts["smB"] = sb("smB", [128, 128], BF16)
    ts["e0"] = sb("e0", [128, 512], F32)
    ts["e1"] = sb("e1", [128, 512], F32)
    ts["e2"] = sb("e2", [128, 512], F32)
    ts["e3"] = sb("e3", [128, 512], F32)
    ts["xrl0"] = sb("xrl0", [128, 512], F32)
    ts["xrl1"] = sb("xrl1", [128, 512], F32)
    ts["st_sb"] = sb("st_sb", [128, 528], F32)
    ts["pfx"] = sb("pfx", [128, 528], F32)
    ts["bsm"] = sb("bsm", [128, 528], F32)
    ts["agr"] = sb("agr", [128, 528], F32)
    ts["pmrow"] = sb("pmrow", [1, 8], F32)
    ts["bmrow"] = sb("bmrow", [1, 8], F32)
    ts["pmb"] = sb("pmb", [128, 8], F32)
    ts["bmb"] = sb("bmb", [128, 8], F32)
    ts["wvec"] = sb("wvec", [1, 64], F32)
    ts["rmax"] = sb("rmax", [1, 32], F32)
    ts["wcol"] = sb("wcol", [64, 1], F32)
    ts["rmcol"] = sb("rmcol", [64, 1], F32)
    ts["kon"] = sb("kon", [128, 8], F32)
    ts["ident_f"] = sb("c_ident_f", [128, 128], F32)
    ts["ones_1b"] = sb("c_ones_1b", [1, 128], BF16)

    psA = nc.alloc_psum_tensor("psA", [128, 2048], F32)
    psB = nc.alloc_psum_tensor("psB", [128, 1536], F32)
    psBT = nc.alloc_psum_tensor("psBT", [128, 1024], BF16)
    ts["psA"], ts["psB"] = psA, psB
    PB = [psA[:, 0:512], psA[:, 512:1024], psA[:, 1024:1536], psA[:, 1536:2048],
          psB[:, 0:512], psB[:, 512:1024], psB[:, 1024:1536], psB[:, 1024:1536]]
    ts["PB"] = PB
    ts["PBT"] = psBT[:, :]
    if arena:
        build_base_arena(nc, ts, T)
    return ts


def build_base_arena(nc, ts, T):
    at = ts["_at"]
    AB = _ab_of(nc)
    ts["AB"] = AB
    xn_b = 16 * T
    ts["xn"] = at("xn", [128, 8 * T], BF16, AB)
    ts["loc"] = at("loc_t", [128, 8 * T], BF16, AB)
    ts["v"] = at("v_tm", [128, (T // 128) * 1024], BF16, AB + xn_b)
    ts["w0"] = at("w0", [128, 8192], BF16, AB + 2 * xn_b)
    ts["sq"] = at("sq_t", [128, 8 * T], BF16, AB + 2 * xn_b + 16384)
    skoff = AB + 3 * xn_b + 16384
    ts["sk"] = at("sk_tm", [128, (T // 128) * 1024], BF16, skoff)
    end = AB + 4 * xn_b + 16384
    ts["relT"] = at("relT", [64, T], BF16, end)
    ts["rr"] = ts["relT"]
    end2 = end + 2 * T
    assert end2 <= 206 * 1024, f"arena overflow {end2}"
    if 16 * T >= 16384:
        ts["selrr"] = at("selrr", [64, 4096], BF16, skoff)
        ts["memb"] = at("memb", [128, 4096], BF16, skoff + 8192)
    else:
        assert end2 + 16384 <= 206 * 1024
        ts["selrr"] = at("selrr", [64, 4096], BF16, end2)
        ts["memb"] = at("memb", [128, 4096], BF16, end2 + 8192)
    xt = []
    for c in range(FCH):
        xt.append(at(f"xf{c}", [128, T], F32, AB + c * 4 * T))
    ts["x_fm"] = xt
    ts["ln_sq"] = at("ln_sq", [128, T], F32, AB + 2 * xn_b + 16384)
    ts["ln_s0"] = at("ln_s0", [1, T], F32, skoff)
    ts["ln_s1"] = at("ln_s1", [1, T], F32, skoff + 4 * T)
    ts["ln_s2"] = at("ln_s2", [1, T], F32, skoff + 8 * T)
    ts["ln_s3"] = at("ln_s3", [1, T], F32, skoff + 12 * T)
    return ts


def emit_load_x(E, T, ts, w, tag="ldx"):
    b = E.block

    @b.sync
    def _(s):
        for c in range(FCH):
            s.dma_start(out=ts["x_fm"][c][:, :],
                        in_=w["x_in"].ap()[c * 128:(c + 1) * 128, :]
                        ).then_inc(E.inc(tag, 16)[0], 16)


def emit_const_init(E, ts, w, tag="cst"):
    b = E.block

    @b.vector
    def _(v):
        v.memset(ts["kon"][:, 0:1], 1e-5)
        v.memset(ts["kon"][:, 1:2], -0.5)
        v.memset(ts["kon"][:, 2:3], -1.0)
        v.memset(ts["kon"][:, 3:4], 1.0)
        E.fin(v, "kon")

    @b.gpsimd
    def _(g):
        g.dma_start(out=ts["ident_b"][:, :], in_=w["ident"].ap()
                    ).then_inc(E.inc(tag + "_g", 16)[0], 16)
        g.dma_start(out=ts["ones_b1"][:, :], in_=w["ones_f1"].ap()
                    ).then_inc(E.inc(tag + "_g", 16)[0], 16)
        g.dma_start(out=ts["ones_1b"][:, :], in_=w["ones_1f"].ap()
                    ).then_inc(E.inc(tag + "_g", 16)[0], 16)

    @b.sync
    def _(s):
        s.dma_start(out=ts["ones_f1"][:, :], in_=w["ones_f1"].ap()
                    ).then_inc(E.inc(tag, 16)[0], 16)
        s.dma_start(out=ts["ones_1f"][:, :], in_=w["ones_1f"].ap()
                    ).then_inc(E.inc(tag, 16)[0], 16)
        s.dma_start(out=ts["mask_ut"][:, :], in_=w["mask_ut"].ap()
                    ).then_inc(E.inc(tag, 16)[0], 16)
        s.dma_start(out=ts["ident_f"][:, :], in_=w["ident"].ap()
                    ).then_inc(E.inc(tag, 16)[0], 16)


def build_l0a_program(T):
    nc = bass.Bass("TRN2", target_bir_lowering=False)
    w = {}
    w["x_in"] = nc.dram_tensor("x_in", [D_MODEL, T], F32, kind="ExternalInput")
    w["x_out"] = nc.dram_tensor("x_out", [D_MODEL, T], F32, kind="ExternalOutput")
    for nm, shp in (("wqT", [1024, 1024]), ("wkT", [1024, 1024]),
                    ("wvT", [1024, 1024]), ("woT", [1024, 1024]),
                    ("MN", [1024, 64]), ("selRR", [M_MEM, FCH, 64, 128]),
                    ("MEMB", [M_MEM, FCH, 128, 128]), ("gate", [16, 1]),
                    ("ident", [128, 128]), ("ones_f1", [128, 1]),
                    ("ones_1f", [1, 128]), ("mask_ut", [128, 128]),
                    ("pmask", [1, 8]), ("bmask", [1, 8])):
        w[nm] = nc.dram_tensor(nm, shp, F32, kind="ExternalInput")
    w["gate_sc"] = nc.dram_tensor("gate_sc", [16, 1], F32)
    w["rm_sc"] = nc.dram_tensor("rm_sc", [64, 1], F32)
    w["cc_in"] = nc.dram_tensor("cc_in", [1024, 66], F32)
    w["cc_out"] = nc.dram_tensor("cc_out", [NCORES * 1024, 66], F32)

    es = ExitStack()
    with es:
        ts = build_base_tiles(nc, T)
        with nc.Block() as block:
            E = Emit(nc, block, es)
            emit_const_init(E, ts, w)
            emit_load_x(E, T, ts, w)
            xn_list = [ts["xn"][:, c * T:(c + 1) * T] for c in range(FCH)]
            emit_ln(E, T, ts["x_fm"], xn_list, "ln",
                    consts=ts,
                    scratch={"sq": ts["ln_sq"], "stat0": ts["ln_s0"],
                             "stat1": ts["ln_s1"], "stat2": ts["ln_s2"],
                             "stat3": ts["ln_s3"]},
                    banks={"A": ts["psA"]},
                    pre=("ldx", "cst", "cst_g"))
            xn_tiles = ts["xn"]
            ts2 = dict(ts)
            ts2["xn"] = xn_tiles
            emit_l0_attn(E, T, ts2, w, pre=("ln_done",))
            @block.sync
            def _(s):
                E.wait_all(s, ["l0a_done"])
    return nc


def _bf(x):
    return np.ascontiguousarray(x, dtype=np.float32)


def prep_shared_inputs(w):
    d = {}
    d["wqT"] = _bf(w["L0_attn_w_q"].T)
    d["wkT"] = _bf(w["L0_attn_w_k"].T)
    d["wvT"] = _bf(w["L0_attn_w_v"].T)
    d["woT"] = _bf(w["L0_attn_w_o"].T)
    mn = w["L0_attn_memory_norms"]
    MN = np.zeros((D_MODEL, 64), np.float32)
    for m in range(M_MEM):
        for h in range(NH):
            MN[h * HD:(h + 1) * HD, h * M_MEM + m] = mn[m, h]
    d["MN"] = MN
    mem = w["L0_attn_memories"]
    MEMB = np.zeros((M_MEM, FCH, 128, 128), np.float32)
    for m in range(M_MEM):
        for oc in range(FCH):
            for h2 in range(2):
                h = oc * 2 + h2
                MEMB[m, oc, h2 * 64:(h2 + 1) * 64, h2 * 64:(h2 + 1) * 64] = mem[m, h]
    d["MEMB"] = MEMB
    selRR = np.zeros((M_MEM, FCH, 64, 128), np.float32)
    for m in range(M_MEM):
        for oc in range(FCH):
            for p in range(128):
                h = oc * 2 + p // 64
                selRR[m, oc, h * M_MEM + m, p] = 1.0
    d["selRR"] = selRR
    d["gate"] = _bf(w["L0_attn_gate"].reshape(NH, 1))
    d["ident"] = np.eye(128, dtype=np.float32)
    d["ones_f1"] = np.ones((128, 1), np.float32)
    d["ones_1f"] = np.ones((1, 128), np.float32)
    jj, ii = np.meshgrid(np.arange(128), np.arange(128), indexing="ij")
    d["mask_ut"] = (jj <= ii).astype(np.float32)
    return d


def prep_core_inputs(x_flat, c, T):
    d = {}
    d["x_fm_in"] = np.ascontiguousarray(x_flat[c * T:(c + 1) * T].T)
    cores_per_b = NCORES // 2
    b = c // cores_per_b
    pmask = np.zeros((1, NCORES), np.float32)
    bmask = np.zeros((1, NCORES), np.float32)
    halo = np.zeros((1, NCORES), np.float32)
    for r in range(NCORES):
        if r // cores_per_b == b:
            bmask[0, r] = 1.0
            if r < c:
                pmask[0, r] = 1.0
    if c % cores_per_b != 0:
        halo[0, c - 1] = 1.0
    d["pmask"], d["bmask"], d["halo_sel"] = pmask, bmask, halo
    return d


def l0a_in_maps(x_flat, p, T):
    sh = prep_shared_inputs(p)
    maps = []
    for c in range(NCORES):
        d = {}
        d["x_in"] = np.ascontiguousarray(x_flat[c * T:(c + 1) * T].T)
        core = prep_core_inputs(x_flat, c, T)
        d["pmask"], d["bmask"] = core["pmask"], core["bmask"]
        for k in ("wqT", "wkT", "wvT", "woT", "MN", "selRR", "MEMB", "gate",
                  "ident", "ones_f1", "ones_1f", "mask_ut"):
            d[k] = sh[k]
        maps.append(d)
    return maps


def run_l0a(x, p, T=None):
    B, S, D = x.shape
    x_flat = np.ascontiguousarray(x.reshape(B * S, D), dtype=np.float32)
    if T is None:
        T = B * S // NCORES
    nc = build_l0a_program(T)
    maps = l0a_in_maps(x_flat, p, T)
    res = run_bass_kernel_spmd(nc, maps, core_ids=list(range(NCORES)))
    outs = [np.asarray(res.results[c]["x_out"]).T for c in range(NCORES)]
    return np.concatenate(outs, 0).reshape(B, S, D)


def build_fused_program(T):
    nc = bass.Bass("TRN2", target_bir_lowering=False)
    w = {}

    def ext(nm, shp):
        w[nm] = nc.dram_tensor(nm, shp, F32, kind="ExternalInput")

    w["x_in"] = nc.dram_tensor("x_in", [D_MODEL, T], F32, kind="ExternalInput")
    w["x_out"] = nc.dram_tensor("x_out", [D_MODEL, T], F32, kind="ExternalOutput")
    for nm in ("x1", "x2", "x3"):
        w[nm] = nc.dram_tensor(nm, [D_MODEL, T], F32)
    for nm, shp in (("ident", [128, 128]), ("ones_f1", [128, 1]),
                    ("ones_1f", [1, 128]), ("mask_ut", [128, 128]),
                    ("pmask", [1, 8]), ("bmask", [1, 8]), ("hmask", [1, 8]),
                    ("wqT", [1024, 1024]), ("wkT", [1024, 1024]),
                    ("wvT", [1024, 1024]), ("woT", [1024, 1024]),
                    ("MN", [1024, 64]), ("selRR", [M_MEM, FCH, 64, 128]),
                    ("MEMB", [M_MEM, FCH, 128, 128]), ("gate", [16, 1]),
                    ("rwT", [1024, 8]), ("gwT", [8, 1024, 2816]),
                    ("uwT", [8, 1024, 2816]), ("dwT", [8, 2816, 1024]),
                    ("sel8", [8, 1024]),
                    ("w_in", [1024, 4096]), ("wxp", [2048, 96]),
                    ("wdt", [64, 2048]), ("w_out", [2048, 1024]),
                    ("A_log", [2048, 16]), ("selN", [16, 2048]),
                    ("cwP", [128, 64]), ("cbP", [128, 16]), ("DP", [128, 16]),
                    ("nscP", [128, 8]), ("bdtP", [128, 16]),
                    ("bdt_row", [1, 2048]),
                    ("w1a", [1024, 512]), ("w1b", [1024, 512]),
                    ("w2w", [512, 64]), ("wqw", [1024, 256]),
                    ("wkw", [1024, 256]), ("wvw", [1024, 1024]),
                    ("wvpw", [1024, 1024]), ("wow", [1024, 1024]),
                    ("WihT", [1024, 3072]), ("WhhT", [1024, 3072]),
                    ("slot_init", [64, 1024]), ("slot_initT", [1024, 64]),
                    ("tau", [1, 1]), ("brows", [1, 10816]),
                    ("b1P", [128, 4]), ("boP", [128, 8])):
        ext(nm, shp)
    w["gate_sc"] = nc.dram_tensor("gate_sc", [16, 1], F32)
    w["rm_sc"] = nc.dram_tensor("rm_sc", [64, 1], F32)
    w["cc_in"] = nc.dram_tensor("cc_in", [1024, 66], F32)
    w["cc_out"] = nc.dram_tensor("cc_out", [NCORES * 1024, 66], F32)
    w["xn32"] = nc.dram_tensor("xn32", [D_MODEL, T], F32)
    w["res_sp"] = nc.dram_tensor("res_sp", [2048, T], BF16)
    w["cc2_in"] = nc.dram_tensor("cc2_in", [2048, 4], F32)
    w["cc2_out"] = nc.dram_tensor("cc2_out", [NCORES * 2048, 4], F32)
    w["cc3_in"] = nc.dram_tensor("cc3_in", [2048, 32], F32)
    w["cc3_out"] = nc.dram_tensor("cc3_out", [NCORES * 2048, 32], F32)
    w["cv_sc"] = nc.dram_tensor("cv_sc", [1, 512], F32)
    w["cc4_in"] = nc.dram_tensor("cc4_in", [64, 1040], F32)
    w["cc4_out"] = nc.dram_tensor("cc4_out", [NCORES * 64, 1040], F32)

    es = ExitStack()
    with es:
        ts = build_base_tiles(nc, T)
        build_moe_tiles(nc, ts, T)
        build_l1a_tiles(nc, ts, T)
        build_l1f_tiles(nc, ts, T)
        with nc.Block() as block:
            E = Emit(nc, block, es)
            xn_list = [ts["xn"][:, c * T:(c + 1) * T] for c in range(FCH)]
            ln_scr = {"sq": ts["ln_sq"], "stat0": ts["ln_s0"],
                      "stat1": ts["ln_s1"], "stat2": ts["ln_s2"],
                      "stat3": ts["ln_s3"]}
            emit_const_init(E, ts, w)

            wd = dict(w); wd["x_out"] = w["x1"]
            emit_load_x(E, T, ts, wd)
            emit_ln(E, T, ts["x_fm"], xn_list, "ln", consts=ts, scratch=ln_scr,
                    banks={"A": ts["psA"]}, pre=("ldx", "cst", "cst_g"))
            emit_l0_attn(E, T, ts, wd, pre=("ln_done",))

            @block.sync
            def _(s):
                E.wait_all(s, ["l0a_done"])
            barrier(E)
            E.recycle()

            wd = dict(w); wd["x_in"] = w["x1"]; wd["x_out"] = w["x2"]
            emit_load_x(E, T, ts, wd)
            emit_ln(E, T, ts["x_fm"], xn_list, "ln", consts=ts, scratch=ln_scr,
                    banks={"A": ts["psA"]}, pre=("ldx",))
            emit_l0_ffn(E, T, ts, wd, pre=("ln_done",))

            @block.sync
            def _(s):
                E.wait_all(s, ["moe_done"])
            barrier(E)
            E.recycle()

            wd = dict(w); wd["x_in"] = w["x2"]; wd["x_out"] = w["x3"]
            emit_load_x(E, T, ts, wd)

            @block.gpsimd
            def _(g):
                g.dma_start(out=ts["nsc_sb"][:, :], in_=w["nscP"].ap()
                            ).then_inc(E.inc("nscld", 16)[0], 16)
            emit_ln(E, T, ts["x_fm"], None, "ln", xn_f32=ts["x_fm"], consts=ts,
                    scratch=ln_scr, banks={"A": ts["psA"]}, pre=("ldx",))

            @block.sync
            def _(s):
                E.wait_all(s, ["ln_done"])
                for c in range(FCH):
                    s.dma_start(out=wd["xn32"].ap()[c * 128:(c + 1) * 128, :],
                                in_=ts["x_fm"][c][:, :]
                                ).then_inc(E.inc("l1a_xnsp", 16)[0], 16)
            emit_ln(E, T, None, xn_list, "rms", rms_of=ts["x_fm"],
                    nsc=ts["nsc_sb"], consts=ts, scratch=ln_scr,
                    banks={"A": ts["psA"]},
                    pre=("ln_done", "nscld", "l1a_xnsp"))
            emit_l1_attn(E, T, ts, wd, pre=("rms_done",))

            @block.sync
            def _(s):
                E.wait_all(s, ["l1a_done"])
            barrier(E)
            E.recycle()

            wd = dict(w); wd["x_in"] = w["x3"]
            emit_load_x(E, T, ts, wd)
            emit_ln(E, T, ts["x_fm"], xn_list, "ln", consts=ts, scratch=ln_scr,
                    banks={"A": ts["psA"]}, pre=("ldx",))
            emit_l1_ffn(E, T, ts, wd, pre=("ln_done",))

            @block.sync
            def _(s):
                E.wait_all(s, ["l1f_done"])
    return nc


def fused_in_maps(x_flat, p, T):
    maps = []
    parts = [l0a_in_maps(x_flat, p, T), moe_in_maps(x_flat, p, T),
             l1a_in_maps(x_flat, p, T), l1f_in_maps(x_flat, p, T)]
    for c in range(NCORES):
        d = {}
        for m in parts:
            d.update(m[c])
        maps.append(d)
    return maps


_PROG_CACHE = {}


def _get_prog(kind, T):
    key = (kind, T)
    if key not in _PROG_CACHE:
        _PROG_CACHE[key] = {"l0a": build_l0a_program, "moe": build_moe_program,
                            "l1a": build_l1a_program, "l1f": build_l1f_program,
                            "fused": build_fused_program}[kind](T)
    return _PROG_CACHE[key]


def kernel(**inputs):
    p = inputs
    x = np.asarray(p["L0_attn_hidden_states"], dtype=np.float32)
    B, S, D = x.shape
    T = B * S // NCORES
    x_flat = np.ascontiguousarray(x.reshape(B * S, D))
    cores = list(range(NCORES))
    nc = _get_prog("fused", T)
    maps = fused_in_maps(x_flat, p, T)
    res = run_bass_kernel_spmd(nc, maps, core_ids=cores)
    xs = [np.asarray(res.results[c]["x_out"]) for c in cores]
    out = np.concatenate([xc.T for xc in xs], 0).reshape(B, S, D)
    return np.ascontiguousarray(out, dtype=np.float32)
